# version 05
# baseline (speedup 1.0000x reference)
import contextlib
import numpy as np
import concourse.bass as bass
import concourse.mybir as mybir
from concourse.bass_utils import run_bass_kernel_spmd

F32 = mybir.dt.float32
BF16 = mybir.dt.bfloat16
AF = mybir.ActivationFunctionType
OP = mybir.AluOpType
AP = bass.AP

import os
TOK, D, TILE = 4096, 1024, 512
NT, PC, TC = TOK // TILE, D // 128, TILE // 128
NT = int(os.environ.get("DK_NT", NT))
H, N, DFF, NE = 16, 64, 4096, 20
SC = float(np.exp(-0.5))
ENGS = ("sync", "scalar", "vector", "tensor", "gpsimd")

WN1D = ["ln_pre_w", "ln_pre_b", "ln_attn_w", "ln_attn_b", "ln_ffn_w", "ln_ffn_b",
        "x_r", "x_w", "x_k", "x_v", "x_a", "x_g", "w0", "a0", "k_k", "k_a",
        "gn_w", "gn_b", "ffn_x_k"]


class Plan:

    def __init__(self):
        self.groups = {e: [] for e in ENGS}
        self.prog = {e: 0 for e in ENGS}
        self.dmac = {e: 0 for e in ENGS}
        self.seq = 0
        self.bank_w = [None] * 8
        self.bank_r = [None] * 8

    def _mk(self, eng, fn, after, is_dma, n_dma=0):
        waits = {}
        for tokn in after:
            if tokn is None:
                continue
            s, v = tokn
            if s == eng and eng != "gpsimd":
                continue
            waits[s] = max(waits.get(s, 0), v)
        import traceback
        if os.environ.get("DK_TRACE_SEQ") and self.seq == int(os.environ["DK_TRACE_SEQ"]):
            print("SEQ", self.seq, "eng", eng, "dma", is_dma)
            traceback.print_stack()
        self.groups[eng].append((waits, fn, n_dma if is_dma else None, self.seq))
        self.seq += 1
        if is_dma:
            self.dmac[eng] += 16 * n_dma
            return ("dma_" + eng, self.dmac[eng])
        self.prog[eng] += 1
        return (eng, self.prog[eng])

    def g(self, eng, fn, after=(), wbank=(), rbank=()):
        after = list(after)
        for b in wbank:
            after.append(self.bank_r[b])
            after.append(self.bank_w[b])
        for b in rbank:
            after.append(self.bank_w[b])
        tok = self._mk(eng, fn, after, False)
        for b in wbank:
            self.bank_w[b] = tok
        for b in rbank:
            self.bank_r[b] = tok
        return tok

    def dma_g(self, eng, fn, n_dma, after=()):
        return self._mk(eng, fn, list(after), True, n_dma)


def build_nc(debug=False, dbg_tile=NT - 1):
    nc = bass.Bass("TRN2", target_bir_lowering=False, debug=False,
                   detect_race_conditions=False)

    x_in = nc.dram_tensor("x", [TOK, D], F32, kind="ExternalInput").ap()
    y_out = nc.dram_tensor("out", [TOK, D], F32, kind="ExternalOutput").ap()
    dr = {}
    for nm in WN1D:
        dr[nm] = nc.dram_tensor(nm, [D], F32, kind="ExternalInput").ap()
    for nm, sh in [("r_k", [H, N]), ("w1", [D, 64]), ("w2", [64, D]),
                   ("a1", [D, 64]), ("a2", [64, D]), ("g1", [D, 128]),
                   ("g2", [128, D]), ("W_r", [D, D]), ("W_k", [D, D]),
                   ("W_v", [D, D]), ("W_o", [D, D]), ("W_ffn1", [D, DFF]),
                   ("W_ffn2", [DFF, D]), ("wg1", [D, 256]), ("bg1", [256]),
                   ("wg2", [256, 40]), ("bg2", [40]), ("We", [NE, D, D]),
                   ("be", [NE, D])]:
        dr[nm] = nc.dram_tensor(nm, sh, F32, kind="ExternalInput").ap()

    BIGW = ([("W_r", 0), ("W_k", 0), ("W_v", 0), ("W_o", 0)]
            + [("ffn1", i) for i in range(4)] + [("ffn2", i) for i in range(4)]
            + [("We", e) for e in range(NE)])
    NU = len(BIGW)
    wcache = nc.dram_tensor("wcache", [128, 2 * NU * 4096], BF16, kind="Internal").ap()

    dbg = {}
    if debug:
        for nm in ["d_u1", "d_xn", "d_r", "d_e", "d_kk", "d_kmod", "d_v", "d_b",
                   "d_g", "d_o", "d_x1", "d_xf", "d_x2", "d_x3"]:
            dbg[nm] = nc.dram_tensor(nm, [128, PC * TILE], F32, kind="ExternalOutput").ap()
        dbg["d_gates"] = nc.dram_tensor("d_gates", [20, TILE], F32, kind="ExternalOutput").ap()

    es = contextlib.ExitStack()
    E = es.enter_context

    id_bf = E(nc.sbuf_tensor("id_bf", [128, 128], BF16))
    id_f32 = E(nc.sbuf_tensor("id_f32", [128, 128], F32))
    msk_su = E(nc.sbuf_tensor("msk_su", [128, 128], BF16))
    msk_iu = E(nc.sbuf_tensor("msk_iu", [128, 128], BF16))
    msk_sl = E(nc.sbuf_tensor("msk_sl", [128, 128], BF16))
    rscan = E(nc.sbuf_tensor("rscan", [128, TILE], BF16))
    seg16b = E(nc.sbuf_tensor("seg16b", [128, PC * 16], BF16))
    bc16 = E(nc.sbuf_tensor("bc16", [16, PC * 128], F32))
    ones_col = E(nc.sbuf_tensor("ones_col", [128, 1], F32))
    cst_eps = E(nc.sbuf_tensor("cst_eps", [128, 2], F32))
    ones_r128 = E(nc.sbuf_tensor("ones_r128", [1, 128], F32))

    VC = {nm: E(nc.sbuf_tensor("c_" + nm, [128, PC], F32)) for nm in WN1D}
    VC["r_k"] = E(nc.sbuf_tensor("c_r_k", [128, PC], F32))
    VC["omka"] = E(nc.sbuf_tensor("c_omka", [128, PC], F32))
    w1w2c = E(nc.sbuf_tensor("w1w2c", [128, PC], F32))
    w2b1c = E(nc.sbuf_tensor("w2b1c", [128, PC], F32))
    bg1c = E(nc.sbuf_tensor("bg1c", [128, 2], F32))
    bg2c = E(nc.sbuf_tensor("bg2c", [40, 1], F32))
    w1sq = E(nc.sbuf_tensor("w1sq", [128, PC], F32))
    w1b1 = E(nc.sbuf_tensor("w1b1", [128, PC], F32))
    sb1 = E(nc.sbuf_tensor("sb1", [1, 2], F32))
    scr_red = E(nc.sbuf_tensor("scr_red", [128, 2], F32))

    w1_sb = E(nc.sbuf_tensor("w1_sb", [128, PC * 64], BF16))
    a1_sb = E(nc.sbuf_tensor("a1_sb", [128, PC * 64], BF16))
    w2_sb = E(nc.sbuf_tensor("w2_sb", [64, D], BF16))
    a2_sb = E(nc.sbuf_tensor("a2_sb", [64, D], BF16))
    g1_sb = E(nc.sbuf_tensor("g1_sb", [128, PC * 128], BF16))
    g2_sb = E(nc.sbuf_tensor("g2_sb", [128, D], BF16))
    wg1_sb = E(nc.sbuf_tensor("wg1_sb", [128, PC * 256], BF16))
    wg2_sb = E(nc.sbuf_tensor("wg2_sb", [128, 2 * 40], BF16))
    be_sb = E(nc.sbuf_tensor("be_sb", [NE, D], BF16))

    sT = E(nc.sbuf_tensor("sT", [128, PC * N], F32))
    sT_bf = E(nc.sbuf_tensor("sT_bf", [128, PC * N], BF16))
    xn_halo = E(nc.sbuf_tensor("xn_halo", [128, PC], BF16))
    xf_halo = E(nc.sbuf_tensor("xf_halo", [128, PC], BF16))

    NSLOT = 2
    wslot = [E(nc.sbuf_tensor(f"wslot{i}", [128, 4096], BF16)) for i in range(NSLOT)]

    tmb0 = E(nc.sbuf_tensor("tmb0", [128, D], F32))
    tmb = [tmb0, tmb0]

    x_fm = E(nc.sbuf_tensor("x_fm", [128, PC * TILE], F32))
    xsq_fm = E(nc.sbuf_tensor("xsq_fm", [128, TILE], F32))
    s2w_f = E(nc.sbuf_tensor("s2w_f", [128, TILE], F32))
    bt2_f = E(nc.sbuf_tensor("bt2_f", [128, TILE], F32))
    pass

    lniv = bt2_f[0:1, 0:TILE]
    xn_bf = E(nc.sbuf_tensor("xn_bf", [128, PC * TILE], BF16))
    dx_bf = E(nc.sbuf_tensor("dx_bf", [128, PC * TILE], BF16))
    e_bf = E(nc.sbuf_tensor("e_bf", [128, PC * TILE], BF16))
    b_bf = E(nc.sbuf_tensor("b_bf", [128, PC * TILE], BF16))
    r_bf = E(nc.sbuf_tensor("r_bf", [128, PC * TILE], BF16))
    kk_bf = E(nc.sbuf_tensor("kk_bf", [128, PC * TILE], BF16))
    km_bf = E(nc.sbuf_tensor("km_bf", [128, PC * TILE], BF16))
    v_bf = E(nc.sbuf_tensor("v_bf", [128, PC * TILE], BF16))
    g_bf = E(nc.sbuf_tensor("g_bf", [128, PC * TILE], BF16))
    o_bf = E(nc.sbuf_tensor("o_bf", [128, PC * TILE], BF16))
    mixp = [E(nc.sbuf_tensor(f"mixp{i}", [128, TILE], BF16)) for i in range(2)]
    t1_bf = E(nc.sbuf_tensor("t1_bf", [64, TILE], BF16))
    sg_bf = E(nc.sbuf_tensor("sg_bf", [128, TILE], BF16))
    pp_bf = E(nc.sbuf_tensor("pp_bf", [128, TILE], BF16))

    ARENA_COLS = 18944
    arena = E(nc.sbuf_tensor("arena", [128, ARENA_COLS], BF16))
    _ac = [0]
    def carve(n):
        a0 = _ac[0]; _ac[0] += n
        assert _ac[0] <= ARENA_COLS
        return arena[:, a0:a0 + n]
    TIL = {nm: carve(4 * TILE) for nm in ("at", "bt", "kt", "rt")}
    E2G = carve(4 * TILE)
    X_sb = carve(8 * 128)
    XT_sb = carve(8 * 128)
    Hm_sb = carve(8 * 128)
    M1_sb = carve(8 * 128)
    M2_sb = carve(8 * 128)
    u_sb = carve(8 * 64)
    vtm_sb = carve(8 * 64)
    rhs_sb = carve(4 * 128)
    btm_sb = carve(8 * 64)
    ktm_sb = carve(8 * 64)
    bbkb = carve(2 * 4 * 128)
    def h2c(q):
        return arena[:, q * 512:(q + 1) * 512]

    cs_f = E(nc.sbuf_tensor("cs_f", [128, TILE], F32))
    cse_f = E(nc.sbuf_tensor("cse_f", [128, TILE], F32))
    csL_f = E(nc.sbuf_tensor("csL_f", [128, TC], F32))
    DLG = E(nc.sbuf_tensor("DLG", [128, 4 * TC], F32))
    dec = [E(nc.sbuf_tensor(f"dec{i}", [128, TILE], BF16)) for i in range(2)]

    hrow = E(nc.sbuf_tensor("hrow", [16, TILE], F32))
    hrow2 = E(nc.sbuf_tensor("hrow2", [16, TILE], F32))
    hrow3 = E(nc.sbuf_tensor("hrow3", [16, TILE], F32))
    rows20 = E(nc.sbuf_tensor("rows20", [20, TILE + TILE], F32))
    lgt_f = rows20[0:20, 0:TILE]
    lnmu = rows20[0:1, TILE:2 * TILE]

    hg_bf = arena[:, 0:2 * TILE]
    lgtT = arena[:, 5 * TILE:5 * TILE + 160].bitcast(F32)
    gscr = arena[:, 5 * TILE + 160:5 * TILE + 320].bitcast(F32)
    gscr2 = arena[:, 5 * TILE + 320:5 * TILE + 480].bitcast(F32)
    grow = E(nc.sbuf_tensor("grow", [128, 16], F32))
    vspace = E(nc.sbuf_tensor("vspace", [128, 1], F32))
    gates_bf = arena[0:20, 2 * TILE:3 * TILE]
    sel2 = [arena[0:20, 15 * TILE + i * 128:15 * TILE + (i + 1) * 128] for i in range(2)]
    GBC0 = 16 * TILE
    def gbc_t(ee):
        return arena[:, GBC0 + ee * TILE:GBC0 + (ee + 1) * TILE]

    stat_s = E(nc.sbuf_tensor("stat_s", [128, TC], F32))
    stat_q = E(nc.sbuf_tensor("stat_q", [128, TC], F32))
    stat_h = E(nc.sbuf_tensor("stat_h", [128, 4 * TC], F32))
    stat_m = E(nc.sbuf_tensor("stat_m", [128, TC], F32))
    stat_i = E(nc.sbuf_tensor("stat_i", [128, TC], F32))

    PSA = E(nc.psum_tensor("PSA", [128, 1024], F32))
    PSB = E(nc.psum_tensor("PSB", [128, 1024], F32))
    PSC = E(nc.psum_tensor("PSC", [128, 1024], F32))
    PSD = E(nc.psum_tensor("PSD", [128, 1024], F32))
    PST = [PSA, PSB, PSC, PSD]
    def BK(i):
        return PST[i // 2][:, (i % 2) * 512:(i % 2) * 512 + 512]

    sem = {e: E(nc.semaphore("sem_" + e)) for e in ENGS}
    dsem = {e: E(nc.semaphore("dsem_" + e)) for e in ENGS}


    p = Plan()

    def fm(tt, pc):
        return tt[:, pc * TILE:(pc + 1) * TILE]

    def fmc(tt, pc, c):
        return tt[:, pc * TILE + c * 128: pc * TILE + (c + 1) * 128]

    def mask8(m):
        return AP(m.ap().tensor, 0, [[128, 128], [0, 8], [1, 128]])

    def h8(tt):
        return tt.rearrange("p (h j) -> p h j", h=8)

    def ps_h8(pst):
        return pst[:].rearrange("p (h j) -> p h j", h=8)

    def init_memsets(e):
        e.memset(id_bf[:], 1.0)
        e.memset(id_f32[:], 1.0)
        e.memset(msk_su[:], 1.0)
        e.memset(msk_iu[:], 1.0)
        e.memset(msk_sl[:], 1.0)
        e.memset(rscan[:], 1.0)
        for c in range(TC):
            e.memset(rscan[:, c * 128:c * 128 + 1], 0.0)
        e.memset(seg16b[:], 1.0)
        e.memset(bc16[:], 1.0)
        e.memset(ones_col[:], 1.0)
        e.memset(cst_eps[:, 0:1], 1e-5)
        e.memset(cst_eps[:, 1:2], 0.00064)
        e.memset(ones_r128[:], 1.0)
        e.memset(sT[:], 0.0)
        e.memset(sT_bf[:], 0.0)
        e.memset(xn_halo[:], 0.0)
        return e.memset(xf_halo[:], 0.0)

    tok_ms = p.g("vector", init_memsets)
    _asel = [
        (id_bf, OP.is_equal, 1, -1), (id_f32, OP.is_equal, 1, -1),
        (msk_su, OP.is_gt, 1, -1), (msk_iu, OP.is_ge, 1, -1),
        (msk_sl, OP.is_gt, -1, 1),
    ]
    for _t, _op, _st, _cm in _asel:
        p.g("gpsimd", (lambda e, _t=_t, _op=_op, _st=_st, _cm=_cm: e.affine_select(
            out=_t[:], in_=_t[:], pattern=[[_st, 128]], compare_op=_op,
            fill=0.0, base=0, channel_multiplier=_cm)), after=[tok_ms])
    p.g("gpsimd", (lambda e: e.affine_select(
        out=bc16[:].rearrange("r (b o) -> r b o", o=64),
        in_=bc16[:].rearrange("r (b o) -> r b o", o=64),
        pattern=[[1, 16], [0, 64]], compare_op=OP.is_equal,
        fill=0.0, base=0, channel_multiplier=-1)), after=[tok_ms])
    p.g("gpsimd", (lambda e: e.affine_select(
        out=seg16b[0:64, :].rearrange("p (c j) -> p c j", j=16),
        in_=seg16b[0:64, :].rearrange("p (c j) -> p c j", j=16),
        pattern=[[-2, PC], [1, 16]], compare_op=OP.is_equal,
        fill=0.0, base=0, channel_multiplier=0)), after=[tok_ms])
    p.g("gpsimd", (lambda e: e.affine_select(
        out=seg16b[64:128, :].rearrange("p (c j) -> p c j", j=16),
        in_=seg16b[64:128, :].rearrange("p (c j) -> p c j", j=16),
        pattern=[[-2, PC], [1, 16]], compare_op=OP.is_equal,
        fill=0.0, base=-1, channel_multiplier=0)), after=[tok_ms])
    tok_c0 = ("gpsimd", p.prog["gpsimd"])

    VSTAGE_LIST = WN1D + ["r_k", "bg1"]
    vstage = x_fm[0:8, 0:len(VSTAGE_LIST) * 128]

    def load_smalls(e):
        ins = []
        for vi, nm in enumerate(VSTAGE_LIST):
            if nm == "r_k":
                src_ = dr["r_k"].rearrange("h n -> (h n)").rearrange("(c p) -> c p", p=128)
            elif nm == "bg1":
                src_ = dr["bg1"].rearrange("(c p) -> c p", p=128)
            else:
                src_ = dr[nm].rearrange("(c p) -> c p", p=128)
            rows = src_.shape[0]
            ins.append(e.dma_start(out=vstage[0:rows, vi * 128:(vi + 1) * 128], in_=src_))
        ins.append(e.dma_start(out=bg2c[:], in_=AP(dr["bg2"].tensor, 0, [[1, 40], [1, 1]])))
        ins.append(e.dma_start(out=w1_sb[:].rearrange("p (c m) -> p c m", m=64),
                   in_=dr["w1"].rearrange("(c p) m -> p c m", p=128)))
        ins.append(e.dma_start(out=a1_sb[:].rearrange("p (c m) -> p c m", m=64),
                   in_=dr["a1"].rearrange("(c p) m -> p c m", p=128)))
        ins.append(e.dma_start(out=w2_sb[:], in_=dr["w2"]))
        ins.append(e.dma_start(out=a2_sb[:], in_=dr["a2"]))
        ins.append(e.dma_start(out=g1_sb[:].rearrange("p (c m) -> p c m", m=128),
                   in_=dr["g1"].rearrange("(c p) m -> p c m", p=128)))
        ins.append(e.dma_start(out=g2_sb[:], in_=dr["g2"]))
        ins.append(e.dma_start(out=wg1_sb[:].rearrange("p (c m) -> p c m", m=256),
                   in_=dr["wg1"].rearrange("(c p) m -> p c m", p=128)))
        ins.append(e.dma_start(out=wg2_sb[:].rearrange("p (c m) -> p c m", m=40),
                   in_=dr["wg2"].rearrange("(c p) m -> p c m", p=128)))
        ins.append(e.dma_start(out=be_sb[:], in_=dr["be"]))
        return ins

    N_SMALL = len(WN1D) + 2 + 1 + 9
    tok_sm = p.dma_g("gpsimd", load_smalls, N_SMALL)

    def vtr_mm(e):
        ins = None
        for vi, nm in enumerate(VSTAGE_LIST):
            rows = 2 if nm == "bg1" else 8
            ins = e.matmul(out=BK(0)[:, vi * 8: vi * 8 + rows],
                           lhsT=vstage[0:rows, vi * 128:(vi + 1) * 128],
                           rhs=id_f32[0:rows, 0:rows],
                           start=(vi == 0), stop=(vi == len(VSTAGE_LIST) - 1))
        return ins
    tok_vt0 = p.g("tensor", vtr_mm, after=[tok_sm, tok_c0], wbank=[0])

    def vtr_ev(e):
        ins = None
        for vi, nm in enumerate(VSTAGE_LIST):
            if nm == "bg1":
                ins = e.tensor_copy(out=bg1c[:], in_=BK(0)[:, vi * 8: vi * 8 + 2])
            else:
                dst = VC["r_k"] if nm == "r_k" else VC[nm]
                ins = e.tensor_copy(out=dst[:], in_=BK(0)[:, vi * 8: vi * 8 + 8])
        return ins
    tok_vt0e = p.g("vector", vtr_ev, after=[tok_vt0], rbank=[0])

    def derive(e):
        e.tensor_scalar(out=VC["omka"][:], in0=VC["k_a"][:], scalar1=-1.0,
                        scalar2=1.0, op0=OP.mult, op1=OP.add)
        e.tensor_mul(out=w1w2c[:], in0=VC["ln_pre_w"][:], in1=VC["ln_attn_w"][:])
        e.tensor_mul(out=w2b1c[:], in0=VC["ln_pre_b"][:], in1=VC["ln_attn_w"][:])
        e.tensor_mul(out=w1sq[:], in0=VC["ln_pre_w"][:], in1=VC["ln_pre_w"][:])
        e.tensor_mul(out=w1b1[:], in0=VC["ln_pre_w"][:], in1=VC["ln_pre_b"][:])
        e.reduce_sum(out=scr_red[:, 0:1], in_=VC["ln_pre_b"][:], axis=mybir.AxisListType.X)
        e.tensor_tensor(out=gscr[:, 0:PC], in0=VC["ln_pre_b"][:], in1=VC["ln_pre_b"][:],
                        op=OP.mult)
        return e.reduce_sum(out=scr_red[:, 1:2], in_=gscr[:, 0:PC], axis=mybir.AxisListType.X)
    tok_dv = p.g("vector", derive, after=[tok_vt0e])
    tok_sb1 = p.g("tensor", lambda e: e.matmul(out=BK(0)[0:1, 0:2], lhsT=ones_col[:],
                                               rhs=scr_red[:], start=True, stop=True),
                  after=[tok_dv], wbank=[0])
    tok_sb1e = p.g("vector", lambda e: e.tensor_copy(out=sb1[:], in_=BK(0)[0:1, 0:2]),
                   after=[tok_sb1], rbank=[0])

    def src_ap(u, hh):
        nm, idx = BIGW[u]
        if nm == "ffn1":
            base = dr["W_ffn1"][:, idx * D + hh * 512: idx * D + (hh + 1) * 512]
        elif nm == "ffn2":
            base = dr["W_ffn2"][idx * D:(idx + 1) * D, hh * 512:(hh + 1) * 512]
        elif nm == "We":
            base = dr["We"][idx][:, hh * 512:(hh + 1) * 512]
        else:
            base = dr[nm][:, hh * 512:(hh + 1) * 512]
        return base.rearrange("(c p) m -> p c m", p=128)

    wload = {}
    slot_free = [None] * 2
    wb_done = {}
    st_hu = [0]

    def hu_sb(hu, kc, mc):
        return wslot[hu % 2][:, kc * 512 + mc * 128: kc * 512 + (mc + 1) * 128]

    def ensure_hu(t, hu):
        while st_hu[0] <= hu:
            cur = st_hu[0]
            s = cur % 2
            aft = [slot_free[s]]
            if t == 0:
                aft += [wb_done.get(cur - 2), tok_sm]
                tokn = p.dma_g("gpsimd",
                               (lambda e, cur=cur, s=s: [e.dma_start(
                                   out=wslot[s][:].rearrange("p (c m) -> p c m", m=512),
                                   in_=src_ap(cur // 2, cur % 2))]), 1, after=aft)
                wload[(t, cur)] = tokn
                wb_done[cur] = p.dma_g("sync",
                                       (lambda e, cur=cur, s=s: [e.dma_start(
                                           out=wcache[:, cur * 4096:(cur + 1) * 4096],
                                           in_=wslot[s][:])]), 1, after=[tokn])
            else:
                q = "sync" if cur % 2 == 0 else "scalar"
                wload[(t, cur)] = p.dma_g(q,
                                          (lambda e, cur=cur, s=s: [e.dma_start(
                                              out=wslot[s][:],
                                              in_=wcache[:, cur * 4096:(cur + 1) * 4096])]),
                                          1, after=aft)
            st_hu[0] += 1

    dbg_after = []

    def dump_bf(t, name, bft, after):
        if debug and t == dbg_tile and name in dbg:
            if os.environ.get("DK_PRINT_SEQ"):
                print("DUMP", name, "seq", p.seq)
            dbg_after.append(p.dma_g("gpsimd",
                                     (lambda e: [e.dma_start(out=dbg[name][:], in_=bft[:])]),
                                     1, after=after))

    def dump_f(t, name, ft, after):
        if debug and t == dbg_tile and name in dbg:
            if os.environ.get("DK_PRINT_SEQ"):
                print("DUMP", name, "seq", p.seq)
            dbg_after.append(p.dma_g("gpsimd",
                                     (lambda e: [e.dma_start(out=dbg[name][:], in_=ft[:])]),
                                     1, after=after))

    tok_store_last = None
    tmb_free = [None, None]

    for t in range(NT):
        st_hu[0] = 0
        ensure_hu(t, 1)

        tok_u1 = None
        for tc in range(TC):
            buf = tmb[tc % 2]
            tok_ld = p.dma_g("sync",
                             (lambda e, t=t, tc=tc, buf=buf: [e.dma_start(
                                 out=buf[:],
                                 in_=x_in[t * TILE + tc * 128: t * TILE + (tc + 1) * 128, :])]),
                             1, after=[tmb_free[0], tmb_free[1]])

            if debug and t == dbg_tile and tc == 0 and os.environ.get("DK_DUMP_BUF"):
                if os.environ.get("DK_PRINT_SEQ"):
                    print("DUMP_BUF seq", p.seq)
                dbg_after.append(p.dma_g("gpsimd",
                                         (lambda e, buf=buf: [e.dma_start(
                                             out=dbg["d_u1"][:, 0:1024], in_=buf[:])]),
                                         1, after=[tok_ld]))

            def p0s(e, tc=tc, buf=buf):
                ins = None
                for hh in range(2):
                    e.activation(out=xsq_fm[:], in_=buf[:, hh * 512:(hh + 1) * 512],
                                 func=AF.Copy, accum_out=stat_h[:, hh:hh + 1])
                    ins = e.activation(out=xsq_fm[:], in_=buf[:, hh * 512:(hh + 1) * 512],
                                       func=AF.Square, accum_out=stat_h[:, 2 + hh:3 + hh])
                return ins
            tok_s = p.g("scalar", p0s, after=[tok_ld, tok_c0])

            def p0r(e, tc=tc):
                e.tensor_add(out=stat_s[:, 0:1], in0=stat_h[:, 0:1], in1=stat_h[:, 1:2])
                e.tensor_add(out=stat_q[:, 0:1], in0=stat_h[:, 2:3], in1=stat_h[:, 3:4])
                e.tensor_scalar_mul(out=stat_m[:, 0:1], in0=stat_s[:, 0:1], scalar1=1.0 / D)
                e.tensor_scalar_mul(out=stat_q[:, 0:1], in0=stat_q[:, 0:1], scalar1=1.0 / D)
                e.tensor_tensor(out=stat_i[:, 0:1], in0=stat_m[:, 0:1], in1=stat_m[:, 0:1],
                                op=OP.mult)
                e.tensor_tensor(out=stat_q[:, 0:1], in0=stat_q[:, 0:1],
                                in1=stat_i[:, 0:1], op=OP.subtract)
                return e.tensor_scalar_add(out=stat_q[:, 0:1], in0=stat_q[:, 0:1],
                                           scalar1=1e-5)
            tok_v = p.g("vector", p0r, after=[tok_s])
            tok_q = p.g("scalar", (lambda e: e.activation(
                out=stat_q[:, 0:1], in_=stat_q[:, 0:1], func=AF.Sqrt,
                scale=1.0)), after=[tok_v])

            def p0a(e, tc=tc, buf=buf):
                e.reciprocal(out=stat_i[:, 0:1], in_=stat_q[:, 0:1])
                return e.tensor_scalar(out=buf[:], in0=buf[:],
                                       scalar1=stat_m[:, 0:1],
                                       scalar2=stat_i[:, 0:1],
                                       op0=OP.subtract, op1=OP.mult)
            tok_ap = p.g("vector", p0a, after=[tok_q])
            if debug and t == dbg_tile and tc == 0 and os.environ.get("DK_DUMP_BUF") == "3":
                if os.environ.get("DK_PRINT_SEQ"):
                    print("DUMP_BUF3 seq", p.seq)
                dbg_after.append(p.dma_g("gpsimd",
                                         (lambda e: [e.dma_start(out=dbg["d_u1"][:, 0:4], in_=stat_h[:, 0:4], allow_slow_non_contiguous=True),
                                                     e.dma_start(out=dbg["d_u1"][:, 4:5], in_=stat_m[:, 0:1], allow_slow_non_contiguous=True),
                                                     e.dma_start(out=dbg["d_u1"][:, 5:6], in_=stat_q[:, 0:1], allow_slow_non_contiguous=True),
                                                     e.dma_start(out=dbg["d_u1"][:, 6:7], in_=stat_i[:, 0:1], allow_slow_non_contiguous=True)]),
                                         4, after=[tok_ap]))
            if debug and t == dbg_tile and tc == 0 and os.environ.get("DK_DUMP_BUF") == "2":
                if os.environ.get("DK_PRINT_SEQ"):
                    print("DUMP_BUF2 seq", p.seq)
                dbg_after.append(p.dma_g("gpsimd",
                                         (lambda e, buf=buf: [e.dma_start(
                                             out=dbg["d_u1"][:, 0:1024], in_=buf[:])]),
                                         1, after=[tok_ap]))
            for gi in range(2):
                bk = (tc % 2) * 2 + gi
                def tpg(e, tc=tc, gi=gi, bk=bk, buf=buf):
                    ins = None
                    for pc4 in range(4):
                        pc = gi * 4 + pc4
                        ins = e.matmul(out=BK(bk)[:, pc4 * 128:(pc4 + 1) * 128],
                                       lhsT=buf[:, pc * 128:(pc + 1) * 128],
                                       rhs=id_f32[:], start=(pc4 == 0), stop=(pc4 == 3))
                    return ins
                tk = p.g("tensor", tpg, after=[tok_ap, tok_c0], wbank=[bk])
                def tpe(e, tc=tc, gi=gi, bk=bk):
                    ins = None
                    for pc4 in range(4):
                        pc = gi * 4 + pc4
                        ins = e.tensor_copy(out=fmc(x_fm, pc, tc),
                                            in_=BK(bk)[:, pc4 * 128:(pc4 + 1) * 128])
                    return ins
                tok_u1 = p.g("vector", tpe, after=[tk], rbank=[bk])
            tmb_free[tc % 2] = tok_u1
        dump_f(t, "d_u1", x_fm, [tok_u1])

        tk_prev = None
        for pc in range(PC):
            sq_tok = p.g("scalar", (lambda e, pc=pc: e.activation(
                out=xsq_fm[:], in_=fm(x_fm, pc), func=AF.Square)),
                after=[tok_u1 if pc == 0 else tk_prev])
            def lna_mm(e, pc=pc):
                e.matmul(out=BK(4)[0:1, 0:512], lhsT=w1b1[:, pc:pc + 1], rhs=fm(x_fm, pc),
                         start=(pc == 0), stop=(pc == PC - 1))
                e.matmul(out=BK(5)[0:1, 0:512], lhsT=w1sq[:, pc:pc + 1], rhs=xsq_fm[:],
                         start=(pc == 0), stop=(pc == PC - 1))
                return e.matmul(out=BK(6)[0:1, 0:512], lhsT=VC["ln_pre_w"][:, pc:pc + 1],
                                rhs=fm(x_fm, pc), start=(pc == 0), stop=(pc == PC - 1))
            tk_prev = p.g("tensor", lna_mm, after=[sq_tok, tok_u1],
                          wbank=[4, 5, 6] if pc == 0 else [])
        def lna_rows(e):
            e.tensor_scalar(out=lnmu, in0=BK(6)[0:1, 0:512],
                            scalar1=sb1[0:1, 0:1], scalar2=1.0 / D, op0=OP.add, op1=OP.mult)
            e.tensor_scalar_mul(out=xsq_fm[0:1, :], in0=BK(4)[0:1, 0:512], scalar1=2.0)
            e.tensor_tensor(out=xsq_fm[0:1, :], in0=xsq_fm[0:1, :], in1=BK(5)[0:1, 0:512],
                            op=OP.add)
            e.tensor_scalar(out=xsq_fm[0:1, :], in0=xsq_fm[0:1, :],
                            scalar1=sb1[0:1, 1:2], scalar2=1.0 / D, op0=OP.add, op1=OP.mult)
            e.tensor_tensor(out=s2w_f[0:1, :], in0=lnmu, in1=lnmu, op=OP.mult)
            e.tensor_tensor(out=lniv, in0=xsq_fm[0:1, :], in1=s2w_f[0:1, :],
                            op=OP.subtract)
            return e.tensor_scalar_add(out=lniv, in0=lniv, scalar1=1e-5)
        tok_lr = p.g("vector", lna_rows, after=[tk_prev, tok_sb1e], rbank=[4, 5, 6])
        _lr = [tok_lr]
        if debug and t == dbg_tile and os.environ.get("DK_DUMP_LNA"):
            _d = p.dma_g("gpsimd",
                         (lambda e: [e.dma_start(out=dbg["d_u1"][4:5, 0:512], in_=lniv)]),
                         1, after=[tok_lr])
            dbg_after.append(_d); _lr.append(_d)
        tok_lq = p.g("scalar", lambda e: e.activation(out=lniv, in_=lniv,
                                                      func=AF.Sqrt, scale=1.0),
                     after=_lr)
        _lq = [tok_lq]
        if debug and t == dbg_tile and os.environ.get("DK_DUMP_LNA"):
            _d = p.dma_g("gpsimd",
                         (lambda e: [e.dma_start(out=dbg["d_u1"][5:6, 0:512], in_=lniv)]),
                         1, after=[tok_lq])
            dbg_after.append(_d); _lq.append(_d)
        def lna_i(e):
            e.reciprocal(out=lniv, in_=lniv)
            e.tensor_tensor(out=s2w_f[0:1, :], in0=lnmu, in1=lniv, op=OP.mult)
            return e.tensor_scalar_mul(out=s2w_f[0:1, :], in0=s2w_f[0:1, :], scalar1=-1.0)
        tok_li = p.g("vector", lna_i, after=_lq)

        if debug and t == dbg_tile and os.environ.get("DK_DUMP_LNA"):
            if os.environ.get("DK_PRINT_SEQ"):
                print("DUMP_LNA seq", p.seq)
            dbg_after.append(p.dma_g("gpsimd",
                                     (lambda e: [e.dma_start(out=dbg["d_u1"][0:1, 0:512], in_=lnmu),
                                                 e.dma_start(out=dbg["d_u1"][1:2, 0:512], in_=lniv),
                                                 e.dma_start(out=dbg["d_u1"][2:3, 0:512], in_=s2w_f[0:1, :]),
                                                 e.dma_start(out=dbg["d_u1"][3:4, 0:2], in_=sb1[0:1, :]),
                                                 e.dma_start(out=dbg["d_u1"][:, 8:10], in_=scr_red[:, 0:2], allow_slow_non_contiguous=True),
                                                 e.dma_start(out=dbg["d_u1"][:, 16:16 + PC], in_=gscr[:, 0:PC], allow_slow_non_contiguous=True)]),
                                     6, after=[tok_li]))
        def lna_bc(e):
            e.matmul(out=BK(5), lhsT=ones_r128[:], rhs=lniv, start=True, stop=True)
            return e.matmul(out=BK(6), lhsT=ones_r128[:], rhs=s2w_f[0:1, :],
                            start=True, stop=True)
        tok_lbc = p.g("tensor", lna_bc, after=[tok_li], wbank=[5, 6])
        tok_xn_last = None
        for pc in range(PC):
            def lna_xn(e, pc=pc):
                e.tensor_scalar(out=xsq_fm[:], in0=BK(5),
                                scalar1=w2b1c[:, pc:pc + 1],
                                scalar2=VC["ln_attn_b"][:, pc:pc + 1],
                                op0=OP.mult, op1=OP.add)
                e.scalar_tensor_tensor(out=xsq_fm[:], in0=BK(6),
                                       scalar=VC["ln_attn_w"][:, pc:pc + 1],
                                       in1=xsq_fm[:], op0=OP.mult, op1=OP.add)
                e.tensor_tensor(out=bt2_f[:], in0=fm(x_fm, pc), in1=BK(5), op=OP.mult)
                return e.scalar_tensor_tensor(out=fm(xn_bf, pc), in0=bt2_f[:],
                                              scalar=w1w2c[:, pc:pc + 1],
                                              in1=xsq_fm[:], op0=OP.mult, op1=OP.add)
            tok_xn_last = p.g("vector", (lambda e, pc=pc: lna_xn(e, pc)),
                              after=[tok_lbc], rbank=[5, 6])
        dump_bf(t, "d_xn", xn_bf, [tok_xn_last])

        def p_dx(e):
            ins = None
            for pc in range(PC):
                e.tensor_sub(out=fm(dx_bf, pc)[:, 1:TILE], in0=fm(xn_bf, pc)[:, 0:TILE - 1],
                             in1=fm(xn_bf, pc)[:, 1:TILE])
                e.tensor_sub(out=fm(dx_bf, pc)[:, 0:1], in0=xn_halo[:, pc:pc + 1],
                             in1=fm(xn_bf, pc)[:, 0:1])
                ins = e.tensor_copy(out=xn_halo[:, pc:pc + 1],
                                    in_=fm(xn_bf, pc)[:, TILE - 1:TILE])
            return ins
        tok_dx = p.g("vector", p_dx, after=[tok_xn_last])

        def dense_mm(hu0, rhs_of, evac_fn, evac_eng, gname=None, after0=()):
            ensure_hu(t, hu0 + 1)
            mm_toks = []
            for kc in range(PC):
                mm_after = []
                if gname is not None:
                    def mkmix(e, kc=kc):
                        return e.scalar_tensor_tensor(
                            out=mixp[kc % 2][:], in0=fm(dx_bf, kc),
                            scalar=VC[gname][:, kc:kc + 1], in1=fm(xn_bf, kc),
                            op0=OP.mult, op1=OP.add)
                    aft = [tok_dx] + list(after0)
                    if kc >= 2:
                        aft.append(mm_toks[kc - 2])
                    mm_after.append(p.g("vector", mkmix, after=aft))
                def mmk(e, kc=kc, hu0=hu0):
                    ins = None
                    rhs = mixp[kc % 2][:] if gname is not None else rhs_of(kc)
                    for mc in range(8):
                        ins = e.matmul(out=BK(mc), lhsT=hu_sb(hu0 + mc // 4, kc, mc % 4),
                                       rhs=rhs, start=(kc == 0), stop=(kc == PC - 1))
                    return ins
                mm_after += [wload[(t, hu0)], wload[(t, hu0 + 1)]] + list(after0)
                mm_toks.append(p.g("tensor", mmk, after=mm_after,
                                   wbank=list(range(8)) if kc == 0 else []))
            def evall(e):
                ins = None
                for mc in range(8):
                    ins = evac_fn(e, mc, BK(mc))
                return ins
            ev = p.g(evac_eng, evall, after=[mm_toks[-1]], rbank=list(range(8)))
            slot_free[hu0 % 2] = mm_toks[-1]
            slot_free[(hu0 + 1) % 2] = mm_toks[-1]
            return mm_toks[-1], ev

        mm_r, ev_r_t = dense_mm(0, None, (lambda e, mc, bk: e.tensor_copy(
            out=fm(r_bf, mc), in_=bk)), "vector", gname="x_r")

        def ev_k(e, mc, bk):
            e.tensor_copy(out=fm(km_bf, mc), in_=bk)
            return e.tensor_scalar_mul(out=fm(kk_bf, mc), in0=bk,
                                       scalar1=VC["k_k"][:, mc:mc + 1])
        mm_k, ev_k_t = dense_mm(2, None, ev_k, "vector", gname="x_k")

        mm_v, ev_v_t = dense_mm(4, None, (lambda e, mc, bk: e.tensor_copy(
            out=fm(v_bf, mc), in_=bk)), "vector", gname="x_v")

        def build_mix_full(gname, aft):
            def mk(e):
                ins = None
                for kc in range(PC):
                    ins = e.scalar_tensor_tensor(
                        out=fm(o_bf, kc), in0=fm(dx_bf, kc),
                        scalar=VC[gname][:, kc:kc + 1], in1=fm(xn_bf, kc),
                        op0=OP.mult, op1=OP.add)
                return ins
            return p.g("vector", mk, after=aft)

        tok_mixw = build_mix_full("x_w", [ev_v_t])
        def w1mm(e):
            ins = None
            for kc in range(PC):
                ins = e.matmul(out=BK(0)[0:64, :], lhsT=w1_sb[:, kc * 64:(kc + 1) * 64],
                               rhs=fm(o_bf, kc), start=(kc == 0), stop=(kc == PC - 1))
            return ins
        tok_w1 = p.g("tensor", w1mm, after=[tok_mixw], wbank=[0])
        tok_w1e = p.g("scalar", lambda e: e.activation(out=t1_bf[:], in_=BK(0)[0:64, :],
                                                       func=AF.Tanh),
                      after=[tok_w1], rbank=[0])
        tok_we = None
        for wv in range(2):
            def w2mm(e, wv=wv):
                ins = None
                for mc4 in range(4):
                    mc = wv * 4 + mc4
                    ins = e.matmul(out=BK(4 + mc4), lhsT=w2_sb[:, mc * 128:(mc + 1) * 128],
                                   rhs=t1_bf[:], start=True, stop=True)
                return ins
            mt = p.g("tensor", w2mm, after=[tok_w1e], wbank=[4, 5, 6, 7])
            def w2ev(e, wv=wv):
                ins = None
                for mc4 in range(4):
                    mc = wv * 4 + mc4
                    ins = e.activation(out=fm(e_bf, mc), in_=BK(4 + mc4),
                                       func=AF.Sigmoid, bias=VC["w0"][:, mc:mc + 1])
                return ins
            tok_we = p.g("scalar", w2ev, after=[mt], rbank=[4, 5, 6, 7])
        tok_mixa = build_mix_full("x_a", [tok_w1])
        def a1mm(e):
            ins = None
            for kc in range(PC):
                ins = e.matmul(out=BK(1)[0:64, :], lhsT=a1_sb[:, kc * 64:(kc + 1) * 64],
                               rhs=fm(o_bf, kc), start=(kc == 0), stop=(kc == PC - 1))
            return ins
        tok_a1 = p.g("tensor", a1mm, after=[tok_mixa], wbank=[1])
        tok_a1e = p.g("scalar", lambda e: e.copy(out=t1_bf[:], in_=BK(1)[0:64, :]),
                      after=[tok_a1, tok_we], rbank=[1])
        tok_ae = None
        for wv in range(2):
            def a2mm(e, wv=wv):
                ins = None
                for mc4 in range(4):
                    mc = wv * 4 + mc4
                    ins = e.matmul(out=BK(mc4), lhsT=a2_sb[:, mc * 128:(mc + 1) * 128],
                                   rhs=t1_bf[:], start=True, stop=True)
                return ins
            mt = p.g("tensor", a2mm, after=[tok_a1e], wbank=[0, 1, 2, 3])
            def a2ev(e, wv=wv):
                ins = None
                for mc4 in range(4):
                    mc = wv * 4 + mc4
                    ins = e.activation(out=fm(b_bf, mc), in_=BK(mc4),
                                       func=AF.Sigmoid, bias=VC["a0"][:, mc:mc + 1])
                return ins
            tok_ae = p.g("scalar", a2ev, after=[mt], rbank=[0, 1, 2, 3])
        tok_mixg = build_mix_full("x_g", [tok_a1])
        def g1mm(e):
            ins = None
            for kc in range(PC):
                ins = e.matmul(out=BK(2), lhsT=g1_sb[:, kc * 128:(kc + 1) * 128],
                               rhs=fm(o_bf, kc), start=(kc == 0), stop=(kc == PC - 1))
            return ins
        tok_g1 = p.g("tensor", g1mm, after=[tok_mixg], wbank=[2])
        tok_g1e = p.g("scalar", lambda e: e.activation(out=sg_bf[:], in_=BK(2),
                                                       func=AF.Sigmoid),
                      after=[tok_g1], rbank=[2])
        tok_ge = None
        for wv in range(2):
            def g2mm(e, wv=wv):
                ins = None
                for mc4 in range(4):
                    mc = wv * 4 + mc4
                    ins = e.matmul(out=BK(4 + mc4), lhsT=g2_sb[:, mc * 128:(mc + 1) * 128],
                                   rhs=sg_bf[:], start=True, stop=True)
                return ins
            mt = p.g("tensor", g2mm, after=[tok_g1e], wbank=[4, 5, 6, 7])
            def g2ev(e, wv=wv):
                ins = None
                for mc4 in range(4):
                    mc = wv * 4 + mc4
                    ins = e.copy(out=fm(g_bf, mc), in_=BK(4 + mc4))
                return ins
            tok_ge = p.g("scalar", g2ev, after=[mt], rbank=[4, 5, 6, 7])

        mm_sq_tok = None
        for kc in range(PC):
            sq_t = p.g("vector", (lambda e, kc=kc: e.tensor_mul(
                out=pp_bf[:], in0=fm(kk_bf, kc), in1=fm(kk_bf, kc))),
                after=[ev_k_t] if kc == 0 else [mm_sq_tok])
            mm_sq_tok = p.g("tensor", (lambda e, kc=kc: e.matmul(
                out=BK(3)[0:16, :], lhsT=seg16b[:, kc * 16:(kc + 1) * 16],
                rhs=pp_bf[:], start=(kc == 0), stop=(kc == PC - 1))),
                after=[sq_t], wbank=[3] if kc == 0 else [])
        tok_kn1 = p.g("scalar", lambda e: e.activation(out=hrow[:, :], in_=BK(3)[0:16, :],
                                                       func=AF.Sqrt),
                      after=[mm_sq_tok], rbank=[3])
        def kkn2(e):
            e.tensor_scalar_max(out=hrow[:, :], in0=hrow[:, :], scalar1=1e-12)
            return e.reciprocal(out=hrow2[:, :], in_=hrow[:, :])
        tok_kn2 = p.g("vector", kkn2, after=[tok_kn1])
        tok_kkn = None
        for pc in range(PC):
            mt = p.g("tensor", (lambda e, pc=pc: e.matmul(
                out=BK(2), lhsT=bc16[0:16, pc * 128:(pc + 1) * 128],
                rhs=hrow2[0:16, :], start=True, stop=True)),
                after=[tok_kn2], wbank=[2])
            def kkap(e, pc=pc):
                e.tensor_tensor(out=fm(kk_bf, pc), in0=fm(kk_bf, pc), in1=BK(2), op=OP.mult)
                e.tensor_scalar(out=pp_bf[:], in0=fm(b_bf, pc),
                                scalar1=VC["k_a"][:, pc:pc + 1],
                                scalar2=VC["omka"][:, pc:pc + 1], op0=OP.mult, op1=OP.add)
                e.tensor_tensor(out=fm(km_bf, pc), in0=fm(km_bf, pc), in1=pp_bf[:], op=OP.mult)
                return e.tensor_tensor(out=fm(b_bf, pc), in0=fm(kk_bf, pc), in1=fm(b_bf, pc),
                                       op=OP.mult)
            tok_kkn = p.g("vector", (lambda e, pc=pc: kkap(e, pc)), after=[mt, tok_ae],
                          rbank=[2])
        dump_bf(t, "d_r", r_bf, [ev_r_t])
        dump_bf(t, "d_e", e_bf, [tok_we])
        dump_bf(t, "d_kk", kk_bf, [tok_kkn])
        dump_bf(t, "d_kmod", km_bf, [tok_kkn])
        dump_bf(t, "d_v", v_bf, [ev_v_t])
        dump_bf(t, "d_b", b_bf, [tok_kkn])
        dump_bf(t, "d_g", g_bf, [tok_ge])

        def tilG(nm, j, c):
            return TIL[nm][:, j * TILE + c * 128: j * TILE + (c + 1) * 128]

        def e2G(j, c):
            return E2G[:, j * TILE + c * 128: j * TILE + (c + 1) * 128]

        tok_grp = tok_kkn
        tok_oev = None
        for G in range(2):
            til_last = None
            for j in range(4):
                pc = 4 * G + j
                tk_cs = p.g("vector", (lambda e, pc=pc: e.tensor_tensor_scan(
                    out=cs_f[:], data0=rscan[:], data1=fm(e_bf, pc), initial=0.0,
                    op0=OP.mult, op1=OP.add)),
                    after=[tok_we, tok_grp, til_last])
                def mkcse(e, pc=pc):
                    e.tensor_tensor(out=cse_f[:], in0=cs_f[:], in1=fm(e_bf, pc),
                                    op=OP.subtract)
                    return e.tensor_copy(
                        out=csL_f[:, 0:TC],
                        in_=AP(cs_f.ap().tensor, 127, [[TILE, 128], [128, TC]]))
                tk_c2 = p.g("vector", mkcse, after=[tk_cs])
                def mkexp1(e, pc=pc, j=j):
                    e.activation(out=dec[0][:], in_=cs_f[:], func=AF.Exp, scale=-SC)
                    e.activation(out=dec[1][:], in_=cse_f[:], func=AF.Exp, scale=-SC)
                    return e.activation(out=DLG[:, j * TC:(j + 1) * TC],
                                        in_=csL_f[:, 0:TC], func=AF.Exp, scale=-SC)
                tk_ex = p.g("scalar", mkexp1, after=[tk_c2])
                def mktil1(e, pc=pc, j=j):
                    sl = slice(j * TILE, (j + 1) * TILE)
                    e.tensor_mul(out=TIL["rt"][:, sl], in0=fm(r_bf, pc), in1=dec[0][:])
                    e.scalar_tensor_tensor(out=TIL["at"][:, sl], in0=fm(kk_bf, pc),
                                           scalar=-1.0, in1=dec[1][:],
                                           op0=OP.mult, op1=OP.mult)
                    ins = None
                    for cq in range(TC):
                        ins = e.tensor_scalar(
                            out=cse_f[:, cq * 128:(cq + 1) * 128],
                            in0=cs_f[:, cq * 128:(cq + 1) * 128],
                            scalar1=csL_f[:, cq:cq + 1], scalar2=None, op0=OP.subtract)
                    return ins
                tk_t1 = p.g("vector", mktil1, after=[tk_ex])
                def mkexp2(e, pc=pc, j=j):
                    e.activation(out=dec[0][:], in_=cs_f[:], func=AF.Exp, scale=SC)
                    return e.activation(out=E2G[:, j * TILE:(j + 1) * TILE], in_=cse_f[:],
                                        func=AF.Exp, scale=SC)
                tk_e2 = p.g("scalar", mkexp2, after=[tk_t1])
                def mktil2(e, pc=pc, j=j):
                    sl = slice(j * TILE, (j + 1) * TILE)
                    e.tensor_mul(out=TIL["bt"][:, sl], in0=fm(b_bf, pc), in1=dec[0][:])
                    return e.tensor_mul(out=TIL["kt"][:, sl], in0=fm(km_bf, pc), in1=dec[0][:])
                til_last = p.g("vector", mktil2, after=[tk_e2])
            tok_til = til_last

            tok_state_ev = None
            for c in range(TC):
                def ghmm(e, G=G, c=c):
                    ins = None
                    for q, (dst, sa, sb) in enumerate(
                            [(PSA, "bt", "at"), (PSB, "at", "bt"), (PSC, "kt", "at")]):
                        for hf in range(2):
                            for j in range(4):
                                sl = 4 * hf + j
                                ins = e.matmul(
                                    out=dst[:, sl * 128:(sl + 1) * 128],
                                    lhsT=tilG(sa, j, c)[hf * 64:(hf + 1) * 64, :],
                                    rhs=tilG(sb, j, c)[hf * 64:(hf + 1) * 64, :],
                                    start=(j == 0), stop=(j == 3),
                                    tile_position=(64 * hf, 0))
                    return ins
                tok_gh = p.g("tensor", ghmm, after=[tok_til, tok_state_ev],
                             wbank=[0, 1, 2, 3, 4, 5])
                def ghev(e):
                    e.tensor_tensor(out=h8(X_sb), in0=ps_h8(PSA), in1=mask8(msk_su),
                                    op=OP.mult)
                    e.tensor_tensor(out=h8(XT_sb), in0=ps_h8(PSB), in1=mask8(msk_sl),
                                    op=OP.mult)
                    return e.tensor_tensor(out=h8(Hm_sb), in0=ps_h8(PSC), in1=mask8(msk_su),
                                           op=OP.mult)
                tok_ghe = p.g("vector", ghev, after=[tok_gh], rbank=[0, 1, 2, 3, 4, 5])
                def vtmm(e, G=G, c=c):
                    ins = None
                    for hf in range(2):
                        for j in range(4):
                            ins = e.matmul(
                                out=BK(6 + hf)[:, j * 64:(j + 1) * 64],
                                lhsT=fmc(v_bf, 4 * G + j, c)[hf * 64:(hf + 1) * 64, :],
                                rhs=id_bf[hf * 64:(hf + 1) * 64, hf * 64:hf * 64 + 64],
                                start=(j == 0), stop=(j == 3),
                                tile_position=(64 * hf, 0))
                    return ins
                tok_vt = p.g("tensor", vtmm, after=[], wbank=[6, 7])
                def vtev(e):
                    e.copy(out=vtm_sb[:, 0:256], in_=BK(6)[:, 0:256])
                    return e.copy(out=vtm_sb[:, 256:512], in_=BK(7)[:, 0:256])
                tok_vte = p.g("scalar", vtev, after=[tok_vt], rbank=[6, 7])
                def rhsA(e, G=G, c=c):
                    ins = None
                    for hf in range(2):
                        for j in range(4):
                            ins = e.matmul(
                                out=BK(hf)[hf * 64:(hf + 1) * 64, j * 128:(j + 1) * 128],
                                lhsT=sT_bf[hf * 64:(hf + 1) * 64,
                                           (4 * G + j) * 64:(4 * G + j + 1) * 64],
                                rhs=tilG("at", j, c)[hf * 64:(hf + 1) * 64, :],
                                start=(j == 0), stop=(j == 3),
                                tile_position=(64 * hf, 64 * hf))
                    return ins
                tok_rhsA = p.g("tensor", rhsA, after=[tok_state_ev, tok_til],
                               wbank=[0, 1])
                def rhs0(e):
                    e.copy(out=pp_bf[0:64, :], in_=BK(0)[0:64, 0:512])
                    return e.copy(out=pp_bf[64:128, :], in_=BK(1)[64:128, 0:512])
                tok_rhs0 = p.g("scalar", rhs0, after=[tok_rhsA], rbank=[0, 1])
                def rhsB(e, G=G, c=c):
                    ins = None
                    for hf in range(2):
                        for j in range(4):
                            sl = 4 * hf + j
                            ins = e.matmul(
                                out=BK(2)[hf * 64:(hf + 1) * 64, j * 128:(j + 1) * 128],
                                lhsT=vtm_sb[:, sl * 64:(sl + 1) * 64],
                                rhs=Hm_sb[:, sl * 128:(sl + 1) * 128],
                                start=(j == 0), stop=(j == 3),
                                tile_position=(0, 64 * hf))
                    return ins
                tok_rhs = p.g("tensor", rhsB, after=[tok_vte, tok_ghe], wbank=[2])
                tok_rhse = p.g("vector", lambda e: e.tensor_tensor(
                    out=rhs_sb, in0=BK(2)[:, 0:512], in1=pp_bf[:], op=OP.add),
                               after=[tok_rhs0, tok_rhs], rbank=[2])
                def u0mm(e):
                    ins = None
                    for hf in range(2):
                        for j in range(4):
                            ins = e.matmul(
                                out=BK(hf)[:, j * 64:(j + 1) * 64],
                                lhsT=rhs_sb[hf * 64:(hf + 1) * 64, j * 128:(j + 1) * 128],
                                rhs=id_bf[hf * 64:(hf + 1) * 64, hf * 64:hf * 64 + 64],
                                start=(j == 0), stop=(j == 3),
                                tile_position=(64 * hf, 0))
                    return ins
                tok_u0 = p.g("tensor", u0mm, after=[tok_rhse], wbank=[0, 1])
                def u0ev(e):
                    e.copy(out=u_sb[:, 0:256], in_=BK(0)[:, 0:256])
                    return e.copy(out=u_sb[:, 256:512], in_=BK(1)[:, 0:256])
                tok_ue = p.g("scalar", u0ev, after=[tok_u0], rbank=[0, 1])
                tok_xe = None
                for ks in range(7):
                    do_sq = ks < 6
                    def ummm(e):
                        ins = None
                        for hg in range(8):
                            ins = e.matmul(out=BK(0)[:, hg * 64:(hg + 1) * 64],
                                           lhsT=X_sb[:, hg * 128:(hg + 1) * 128],
                                           rhs=u_sb[:, hg * 64:(hg + 1) * 64],
                                           start=(hg == 0), stop=(hg == 7))
                        return ins
                    tok_um = p.g("tensor", ummm, after=[tok_ue, tok_xe], wbank=[0])
                    tok_sq = None
                    if do_sq:
                        def sqmm(e, ks=ks):
                            ins = None
                            for hg in range(8):
                                ins = e.matmul(out=PSB[:, hg * 128:(hg + 1) * 128],
                                               lhsT=XT_sb[:, hg * 128:(hg + 1) * 128],
                                               rhs=X_sb[:, hg * 128:(hg + 1) * 128],
                                               start=(hg % 4 == 0), stop=(hg % 4 == 3))
                            if ks < 5:
                                for hg in range(8):
                                    ins = e.matmul(out=PSC[:, hg * 128:(hg + 1) * 128],
                                                   lhsT=X_sb[:, hg * 128:(hg + 1) * 128],
                                                   rhs=XT_sb[:, hg * 128:(hg + 1) * 128],
                                                   start=(hg % 4 == 0), stop=(hg % 4 == 3))
                            return ins
                        tok_sq = p.g("tensor", sqmm, after=[tok_xe],
                                     wbank=[2, 3] + ([4, 5] if ks < 5 else []))
                    tok_ue = p.g("vector", lambda e: e.tensor_tensor(
                        out=u_sb, in0=BK(0)[:, 0:512], in1=u_sb, op=OP.add),
                                 after=[tok_um], rbank=[0])
                    if do_sq:
                        def chev(e, ks=ks):
                            ins = e.tensor_copy(out=X_sb, in_=PSB[:])
                            if ks < 5:
                                ins = e.tensor_copy(out=XT_sb, in_=PSC[:])
                            return ins
                        tok_xe = p.g("vector", chev, after=[tok_sq],
                                     rbank=[2, 3] + ([4, 5] if ks < 5 else []))
                def m12mm(e, G=G, c=c):
                    ins = None
                    for q, (dst, sa) in enumerate([(PSB, "bt"), (PSC, "kt")]):
                        for hf in range(2):
                            for j in range(4):
                                sl = 4 * hf + j
                                ins = e.matmul(
                                    out=dst[:, sl * 128:(sl + 1) * 128],
                                    lhsT=tilG(sa, j, c)[hf * 64:(hf + 1) * 64, :],
                                    rhs=tilG("rt", j, c)[hf * 64:(hf + 1) * 64, :],
                                    start=(j == 0), stop=(j == 3),
                                    tile_position=(64 * hf, 0))
                    return ins
                tok_m12 = p.g("tensor", m12mm, after=[tok_xe], wbank=[2, 3, 4, 5])
                def m12ev(e):
                    e.tensor_tensor(out=h8(M1_sb), in0=ps_h8(PSB), in1=mask8(msk_iu),
                                    op=OP.mult)
                    return e.tensor_tensor(out=h8(M2_sb), in0=ps_h8(PSC), in1=mask8(msk_iu),
                                           op=OP.mult)
                tok_m12e = p.g("vector", m12ev, after=[tok_m12], rbank=[2, 3, 4, 5])
                def outA(e, G=G, c=c):
                    ins = None
                    for hf in range(2):
                        for j in range(4):
                            ins = e.matmul(
                                out=BK(hf)[hf * 64:(hf + 1) * 64, j * 128:(j + 1) * 128],
                                lhsT=sT_bf[hf * 64:(hf + 1) * 64,
                                           (4 * G + j) * 64:(4 * G + j + 1) * 64],
                                rhs=tilG("rt", j, c)[hf * 64:(hf + 1) * 64, :],
                                start=(j == 0), stop=(j == 3),
                                tile_position=(64 * hf, 64 * hf))
                    return ins
                tok_outA = p.g("tensor", outA, after=[tok_state_ev, tok_til],
                               wbank=[0, 1])
                def out0(e):
                    e.copy(out=pp_bf[0:64, :], in_=BK(0)[0:64, 0:512])
                    return e.copy(out=pp_bf[64:128, :], in_=BK(1)[64:128, 0:512])
                tok_out0 = p.g("scalar", out0, after=[tok_outA], rbank=[0, 1])
                def outB(e, G=G, c=c):
                    ins = None
                    for hf in range(2):
                        for j in range(4):
                            sl = 4 * hf + j
                            e.matmul(out=BK(6)[hf * 64:(hf + 1) * 64, j * 128:(j + 1) * 128],
                                     lhsT=u_sb[:, sl * 64:(sl + 1) * 64],
                                     rhs=M1_sb[:, sl * 128:(sl + 1) * 128],
                                     start=(j == 0), stop=False, tile_position=(0, 64 * hf))
                        for j in range(4):
                            sl = 4 * hf + j
                            ins = e.matmul(out=BK(6)[hf * 64:(hf + 1) * 64,
                                                     j * 128:(j + 1) * 128],
                                           lhsT=vtm_sb[:, sl * 64:(sl + 1) * 64],
                                           rhs=M2_sb[:, sl * 128:(sl + 1) * 128],
                                           start=False, stop=(j == 3),
                                           tile_position=(0, 64 * hf))
                    return ins
                tok_out = p.g("tensor", outB, after=[tok_m12e, tok_ue], wbank=[6])
                def outev(e, G=G, c=c):
                    return e.tensor_tensor(
                        out=AP(o_bf.ap().tensor, (4 * G) * TILE + c * 128,
                               [[PC * TILE, 128], [TILE, 4], [1, 128]]),
                        in0=BK(6).rearrange("p (j x) -> p j x", j=4),
                        in1=pp_bf[:].rearrange("p (j x) -> p j x", j=4), op=OP.add)
                tok_oev = p.g("vector", outev, after=[tok_out0], rbank=[6])
                def bbkbf(e, G=G, c=c):
                    ins = None
                    for j in range(4):
                        e.tensor_mul(out=bbkb[:, j * 128:(j + 1) * 128],
                                     in0=fmc(b_bf, 4 * G + j, c), in1=e2G(j, c))
                        ins = e.tensor_mul(out=bbkb[:, 512 + j * 128: 512 + (j + 1) * 128],
                                           in0=fmc(km_bf, 4 * G + j, c), in1=e2G(j, c))
                    return ins
                tok_bk = p.g("vector", bbkbf, after=[tok_til])
                def sttr(e):
                    ins = None
                    for q in range(2):
                        for hf in range(2):
                            for j in range(4):
                                ins = e.matmul(
                                    out=BK(2 + 2 * q + hf)[:, j * 64:(j + 1) * 64],
                                    lhsT=bbkb[hf * 64:(hf + 1) * 64,
                                              q * 512 + j * 128: q * 512 + (j + 1) * 128],
                                    rhs=id_bf[hf * 64:(hf + 1) * 64, hf * 64:hf * 64 + 64],
                                    start=(j == 0), stop=(j == 3),
                                    tile_position=(64 * hf, 0))
                    return ins
                tok_st1 = p.g("tensor", sttr, after=[tok_bk], wbank=[2, 3, 4, 5])
                def sttre(e):
                    e.copy(out=btm_sb[:, 0:256], in_=BK(2)[:, 0:256])
                    e.copy(out=btm_sb[:, 256:512], in_=BK(3)[:, 0:256])
                    e.copy(out=ktm_sb[:, 0:256], in_=BK(4)[:, 0:256])
                    return e.copy(out=ktm_sb[:, 256:512], in_=BK(5)[:, 0:256])
                tok_st1e = p.g("scalar", sttre, after=[tok_st1], rbank=[2, 3, 4, 5])
                def stprod(e, G=G, c=c):
                    ins = None
                    for hf in range(2):
                        for j in range(4):
                            sl = 4 * hf + j
                            e.matmul(out=BK(4)[hf * 64:(hf + 1) * 64,
                                               j * 64:(j + 1) * 64],
                                     lhsT=btm_sb[:, sl * 64:(sl + 1) * 64],
                                     rhs=u_sb[:, sl * 64:(sl + 1) * 64],
                                     start=(j == 0), stop=False,
                                     tile_position=(0, 64 * hf))
                        for j in range(4):
                            sl = 4 * hf + j
                            ins = e.matmul(out=BK(4)[hf * 64:(hf + 1) * 64,
                                                     j * 64:(j + 1) * 64],
                                           lhsT=ktm_sb[:, sl * 64:(sl + 1) * 64],
                                           rhs=vtm_sb[:, sl * 64:(sl + 1) * 64],
                                           start=False, stop=(j == 3),
                                           tile_position=(0, 64 * hf))
                    return ins
                tok_st2 = p.g("tensor", stprod, after=[tok_st1e], wbank=[4])
                def stev(e, G=G, c=c):
                    for j in range(4):
                        e.scalar_tensor_tensor(
                            out=sT[:, (4 * G + j) * 64:(4 * G + j + 1) * 64],
                            in0=sT[:, (4 * G + j) * 64:(4 * G + j + 1) * 64],
                            scalar=DLG[:, j * TC + c: j * TC + c + 1],
                            in1=BK(4)[:, j * 64:(j + 1) * 64],
                            op0=OP.mult, op1=OP.add)
                    return e.tensor_copy(
                        out=sT_bf[:, (4 * G) * 64:(4 * G + 4) * 64],
                        in_=sT[:, (4 * G) * 64:(4 * G + 4) * 64])
                tok_state_ev = p.g("vector", stev, after=[tok_st2], rbank=[4])
            tok_grp = tok_state_ev
        tok_wkv = tok_grp
        dump_bf(t, "d_o", o_bf, [tok_wkv, tok_oev])

        gs_tok = None
        for pc in range(PC):
            gs_tok = p.g("tensor", (lambda e, pc=pc: e.matmul(
                out=BK(0)[0:16, :], lhsT=seg16b[:, pc * 16:(pc + 1) * 16],
                rhs=fm(o_bf, pc), start=(pc == 0), stop=(pc == PC - 1))),
                after=[tok_wkv, tok_oev], wbank=[0] if pc == 0 else [])
        gq_tok = None
        for pc in range(PC):
            sq2 = p.g("vector", (lambda e, pc=pc: e.tensor_mul(
                out=pp_bf[:], in0=fm(o_bf, pc), in1=fm(o_bf, pc))),
                after=[tok_wkv, tok_oev] if pc == 0 else [gq_tok])
            gq_tok = p.g("tensor", (lambda e, pc=pc: e.matmul(
                out=BK(1)[0:16, :], lhsT=seg16b[:, pc * 16:(pc + 1) * 16],
                rhs=pp_bf[:], start=(pc == 0), stop=(pc == PC - 1))),
                after=[sq2], wbank=[1] if pc == 0 else [])
        def gnrows(e):
            e.tensor_scalar_mul(out=hrow3[:, :], in0=BK(0)[0:16, :], scalar1=1.0 / N)
            e.tensor_scalar_mul(out=hrow2[:, :], in0=BK(1)[0:16, :], scalar1=1.0 / N)
            e.tensor_tensor(out=hrow[:, :], in0=hrow3[:, :], in1=hrow3[:, :], op=OP.mult)
            e.tensor_tensor(out=hrow2[:, :], in0=hrow2[:, :], in1=hrow[:, :],
                            op=OP.subtract)
            return e.tensor_scalar_add(out=hrow2[:, :], in0=hrow2[:, :], scalar1=0.00064)
        tok_gr = p.g("vector", gnrows, after=[gs_tok, gq_tok], rbank=[0, 1])
        tok_gq = p.g("scalar", lambda e: e.activation(out=hrow2[:, :], in_=hrow2[:, :],
                                                      func=AF.Sqrt, scale=1.0),
                     after=[tok_gr])
        tok_gi = p.g("vector", lambda e: e.reciprocal(out=hrow2[:, :], in_=hrow2[:, :]),
                     after=[tok_gq])
        rk_tok = None
        mt_prev = tok_gi
        for pc in range(PC):
            mt = p.g("vector", (lambda e, pc=pc: e.scalar_tensor_tensor(
                out=pp_bf[:], in0=fm(r_bf, pc), scalar=VC["r_k"][:, pc:pc + 1],
                in1=fm(km_bf, pc), op0=OP.mult, op1=OP.mult)), after=[mt_prev])
            rk_tok = p.g("tensor", (lambda e, pc=pc: e.matmul(
                out=BK(2)[0:16, :], lhsT=seg16b[:, pc * 16:(pc + 1) * 16],
                rhs=pp_bf[:], start=(pc == 0), stop=(pc == PC - 1))),
                after=[mt], wbank=[2] if pc == 0 else [])
            mt_prev = rk_tok
        tok_rkr = p.g("vector", lambda e: e.tensor_copy(out=hrow[:, :], in_=BK(2)[0:16, :]),
                      after=[rk_tok], rbank=[2])
        tok_og = None
        for pc in range(PC):
            def gnbc(e, pc=pc):
                e.matmul(out=BK(3), lhsT=bc16[0:16, pc * 128:(pc + 1) * 128],
                         rhs=hrow2[0:16, :], start=True, stop=True)
                e.matmul(out=BK(4), lhsT=bc16[0:16, pc * 128:(pc + 1) * 128],
                         rhs=hrow3[0:16, :], start=True, stop=True)
                return e.matmul(out=BK(5), lhsT=bc16[0:16, pc * 128:(pc + 1) * 128],
                                rhs=hrow[0:16, :], start=True, stop=True)
            mt = p.g("tensor", gnbc, after=[tok_gi, tok_rkr], wbank=[3, 4, 5])
            def gnap(e, pc=pc):
                e.tensor_sub(out=s2w_f[:], in0=fm(o_bf, pc), in1=BK(4))
                e.tensor_tensor(out=s2w_f[:], in0=s2w_f[:], in1=BK(3), op=OP.mult)
                e.tensor_scalar(out=s2w_f[:], in0=s2w_f[:],
                                scalar1=VC["gn_w"][:, pc:pc + 1],
                                scalar2=VC["gn_b"][:, pc:pc + 1], op0=OP.mult, op1=OP.add)
                e.tensor_tensor(out=bt2_f[:], in0=BK(5), in1=fm(v_bf, pc), op=OP.mult)
                e.tensor_tensor(out=s2w_f[:], in0=s2w_f[:], in1=bt2_f[:], op=OP.add)
                return e.tensor_tensor(out=fm(kk_bf, pc), in0=s2w_f[:], in1=fm(g_bf, pc),
                                       op=OP.mult)
            tok_og = p.g("vector", (lambda e, pc=pc: gnap(e, pc)), after=[mt],
                         rbank=[3, 4, 5])

        def ev_wo(e, mc, bk):
            e.scalar_tensor_tensor(out=fm(x_fm, mc), in0=fm(x_fm, mc),
                                   scalar=VC["ln_pre_w"][:, mc:mc + 1], in1=bk,
                                   op0=OP.mult, op1=OP.add)
            return e.tensor_scalar_add(out=fm(x_fm, mc), in0=fm(x_fm, mc),
                                       scalar1=VC["ln_pre_b"][:, mc:mc + 1])
        mm_wo, tok_x1 = dense_mm(6, (lambda kc: fm(kk_bf, kc)), ev_wo, "vector",
                                 after0=[tok_og])
        dump_f(t, "d_x1", x_fm, [tok_x1])

        tk_prev = None
        for pc in range(PC):
            sq_tok = p.g("scalar", (lambda e, pc=pc: e.activation(
                out=xsq_fm[:], in_=fm(x_fm, pc), func=AF.Square)),
                after=[tok_x1 if pc == 0 else tk_prev])
            def lnf_mm(e, pc=pc):
                e.matmul(out=BK(0)[0:1, :], lhsT=ones_col[:], rhs=fm(x_fm, pc),
                         start=(pc == 0), stop=(pc == PC - 1))
                return e.matmul(out=BK(3)[0:1, :], lhsT=ones_col[:], rhs=xsq_fm[:],
                                start=(pc == 0), stop=(pc == PC - 1))
            tk_prev = p.g("tensor", lnf_mm, after=[sq_tok, tok_x1],
                          wbank=[0, 3] if pc == 0 else [])
        def lnf_rows(e):
            e.tensor_scalar_mul(out=lnmu, in0=BK(0)[0:1, :], scalar1=1.0 / D)
            e.tensor_scalar_mul(out=lniv, in0=BK(3)[0:1, :], scalar1=1.0 / D)
            e.tensor_tensor(out=s2w_f[0:1, :], in0=lnmu, in1=lnmu,
                            op=OP.mult)
            e.tensor_tensor(out=lniv, in0=lniv, in1=s2w_f[0:1, :],
                            op=OP.subtract)
            return e.tensor_scalar_add(out=lniv, in0=lniv, scalar1=1e-5)
        tok_fr = p.g("vector", lnf_rows, after=[tk_prev], rbank=[0, 3])
        tok_fq = p.g("scalar", lambda e: e.activation(out=lniv, in_=lniv,
                                                      func=AF.Sqrt, scale=1.0),
                     after=[tok_fr])
        def lnf_i(e):
            e.reciprocal(out=lniv, in_=lniv)
            e.tensor_tensor(out=s2w_f[0:1, :], in0=lnmu, in1=lniv,
                            op=OP.mult)
            return e.tensor_scalar_mul(out=s2w_f[0:1, :], in0=s2w_f[0:1, :], scalar1=-1.0)
        tok_fi = p.g("vector", lnf_i, after=[tok_fq])
        def lnf_bc(e):
            e.matmul(out=BK(1), lhsT=ones_r128[:], rhs=lniv, start=True, stop=True)
            return e.matmul(out=BK(2), lhsT=ones_r128[:], rhs=s2w_f[0:1, :],
                            start=True, stop=True)
        tok_fbc = p.g("tensor", lnf_bc, after=[tok_fi], wbank=[1, 2])
        tok_xf = None
        for pc in range(PC):
            def lnf_xf(e, pc=pc):
                e.tensor_tensor(out=bt2_f[:], in0=fm(x_fm, pc), in1=BK(1), op=OP.mult)
                e.tensor_tensor(out=bt2_f[:], in0=bt2_f[:], in1=BK(2), op=OP.add)
                return e.tensor_scalar(out=fm(xn_bf, pc), in0=bt2_f[:],
                                       scalar1=VC["ln_ffn_w"][:, pc:pc + 1],
                                       scalar2=VC["ln_ffn_b"][:, pc:pc + 1],
                                       op0=OP.mult, op1=OP.add)
            tok_xf = p.g("vector", (lambda e, pc=pc: lnf_xf(e, pc)),
                         after=[tok_fbc], rbank=[1, 2])
        dump_bf(t, "d_xf", xn_bf, [tok_xf])
        def p_kf(e):
            ins = None
            for pc in range(PC):
                e.tensor_sub(out=fm(dx_bf, pc)[:, 1:TILE], in0=fm(xn_bf, pc)[:, 0:TILE - 1],
                             in1=fm(xn_bf, pc)[:, 1:TILE])
                e.tensor_sub(out=fm(dx_bf, pc)[:, 0:1], in0=xf_halo[:, pc:pc + 1],
                             in1=fm(xn_bf, pc)[:, 0:1])
                e.tensor_copy(out=xf_halo[:, pc:pc + 1], in_=fm(xn_bf, pc)[:, TILE - 1:TILE])
                ins = e.scalar_tensor_tensor(out=fm(dx_bf, pc), in0=fm(dx_bf, pc),
                                             scalar=VC["ffn_x_k"][:, pc:pc + 1],
                                             in1=fm(xn_bf, pc), op0=OP.mult, op1=OP.add)
            return ins
        tok_kf = p.g("vector", p_kf, after=[tok_xf])

        tok_h2 = None
        for ms in range(4):
            hu0 = 8 + 2 * ms
            def ev_f1(e, mc, bk, ms=ms):
                return e.activation(out=h2c(ms * 8 + mc), in_=bk, func=AF.Relu)
            mmt, evt = dense_mm(hu0, (lambda kc: fm(dx_bf, kc)), ev_f1, "scalar",
                                after0=[tok_kf] if ms == 0 else [tok_kf, tok_h2])
            def sq_f1(e, ms=ms):
                ins = None
                for mc in range(8):
                    q = ms * 8 + mc
                    ins = e.tensor_mul(out=h2c(q), in0=h2c(q), in1=h2c(q))
                return ins
            tok_h2 = p.g("vector", sq_f1, after=[evt])
        mm_f2 = None
        for ksl in range(4):
            hu0 = 16 + 2 * ksl
            ensure_hu(t, hu0 + 1)
            def f2mm(e, ksl=ksl, hu0=hu0):
                ins = None
                for kc in range(PC):
                    for mc in range(8):
                        ins = e.matmul(out=BK(mc), lhsT=hu_sb(hu0 + mc // 4, kc, mc % 4),
                                       rhs=h2c(ksl * 8 + kc),
                                       start=(ksl == 0 and kc == 0),
                                       stop=(ksl == 3 and kc == PC - 1))
                return ins
            aft = [tok_h2, wload[(t, hu0)], wload[(t, hu0 + 1)]]
            mm_f2 = p.g("tensor", f2mm, after=aft,
                        wbank=list(range(8)) if ksl == 0 else [])
            slot_free[hu0 % 2] = mm_f2
            slot_free[(hu0 + 1) % 2] = mm_f2
        def ev_f2(e):
            ins = None
            for mc in range(8):
                ins = e.tensor_tensor(out=fm(x_fm, mc), in0=fm(x_fm, mc), in1=BK(mc),
                                      op=OP.add)
            return ins
        tok_x2 = p.g("vector", ev_f2, after=[mm_f2], rbank=list(range(8)))
        dump_f(t, "d_x2", x_fm, [tok_x2])

        tok_xo = tok_x2
        for tc in range(TC):
            buf = tmb[tc % 2]
            tok_ld = p.dma_g("sync",
                             (lambda e, t=t, tc=tc, buf=buf: [e.dma_start(
                                 out=buf[:],
                                 in_=x_in[t * TILE + tc * 128: t * TILE + (tc + 1) * 128, :])]),
                             1, after=[tmb_free[0], tmb_free[1]])
            for gi in range(2):
                bk = (tc % 2) * 2 + gi
                def tpg2(e, tc=tc, gi=gi, bk=bk, buf=buf):
                    ins = None
                    for pc4 in range(4):
                        pc = gi * 4 + pc4
                        ins = e.matmul(out=BK(bk)[:, pc4 * 128:(pc4 + 1) * 128],
                                       lhsT=buf[:, pc * 128:(pc + 1) * 128],
                                       rhs=id_f32[:], start=(pc4 == 0), stop=(pc4 == 3))
                    return ins
                tk = p.g("tensor", tpg2, after=[tok_ld, tok_x2], wbank=[bk])
                def tpe2(e, tc=tc, gi=gi, bk=bk):
                    ins = None
                    for pc4 in range(4):
                        pc = gi * 4 + pc4
                        ins = e.tensor_tensor(out=fmc(x_fm, pc, tc), in0=fmc(x_fm, pc, tc),
                                              in1=BK(bk)[:, pc4 * 128:(pc4 + 1) * 128],
                                              op=OP.add)
                    return ins
                tok_xo = p.g("vector", tpe2, after=[tk], rbank=[bk])
                tmb_free[tc % 2] = tok_xo

        tk_prev = None
        for pc in range(PC):
            sq_tok = p.g("scalar", (lambda e, pc=pc: e.activation(
                out=xsq_fm[:], in_=fm(x_fm, pc), func=AF.Square)),
                after=[tok_xo if pc == 0 else tk_prev])
            def ln3_mm(e, pc=pc):
                e.matmul(out=BK(4)[0:1, :], lhsT=ones_col[:], rhs=fm(x_fm, pc),
                         start=(pc == 0), stop=(pc == PC - 1))
                return e.matmul(out=BK(7)[0:1, :], lhsT=ones_col[:], rhs=xsq_fm[:],
                                start=(pc == 0), stop=(pc == PC - 1))
            tk_prev = p.g("tensor", ln3_mm, after=[sq_tok, tok_xo],
                          wbank=[4, 7] if pc == 0 else [])
        def ln3_rows(e):
            e.tensor_scalar_mul(out=lnmu, in0=BK(4)[0:1, :], scalar1=1.0 / D)
            e.tensor_scalar_mul(out=lniv, in0=BK(7)[0:1, :], scalar1=1.0 / D)
            e.tensor_tensor(out=s2w_f[0:1, :], in0=lnmu, in1=lnmu,
                            op=OP.mult)
            e.tensor_tensor(out=lniv, in0=lniv, in1=s2w_f[0:1, :],
                            op=OP.subtract)
            return e.tensor_scalar_add(out=lniv, in0=lniv, scalar1=1e-5)
        tok_3r = p.g("vector", ln3_rows, after=[tk_prev], rbank=[4, 7])
        tok_3q = p.g("scalar", lambda e: e.activation(out=lniv, in_=lniv,
                                                      func=AF.Sqrt, scale=1.0),
                     after=[tok_3r])
        def ln3_i(e):
            e.reciprocal(out=lniv, in_=lniv)
            e.tensor_tensor(out=s2w_f[0:1, :], in0=lnmu, in1=lniv,
                            op=OP.mult)
            return e.tensor_scalar_mul(out=s2w_f[0:1, :], in0=s2w_f[0:1, :], scalar1=-1.0)
        tok_3i = p.g("vector", ln3_i, after=[tok_3q])
        def ln3_bc(e):
            e.matmul(out=BK(5), lhsT=ones_r128[:], rhs=lniv, start=True, stop=True)
            return e.matmul(out=BK(6), lhsT=ones_r128[:], rhs=s2w_f[0:1, :],
                            start=True, stop=True)
        tok_3bc = p.g("tensor", ln3_bc, after=[tok_3i], wbank=[5, 6])
        tok_x3 = None
        for pc in range(PC):
            def ln3_ap(e, pc=pc):
                e.tensor_tensor(out=bt2_f[:], in0=fm(x_fm, pc), in1=BK(5), op=OP.mult)
                e.tensor_tensor(out=fm(x_fm, pc), in0=bt2_f[:], in1=BK(6), op=OP.add)
                return e.tensor_copy(out=fm(xn_bf, pc), in_=fm(x_fm, pc))
            tok_x3 = p.g("vector", (lambda e, pc=pc: ln3_ap(e, pc)),
                         after=[tok_3bc], rbank=[5, 6])
        dump_f(t, "d_x3", x_fm, [tok_x3])

        def gt1(e):
            ins = None
            for kc in range(PC):
                for q in range(2):
                    ins = e.matmul(out=BK(q), lhsT=wg1_sb[:, kc * 256 + q * 128:
                                                         kc * 256 + (q + 1) * 128],
                                   rhs=fm(xn_bf, kc), start=(kc == 0), stop=(kc == PC - 1))
            return ins
        tok_gt1 = p.g("tensor", gt1, after=[tok_x3], wbank=[0, 1])
        def gt1e(e):
            e.activation(out=hg_bf[:, 0:TILE], in_=BK(0), func=AF.Gelu,
                         bias=bg1c[:, 0:1])
            return e.activation(out=hg_bf[:, TILE:2 * TILE], in_=BK(1), func=AF.Gelu,
                                bias=bg1c[:, 1:2])
        tok_gt1e = p.g("scalar", gt1e, after=[tok_gt1], rbank=[0, 1])
        def gt2(e):
            e.matmul(out=BK(2)[0:40, :], lhsT=wg2_sb[:, 0:40], rhs=hg_bf[:, 0:TILE],
                     start=True, stop=False)
            return e.matmul(out=BK(2)[0:40, :], lhsT=wg2_sb[:, 40:80],
                            rhs=hg_bf[:, TILE:2 * TILE], start=False, stop=True)
        tok_gt2 = p.g("tensor", gt2, after=[tok_gt1e], wbank=[2])
        tok_lg = p.g("vector", lambda e: e.tensor_scalar_add(
            out=lgt_f[:], in0=BK(2)[0:20, :], scalar1=bg2c[0:20, :]),
                     after=[tok_gt2], rbank=[2])
        def lgT(e):
            ins = None
            for tc in range(TC):
                ins = e.matmul(out=BK(3)[:, tc * 20:(tc + 1) * 20],
                               lhsT=lgt_f[:, tc * 128:(tc + 1) * 128],
                               rhs=id_f32[0:20, 0:20], start=(tc == 0), stop=(tc == 3))
            return ins
        tok_lgT = p.g("tensor", lgT, after=[tok_lg], wbank=[3])
        tok_lgTe = p.g("vector", lambda e: e.tensor_copy(out=lgtT[:], in_=BK(3)[:, 0:TC * NE]),
                       after=[tok_lgT], rbank=[3])
        def lgv(tt):
            return tt[:].rearrange("p (c k) -> p c k", c=TC)
        def growc(i):
            return AP(grow.ap().tensor, i * TC, [[16, 128], [1, TC], [0, NE]])
        def top2(e):
            e.tensor_reduce(out=grow[:, 0:TC], in_=lgv(lgtT), axis=mybir.AxisListType.X,
                            op=OP.max)
            for cq in range(TC):
                e.tensor_scalar(out=gscr[:, cq * NE:(cq + 1) * NE],
                                in0=lgtT[:, cq * NE:(cq + 1) * NE],
                                scalar1=grow[:, cq:cq + 1], scalar2=None, op0=OP.subtract)
            return e.tensor_copy(out=gscr2[:], in_=gscr[:])
        tok_t2a = p.g("vector", top2, after=[tok_lgTe])
        tok_t2b = p.g("scalar", lambda e: e.activation(out=gscr2[:], in_=gscr2[:],
                                                       func=AF.Exp),
                      after=[tok_t2a])
        def top2c(e):
            e.tensor_reduce(out=grow[:, TC:2 * TC], in_=lgv(gscr2),
                            axis=mybir.AxisListType.X, op=OP.add)
            e.reciprocal(out=grow[:, 2 * TC:3 * TC], in_=grow[:, TC:2 * TC])
            for cq in range(TC):
                e.tensor_scalar(out=gscr[:, cq * NE:(cq + 1) * NE],
                                in0=lgtT[:, cq * NE:(cq + 1) * NE],
                                scalar1=grow[:, cq:cq + 1], scalar2=None, op0=OP.is_ge)
            e.scalar_tensor_tensor(out=lgv(gscr), in0=lgv(gscr), scalar=-1e30,
                                   in1=lgv(lgtT), op0=OP.mult, op1=OP.add)
            e.tensor_reduce(out=grow[:, 3 * TC:4 * TC], in_=lgv(gscr),
                            axis=mybir.AxisListType.X, op=OP.max)
            for cq in range(TC):
                e.tensor_scalar(out=gscr[:, cq * NE:(cq + 1) * NE],
                                in0=lgtT[:, cq * NE:(cq + 1) * NE],
                                scalar1=grow[:, 3 * TC + cq:3 * TC + cq + 1], scalar2=None, op0=OP.is_ge)
            e.tensor_tensor(out=gscr[:], in0=gscr[:], in1=gscr2[:], op=OP.mult)
            ins = None
            for cq in range(TC):
                ins = e.tensor_scalar(out=gscr2[:, cq * NE:(cq + 1) * NE],
                                      in0=gscr[:, cq * NE:(cq + 1) * NE],
                                      scalar1=grow[:, 2 * TC + cq:2 * TC + cq + 1],
                                      scalar2=None, op0=OP.mult)
            return ins
        tok_t2c = p.g("vector", top2c, after=[tok_t2b])
        def gTb(e):
            ins = None
            for tc in range(TC):
                ins = e.matmul(out=BK(4)[0:20, tc * 128:(tc + 1) * 128],
                               lhsT=gscr2[:, tc * 20:(tc + 1) * 20],
                               rhs=id_f32[:], start=(tc == 0), stop=(tc == 3))
            return ins
        tok_gTb = p.g("tensor", gTb, after=[tok_t2c], wbank=[4])
        tok_gbe = p.g("scalar", lambda e: e.copy(out=gates_bf[:], in_=BK(4)[0:20, :]),
                      after=[tok_gTb], rbank=[4])
        if debug and t == dbg_tile:
            dbg_after.append(p.dma_g("gpsimd",
                                     (lambda e: [e.dma_start(out=dbg["d_gates"][:],
                                                             in_=gates_bf[:])]),
                                     1, after=[tok_gbe]))

        tok_selmm = None
        tok_selev = None
        for ee in range(NE):
            tok_msel = p.g("vector", (lambda e, ee=ee: e.memset(sel2[ee % 2], 1.0)),
                           after=[tok_gbe] if ee < 2 else [tok_selmm])
            tok_sel = p.g("gpsimd", (lambda e, ee=ee: e.affine_select(
                out=sel2[ee % 2], in_=sel2[ee % 2], pattern=[[0, 128]],
                compare_op=OP.is_equal, fill=0.0, base=-ee, channel_multiplier=1)),
                after=[tok_msel])
            def selmm(e, ee=ee):
                return e.matmul(out=BK(ee % 2), lhsT=sel2[ee % 2],
                                rhs=gates_bf, start=True, stop=True)
            tok_selmm = p.g("tensor", selmm, after=[tok_sel, tok_gbe], wbank=[ee % 2])
            eng_ev = "vector" if ee % 2 == 0 else "scalar"
            def selev(e, ee=ee):
                if ee % 2 == 0:
                    return e.tensor_copy(out=gbc_t(ee), in_=BK(ee % 2))
                return e.copy(out=gbc_t(ee), in_=BK(ee % 2))
            tok_selev = p.g(eng_ev, selev, after=[tok_selmm], rbank=[ee % 2])

        def bemm(e):
            ins = None
            for mc in range(8):
                ins = e.matmul(out=BK(mc), lhsT=be_sb[:, mc * 128:(mc + 1) * 128],
                               rhs=gates_bf[:], start=True, stop=False)
            return ins
        tok_y0 = p.g("tensor", bemm, after=[tok_gbe], wbank=list(range(8)))

        tok_gx = None
        mm_moe = tok_y0
        gxb = [kk_bf, o_bf]
        prev_mm = [tok_y0, tok_y0]
        for ee in range(NE):
            gbuf = gxb[ee % 2]
            def gxf(e, ee=ee, gbuf=gbuf):
                ins = None
                for pc in range(PC):
                    ins = e.tensor_tensor(out=fm(gbuf, pc), in0=fm(xn_bf, pc),
                                          in1=gbc_t(ee), op=OP.mult)
                return ins
            tok_gx = p.g("vector", gxf, after=[tok_selev, prev_mm[ee % 2]])
            hu0 = 24 + 2 * ee
            ensure_hu(t, hu0 + 1)
            def moemm(e, ee=ee, hu0=hu0, gbuf=gbuf):
                ins = None
                for kc in range(PC):
                    for mc in range(8):
                        ins = e.matmul(out=BK(mc), lhsT=hu_sb(hu0 + mc // 4, kc, mc % 4),
                                       rhs=fm(gbuf, kc), start=False,
                                       stop=(ee == NE - 1 and kc == PC - 1))
                return ins
            mm_moe = p.g("tensor", moemm,
                         after=[tok_gx, wload[(t, hu0)], wload[(t, hu0 + 1)]])
            prev_mm[ee % 2] = mm_moe
            slot_free[hu0 % 2] = mm_moe
            slot_free[(hu0 + 1) % 2] = mm_moe
        def ev_moe(e):
            ins = None
            for mc in range(8):
                ins = e.tensor_tensor(out=fm(x_fm, mc), in0=fm(x_fm, mc), in1=BK(mc),
                                      op=OP.add)
            return ins
        tok_x4 = p.g("vector", ev_moe, after=[mm_moe], rbank=list(range(8)))

        store_toks = []
        for tc in range(TC):
            buf = tmb[tc % 2]
            for gi in range(2):
                bk = (tc % 2) * 2 + gi
                def tpo(e, tc=tc, gi=gi, bk=bk):
                    ins = None
                    for pc4 in range(4):
                        pc = gi * 4 + pc4
                        ins = e.matmul(out=BK(bk)[:, pc4 * 128:(pc4 + 1) * 128],
                                       lhsT=fmc(x_fm, pc, tc), rhs=id_f32[:],
                                       start=(pc4 == 0), stop=(pc4 == 3))
                    return ins
                tk = p.g("tensor", tpo, after=[tok_x4, tmb_free[0], tmb_free[1]], wbank=[bk])
                def tpoe(e, tc=tc, gi=gi, bk=bk, buf=buf):
                    ins = None
                    for pc4 in range(4):
                        pc = gi * 4 + pc4
                        ins = e.tensor_copy(out=buf[:, pc * 128:(pc + 1) * 128],
                                            in_=BK(bk)[:, pc4 * 128:(pc4 + 1) * 128])
                    return ins
                tok_tpe = p.g("vector", tpoe, after=[tk], rbank=[bk])
            def p8s(e, tc=tc, buf=buf):
                for hh in range(2):
                    e.activation(out=xsq_fm[:], in_=buf[:, hh * 512:(hh + 1) * 512],
                                 func=AF.Copy, accum_out=stat_h[:, 4 * tc + hh:4 * tc + hh + 1])
                    ins = e.activation(out=xsq_fm[:], in_=buf[:, hh * 512:(hh + 1) * 512],
                                       func=AF.Square,
                                       accum_out=stat_h[:, 4 * tc + 2 + hh:4 * tc + 3 + hh])
                return ins
            tok_s8 = p.g("scalar", p8s, after=[tok_tpe])
            def p8r(e, tc=tc):
                e.tensor_add(out=stat_s[:, tc:tc + 1], in0=stat_h[:, 4 * tc:4 * tc + 1],
                             in1=stat_h[:, 4 * tc + 1:4 * tc + 2])
                e.tensor_add(out=stat_q[:, tc:tc + 1], in0=stat_h[:, 4 * tc + 2:4 * tc + 3],
                             in1=stat_h[:, 4 * tc + 3:4 * tc + 4])
                e.tensor_scalar_mul(out=stat_m[:, tc:tc + 1], in0=stat_s[:, tc:tc + 1],
                                    scalar1=1.0 / D)
                e.tensor_scalar_mul(out=stat_q[:, tc:tc + 1], in0=stat_q[:, tc:tc + 1],
                                    scalar1=1.0 / D)
                e.tensor_tensor(out=stat_i[:, tc:tc + 1], in0=stat_m[:, tc:tc + 1],
                                in1=stat_m[:, tc:tc + 1], op=OP.mult)
                e.tensor_tensor(out=stat_q[:, tc:tc + 1], in0=stat_q[:, tc:tc + 1],
                                in1=stat_i[:, tc:tc + 1], op=OP.subtract)
                return e.tensor_scalar_add(out=stat_q[:, tc:tc + 1],
                                           in0=stat_q[:, tc:tc + 1], scalar1=1e-5)
            tok_r8 = p.g("vector", p8r, after=[tok_s8])
            tok_q8 = p.g("scalar", (lambda e, tc=tc: e.activation(
                out=stat_q[:, tc:tc + 1], in_=stat_q[:, tc:tc + 1], func=AF.Sqrt,
                scale=1.0)), after=[tok_r8])
            def p8a(e, tc=tc, buf=buf):
                e.reciprocal(out=stat_i[:, tc:tc + 1], in_=stat_q[:, tc:tc + 1])
                return e.tensor_scalar(out=buf[:], in0=buf[:],
                                       scalar1=stat_m[:, tc:tc + 1],
                                       scalar2=stat_i[:, tc:tc + 1],
                                       op0=OP.subtract, op1=OP.mult)
            tok_a8 = p.g("vector", p8a, after=[tok_q8])
            tok_st8 = p.dma_g("sync",
                              (lambda e, t=t, tc=tc, buf=buf: [e.dma_start(
                                  out=y_out[t * TILE + tc * 128: t * TILE + (tc + 1) * 128, :],
                                  in_=buf[:])]), 1, after=[tok_a8])
            store_toks.append(tok_st8)
            tmb_free[tc % 2] = tok_st8
        tok_store_last = store_toks[-1]

    p.g("vector", lambda e: e.tensor_copy(out=pp_bf[0:1, 0:1], in_=pp_bf[0:1, 0:1]),
        after=[tok_store_last] + list(dbg_after))

    cut = int(os.environ.get("DK_CUT", str(p.seq + 10)))
    if os.environ.get("DK_PRINT_SEQ"):
        print("TOTAL_SEQ", p.seq)
    with nc.Block() as block:
        class VecGuard:
            def __init__(self, eng, sp):
                self.e = eng
                self.sp = sp
                self.prev_out = set()
                self.prev_in = set()

            @staticmethod
            def _names(v):
                if isinstance(v, bass.AP):
                    return {v.name}
                return set()

            def wait_ge(self, *a, **k):
                return self.e.wait_ge(*a, **k)

            def __getattr__(self, nm):
                real = getattr(self.e, nm)
                if not callable(real):
                    return real
                def wrap(*args, **kw):
                    outs, ins = set(), set()
                    if nm == "memset" and args:
                        outs |= self._names(args[0])
                        rest = args[1:]
                    else:
                        rest = args
                    for k, v in kw.items():
                        if k in ("out", "accum_out"):
                            outs |= self._names(v)
                        else:
                            ins |= self._names(v)
                    for v in rest:
                        ins |= self._names(v)
                    if (self.prev_out & (outs | ins)) or (self.prev_in & outs):
                        self.e.tensor_copy(out=self.sp, in_=self.sp)
                    self.prev_out, self.prev_in = outs, ins
                    return real(*args, **kw)
                return wrap

        def emit(ename):
            def run(eng):
                if ename == "vector":
                    eng = VecGuard(eng, vspace[:, 0:1])
                for (waits, fn, n_dma, seq) in p.groups[ename]:
                    if seq >= cut:
                        break
                    for s, v in waits.items():
                        if s.startswith("dma_"):
                            eng.wait_ge(dsem[s[4:]], v)
                        else:
                            eng.wait_ge(sem[s], v)
                    res = fn(eng)
                    if n_dma is not None:
                        assert len(res) == n_dma, \
                            f"{ename} dma group: declared {n_dma}, issued {len(res)}"
                        for ins in res:
                            ins.then_inc(dsem[ename], 16)
                    else:
                        res.then_inc(sem[ename], 1)
            return run
        block.sync(emit("sync"))
        block.scalar(emit("scalar"))
        block.vector(emit("vector"))
        block.tensor(emit("tensor"))
        block.gpsimd(emit("gpsimd"))

    es.close()
    return nc



def kernel(**inputs):
    x = np.ascontiguousarray(np.asarray(inputs["L0_attn_x"], dtype=np.float32))
    nc = build_nc(debug=False)
    base = {}
    ren = {"W_ffn1": "L0_attn_W_ffn1", "W_ffn2": "L0_attn_W_ffn2",
           "wg1": "L0_ffn_wg1", "bg1": "L0_ffn_bg1", "wg2": "L0_ffn_wg2",
           "bg2": "L0_ffn_bg2", "We": "L0_ffn_We", "be": "L0_ffn_be"}
    names = (WN1D + ["r_k", "w1", "w2", "a1", "a2", "g1", "g2",
                     "W_r", "W_k", "W_v", "W_o", "W_ffn1", "W_ffn2",
                     "wg1", "bg1", "wg2", "bg2", "We", "be"])
    for nm in names:
        key = ren.get(nm, "L0_attn_" + nm)
        if key not in inputs:
            key = "L0_ffn_" + nm
        base[nm] = np.ascontiguousarray(np.asarray(inputs[key], dtype=np.float32))
    in_maps = []
    for b in range(8):
        m = dict(base)
        m["x"] = x[b]
        in_maps.append(m)
    res = run_bass_kernel_spmd(nc, in_maps, core_ids=list(range(8)))
    return np.stack([res.results[b]["out"] for b in range(8)], axis=0)


# version 06
# speedup vs baseline: 1.0049x; 1.0049x over previous
import contextlib
import numpy as np
import concourse.bass as bass
import concourse.mybir as mybir
from concourse.bass_utils import run_bass_kernel_spmd

F32 = mybir.dt.float32
BF16 = mybir.dt.bfloat16
AF = mybir.ActivationFunctionType
OP = mybir.AluOpType
AP = bass.AP

import os
TOK, D, TILE = 4096, 1024, 512
NT, PC, TC = TOK // TILE, D // 128, TILE // 128
NT = int(os.environ.get("DK_NT", NT))
H, N, DFF, NE = 16, 64, 4096, 20
SC = float(np.exp(-0.5))
ENGS = ("sync", "scalar", "vector", "tensor", "gpsimd")

WN1D = ["ln_pre_w", "ln_pre_b", "ln_attn_w", "ln_attn_b", "ln_ffn_w", "ln_ffn_b",
        "x_r", "x_w", "x_k", "x_v", "x_a", "x_g", "w0", "a0", "k_k", "k_a",
        "gn_w", "gn_b", "ffn_x_k"]


class Plan:

    def __init__(self):
        self.groups = {e: [] for e in ENGS}
        self.prog = {e: 0 for e in ENGS}
        self.dmac = {e: 0 for e in ENGS}
        self.seq = 0
        self.bank_w = [None] * 8
        self.bank_r = [None] * 8

    def _mk(self, eng, fn, after, is_dma, n_dma=0):
        waits = {}
        for tokn in after:
            if tokn is None:
                continue
            s, v = tokn
            if s == eng and eng != "gpsimd":
                continue
            waits[s] = max(waits.get(s, 0), v)
        import traceback
        if os.environ.get("DK_TRACE_SEQ") and self.seq == int(os.environ["DK_TRACE_SEQ"]):
            print("SEQ", self.seq, "eng", eng, "dma", is_dma)
            traceback.print_stack()
        self.groups[eng].append((waits, fn, n_dma if is_dma else None, self.seq))
        self.seq += 1
        if is_dma:
            self.dmac[eng] += 16 * n_dma
            return ("dma_" + eng, self.dmac[eng])
        self.prog[eng] += 1
        return (eng, self.prog[eng])

    def g(self, eng, fn, after=(), wbank=(), rbank=()):
        after = list(after)
        for b in wbank:
            after.append(self.bank_r[b])
            after.append(self.bank_w[b])
        for b in rbank:
            after.append(self.bank_w[b])
        tok = self._mk(eng, fn, after, False)
        for b in wbank:
            self.bank_w[b] = tok
        for b in rbank:
            self.bank_r[b] = tok
        return tok

    def dma_g(self, eng, fn, n_dma, after=()):
        return self._mk(eng, fn, list(after), True, n_dma)


def build_nc(debug=False, dbg_tile=NT - 1):
    nc = bass.Bass("TRN2", target_bir_lowering=False, debug=False,
                   detect_race_conditions=False)

    x_in = nc.dram_tensor("x", [TOK, D], F32, kind="ExternalInput").ap()
    y_out = nc.dram_tensor("out", [TOK, D], F32, kind="ExternalOutput").ap()
    dr = {}
    for nm in WN1D:
        dr[nm] = nc.dram_tensor(nm, [D], F32, kind="ExternalInput").ap()
    for nm, sh in [("r_k", [H, N]), ("w1", [D, 64]), ("w2", [64, D]),
                   ("a1", [D, 64]), ("a2", [64, D]), ("g1", [D, 128]),
                   ("g2", [128, D]), ("W_r", [D, D]), ("W_k", [D, D]),
                   ("W_v", [D, D]), ("W_o", [D, D]), ("W_ffn1", [D, DFF]),
                   ("W_ffn2", [DFF, D]), ("wg1", [D, 256]), ("bg1", [256]),
                   ("wg2", [256, 40]), ("bg2", [40]), ("We", [NE, D, D]),
                   ("be", [NE, D])]:
        dr[nm] = nc.dram_tensor(nm, sh, F32, kind="ExternalInput").ap()

    BIGW = ([("W_r", 0), ("W_k", 0), ("W_v", 0), ("W_o", 0)]
            + [("ffn1", i) for i in range(4)] + [("ffn2", i) for i in range(4)]
            + [("We", e) for e in range(NE)])
    NU = len(BIGW)
    wcache = nc.dram_tensor("wcache", [128, 2 * NU * 4096], BF16, kind="Internal").ap()

    dbg = {}
    if debug:
        for nm in ["d_u1", "d_xn", "d_r", "d_e", "d_kk", "d_kmod", "d_v", "d_b",
                   "d_g", "d_o", "d_x1", "d_xf", "d_x2", "d_x3"]:
            dbg[nm] = nc.dram_tensor(nm, [128, PC * TILE], F32, kind="ExternalOutput").ap()
        dbg["d_gates"] = nc.dram_tensor("d_gates", [20, TILE], F32, kind="ExternalOutput").ap()

    es = contextlib.ExitStack()
    E = es.enter_context

    id_bf = E(nc.sbuf_tensor("id_bf", [128, 128], BF16))
    id_f32 = E(nc.sbuf_tensor("id_f32", [128, 128], F32))
    msk_su = E(nc.sbuf_tensor("msk_su", [128, 128], BF16))
    msk_iu = E(nc.sbuf_tensor("msk_iu", [128, 128], BF16))
    msk_sl = E(nc.sbuf_tensor("msk_sl", [128, 128], BF16))
    rscan = E(nc.sbuf_tensor("rscan", [128, TILE], BF16))
    seg16b = E(nc.sbuf_tensor("seg16b", [128, PC * 16], BF16))
    bc16 = E(nc.sbuf_tensor("bc16", [16, PC * 128], F32))
    ones_col = E(nc.sbuf_tensor("ones_col", [128, 1], F32))
    cst_eps = E(nc.sbuf_tensor("cst_eps", [128, 2], F32))
    ones_r128 = E(nc.sbuf_tensor("ones_r128", [1, 128], F32))

    VC = {nm: E(nc.sbuf_tensor("c_" + nm, [128, PC], F32)) for nm in WN1D}
    VC["r_k"] = E(nc.sbuf_tensor("c_r_k", [128, PC], F32))
    VC["omka"] = E(nc.sbuf_tensor("c_omka", [128, PC], F32))
    w1w2c = E(nc.sbuf_tensor("w1w2c", [128, PC], F32))
    w2b1c = E(nc.sbuf_tensor("w2b1c", [128, PC], F32))
    bg1c = E(nc.sbuf_tensor("bg1c", [128, 2], F32))
    bg2c = E(nc.sbuf_tensor("bg2c", [40, 1], F32))
    w1sq = E(nc.sbuf_tensor("w1sq", [128, PC], F32))
    w1b1 = E(nc.sbuf_tensor("w1b1", [128, PC], F32))
    sb1 = E(nc.sbuf_tensor("sb1", [1, 2], F32))
    scr_red = E(nc.sbuf_tensor("scr_red", [128, 2], F32))

    w1_sb = E(nc.sbuf_tensor("w1_sb", [128, PC * 64], BF16))
    a1_sb = E(nc.sbuf_tensor("a1_sb", [128, PC * 64], BF16))
    w2_sb = E(nc.sbuf_tensor("w2_sb", [64, D], BF16))
    a2_sb = E(nc.sbuf_tensor("a2_sb", [64, D], BF16))
    g1_sb = E(nc.sbuf_tensor("g1_sb", [128, PC * 128], BF16))
    g2_sb = E(nc.sbuf_tensor("g2_sb", [128, D], BF16))
    wg1_sb = E(nc.sbuf_tensor("wg1_sb", [128, PC * 256], BF16))
    wg2_sb = E(nc.sbuf_tensor("wg2_sb", [128, 2 * 40], BF16))
    be_sb = E(nc.sbuf_tensor("be_sb", [NE, D], BF16))

    sT = E(nc.sbuf_tensor("sT", [128, PC * N], F32))
    sT_bf = E(nc.sbuf_tensor("sT_bf", [128, PC * N], BF16))
    xn_halo = E(nc.sbuf_tensor("xn_halo", [128, PC], BF16))
    xf_halo = E(nc.sbuf_tensor("xf_halo", [128, PC], BF16))

    NSLOT = 2
    wslot = [E(nc.sbuf_tensor(f"wslot{i}", [128, 4096], BF16)) for i in range(NSLOT)]

    tmb0 = E(nc.sbuf_tensor("tmb0", [128, D], F32))
    tmb = [tmb0, tmb0]

    x_fm = E(nc.sbuf_tensor("x_fm", [128, PC * TILE], F32))
    xsq_fm = E(nc.sbuf_tensor("xsq_fm", [128, TILE], F32))
    s2w_f = E(nc.sbuf_tensor("s2w_f", [128, TILE], F32))
    bt2_f = E(nc.sbuf_tensor("bt2_f", [128, TILE], F32))
    pass

    lniv = bt2_f[0:1, 0:TILE]
    xn_bf = E(nc.sbuf_tensor("xn_bf", [128, PC * TILE], BF16))
    dx_bf = E(nc.sbuf_tensor("dx_bf", [128, PC * TILE], BF16))
    e_bf = E(nc.sbuf_tensor("e_bf", [128, PC * TILE], BF16))
    b_bf = E(nc.sbuf_tensor("b_bf", [128, PC * TILE], BF16))
    r_bf = E(nc.sbuf_tensor("r_bf", [128, PC * TILE], BF16))
    kk_bf = E(nc.sbuf_tensor("kk_bf", [128, PC * TILE], BF16))
    km_bf = E(nc.sbuf_tensor("km_bf", [128, PC * TILE], BF16))
    v_bf = E(nc.sbuf_tensor("v_bf", [128, PC * TILE], BF16))
    g_bf = E(nc.sbuf_tensor("g_bf", [128, PC * TILE], BF16))
    o_bf = E(nc.sbuf_tensor("o_bf", [128, PC * TILE], BF16))
    mixp = [E(nc.sbuf_tensor(f"mixp{i}", [128, TILE], BF16)) for i in range(2)]
    t1_bf = E(nc.sbuf_tensor("t1_bf", [64, TILE], BF16))
    sg_bf = E(nc.sbuf_tensor("sg_bf", [128, TILE], BF16))
    pp_bf = E(nc.sbuf_tensor("pp_bf", [128, TILE], BF16))

    ARENA_COLS = 18944
    arena = E(nc.sbuf_tensor("arena", [128, ARENA_COLS], BF16))
    _ac = [0]
    def carve(n):
        a0 = _ac[0]; _ac[0] += n
        assert _ac[0] <= ARENA_COLS
        return arena[:, a0:a0 + n]
    TIL = {nm: carve(4 * TILE) for nm in ("at", "bt", "kt", "rt")}
    E2G = carve(4 * TILE)
    X_sb = carve(8 * 128)
    XT_sb = carve(8 * 128)
    Hm_sb = carve(8 * 128)
    M1_sb = carve(8 * 128)
    M2_sb = carve(8 * 128)
    u_sb = carve(8 * 64)
    vtm_sb = carve(8 * 64)
    rhs_sb = carve(4 * 128)
    btm_sb = carve(8 * 64)
    ktm_sb = carve(8 * 64)
    bbkb = carve(2 * 4 * 128)
    def h2c(q):
        return arena[:, q * 512:(q + 1) * 512]

    cs_f = E(nc.sbuf_tensor("cs_f", [128, TILE], F32))
    cse_f = E(nc.sbuf_tensor("cse_f", [128, TILE], F32))
    csL_f = E(nc.sbuf_tensor("csL_f", [128, TC], F32))
    DLG = E(nc.sbuf_tensor("DLG", [128, 4 * TC], F32))
    dec = [E(nc.sbuf_tensor(f"dec{i}", [128, TILE], BF16)) for i in range(2)]

    hrow = E(nc.sbuf_tensor("hrow", [16, TILE], F32))
    hrow2 = E(nc.sbuf_tensor("hrow2", [16, TILE], F32))
    hrow3 = E(nc.sbuf_tensor("hrow3", [16, TILE], F32))
    rows20 = E(nc.sbuf_tensor("rows20", [20, TILE + TILE], F32))
    lgt_f = rows20[0:20, 0:TILE]
    lnmu = rows20[0:1, TILE:2 * TILE]

    hg_bf = arena[:, 0:2 * TILE]
    lgtT = arena[:, 5 * TILE:5 * TILE + 160].bitcast(F32)
    gscr = arena[:, 5 * TILE + 160:5 * TILE + 320].bitcast(F32)
    gscr2 = arena[:, 5 * TILE + 320:5 * TILE + 480].bitcast(F32)
    grow = E(nc.sbuf_tensor("grow", [128, 16], F32))
    vspace = E(nc.sbuf_tensor("vspace", [128, 1], F32))
    gates_bf = arena[0:20, 2 * TILE:3 * TILE]
    sel2 = [arena[0:20, 15 * TILE + i * 128:15 * TILE + (i + 1) * 128] for i in range(2)]
    GBC0 = 16 * TILE
    def gbc_t(ee):
        return arena[:, GBC0 + ee * TILE:GBC0 + (ee + 1) * TILE]

    stat_s = E(nc.sbuf_tensor("stat_s", [128, TC], F32))
    stat_q = E(nc.sbuf_tensor("stat_q", [128, TC], F32))
    stat_h = E(nc.sbuf_tensor("stat_h", [128, 4 * TC], F32))
    stat_m = E(nc.sbuf_tensor("stat_m", [128, TC], F32))
    stat_i = E(nc.sbuf_tensor("stat_i", [128, TC], F32))

    PSA = E(nc.psum_tensor("PSA", [128, 1024], F32))
    PSB = E(nc.psum_tensor("PSB", [128, 1024], F32))
    PSC = E(nc.psum_tensor("PSC", [128, 1024], F32))
    PSD = E(nc.psum_tensor("PSD", [128, 1024], F32))
    PST = [PSA, PSB, PSC, PSD]
    def BK(i):
        return PST[i // 2][:, (i % 2) * 512:(i % 2) * 512 + 512]

    sem = {e: E(nc.semaphore("sem_" + e)) for e in ENGS}
    dsem = {e: E(nc.semaphore("dsem_" + e)) for e in ENGS}


    p = Plan()

    def fm(tt, pc):
        return tt[:, pc * TILE:(pc + 1) * TILE]

    def fmc(tt, pc, c):
        return tt[:, pc * TILE + c * 128: pc * TILE + (c + 1) * 128]

    def mask8(m):
        return AP(m.ap().tensor, 0, [[128, 128], [0, 8], [1, 128]])

    def h8(tt):
        return tt.rearrange("p (h j) -> p h j", h=8)

    def ps_h8(pst):
        return pst[:].rearrange("p (h j) -> p h j", h=8)

    def init_memsets(e):
        e.memset(id_bf[:], 1.0)
        e.memset(id_f32[:], 1.0)
        e.memset(msk_su[:], 1.0)
        e.memset(msk_iu[:], 1.0)
        e.memset(msk_sl[:], 1.0)
        e.memset(rscan[:], 1.0)
        for c in range(TC):
            e.memset(rscan[:, c * 128:c * 128 + 1], 0.0)
        e.memset(seg16b[:], 1.0)
        e.memset(bc16[:], 1.0)
        e.memset(ones_col[:], 1.0)
        e.memset(cst_eps[:, 0:1], 1e-5)
        e.memset(cst_eps[:, 1:2], 0.00064)
        e.memset(ones_r128[:], 1.0)
        e.memset(sT[:], 0.0)
        e.memset(sT_bf[:], 0.0)
        e.memset(xn_halo[:], 0.0)
        return e.memset(xf_halo[:], 0.0)

    tok_ms = p.g("vector", init_memsets)
    _asel = [
        (id_bf, OP.is_equal, 1, -1), (id_f32, OP.is_equal, 1, -1),
        (msk_su, OP.is_gt, 1, -1), (msk_iu, OP.is_ge, 1, -1),
        (msk_sl, OP.is_gt, -1, 1),
    ]
    for _t, _op, _st, _cm in _asel:
        p.g("gpsimd", (lambda e, _t=_t, _op=_op, _st=_st, _cm=_cm: e.affine_select(
            out=_t[:], in_=_t[:], pattern=[[_st, 128]], compare_op=_op,
            fill=0.0, base=0, channel_multiplier=_cm)), after=[tok_ms])
    p.g("gpsimd", (lambda e: e.affine_select(
        out=bc16[:].rearrange("r (b o) -> r b o", o=64),
        in_=bc16[:].rearrange("r (b o) -> r b o", o=64),
        pattern=[[1, 16], [0, 64]], compare_op=OP.is_equal,
        fill=0.0, base=0, channel_multiplier=-1)), after=[tok_ms])
    p.g("gpsimd", (lambda e: e.affine_select(
        out=seg16b[0:64, :].rearrange("p (c j) -> p c j", j=16),
        in_=seg16b[0:64, :].rearrange("p (c j) -> p c j", j=16),
        pattern=[[-2, PC], [1, 16]], compare_op=OP.is_equal,
        fill=0.0, base=0, channel_multiplier=0)), after=[tok_ms])
    p.g("gpsimd", (lambda e: e.affine_select(
        out=seg16b[64:128, :].rearrange("p (c j) -> p c j", j=16),
        in_=seg16b[64:128, :].rearrange("p (c j) -> p c j", j=16),
        pattern=[[-2, PC], [1, 16]], compare_op=OP.is_equal,
        fill=0.0, base=-1, channel_multiplier=0)), after=[tok_ms])
    tok_c0 = ("gpsimd", p.prog["gpsimd"])

    VSTAGE_LIST = WN1D + ["r_k", "bg1"]
    vstage = x_fm[0:8, 0:len(VSTAGE_LIST) * 128]

    def load_smalls(e):
        ins = []
        for vi, nm in enumerate(VSTAGE_LIST):
            if nm == "r_k":
                src_ = dr["r_k"].rearrange("h n -> (h n)").rearrange("(c p) -> c p", p=128)
            elif nm == "bg1":
                src_ = dr["bg1"].rearrange("(c p) -> c p", p=128)
            else:
                src_ = dr[nm].rearrange("(c p) -> c p", p=128)
            rows = src_.shape[0]
            ins.append(e.dma_start(out=vstage[0:rows, vi * 128:(vi + 1) * 128], in_=src_))
        ins.append(e.dma_start(out=bg2c[:], in_=AP(dr["bg2"].tensor, 0, [[1, 40], [1, 1]])))
        ins.append(e.dma_start(out=w1_sb[:].rearrange("p (c m) -> p c m", m=64),
                   in_=dr["w1"].rearrange("(c p) m -> p c m", p=128)))
        ins.append(e.dma_start(out=a1_sb[:].rearrange("p (c m) -> p c m", m=64),
                   in_=dr["a1"].rearrange("(c p) m -> p c m", p=128)))
        ins.append(e.dma_start(out=w2_sb[:], in_=dr["w2"]))
        ins.append(e.dma_start(out=a2_sb[:], in_=dr["a2"]))
        ins.append(e.dma_start(out=g1_sb[:].rearrange("p (c m) -> p c m", m=128),
                   in_=dr["g1"].rearrange("(c p) m -> p c m", p=128)))
        ins.append(e.dma_start(out=g2_sb[:], in_=dr["g2"]))
        ins.append(e.dma_start(out=wg1_sb[:].rearrange("p (c m) -> p c m", m=256),
                   in_=dr["wg1"].rearrange("(c p) m -> p c m", p=128)))
        ins.append(e.dma_start(out=wg2_sb[:].rearrange("p (c m) -> p c m", m=40),
                   in_=dr["wg2"].rearrange("(c p) m -> p c m", p=128)))
        ins.append(e.dma_start(out=be_sb[:], in_=dr["be"]))
        return ins

    N_SMALL = len(WN1D) + 2 + 1 + 9
    tok_sm = p.dma_g("gpsimd", load_smalls, N_SMALL)

    def vtr_mm(e):
        ins = None
        for vi, nm in enumerate(VSTAGE_LIST):
            rows = 2 if nm == "bg1" else 8
            ins = e.matmul(out=BK(0)[:, vi * 8: vi * 8 + rows],
                           lhsT=vstage[0:rows, vi * 128:(vi + 1) * 128],
                           rhs=id_f32[0:rows, 0:rows],
                           start=(vi == 0), stop=(vi == len(VSTAGE_LIST) - 1))
        return ins
    tok_vt0 = p.g("tensor", vtr_mm, after=[tok_sm, tok_c0], wbank=[0])

    def vtr_ev(e):
        ins = None
        for vi, nm in enumerate(VSTAGE_LIST):
            if nm == "bg1":
                ins = e.tensor_copy(out=bg1c[:], in_=BK(0)[:, vi * 8: vi * 8 + 2])
            else:
                dst = VC["r_k"] if nm == "r_k" else VC[nm]
                ins = e.tensor_copy(out=dst[:], in_=BK(0)[:, vi * 8: vi * 8 + 8])
        return ins
    tok_vt0e = p.g("vector", vtr_ev, after=[tok_vt0], rbank=[0])

    def derive(e):
        e.tensor_scalar(out=VC["omka"][:], in0=VC["k_a"][:], scalar1=-1.0,
                        scalar2=1.0, op0=OP.mult, op1=OP.add)
        e.tensor_mul(out=w1w2c[:], in0=VC["ln_pre_w"][:], in1=VC["ln_attn_w"][:])
        e.tensor_mul(out=w2b1c[:], in0=VC["ln_pre_b"][:], in1=VC["ln_attn_w"][:])
        e.tensor_mul(out=w1sq[:], in0=VC["ln_pre_w"][:], in1=VC["ln_pre_w"][:])
        e.tensor_mul(out=w1b1[:], in0=VC["ln_pre_w"][:], in1=VC["ln_pre_b"][:])
        e.reduce_sum(out=scr_red[:, 0:1], in_=VC["ln_pre_b"][:], axis=mybir.AxisListType.X)
        e.tensor_tensor(out=gscr[:, 0:PC], in0=VC["ln_pre_b"][:], in1=VC["ln_pre_b"][:],
                        op=OP.mult)
        return e.reduce_sum(out=scr_red[:, 1:2], in_=gscr[:, 0:PC], axis=mybir.AxisListType.X)
    tok_dv = p.g("vector", derive, after=[tok_vt0e])
    tok_sb1 = p.g("tensor", lambda e: e.matmul(out=BK(0)[0:1, 0:2], lhsT=ones_col[:],
                                               rhs=scr_red[:], start=True, stop=True),
                  after=[tok_dv], wbank=[0])
    tok_sb1e = p.g("vector", lambda e: e.tensor_copy(out=sb1[:], in_=BK(0)[0:1, 0:2]),
                   after=[tok_sb1], rbank=[0])

    def src_ap(u, hh):
        nm, idx = BIGW[u]
        if nm == "ffn1":
            base = dr["W_ffn1"][:, idx * D + hh * 512: idx * D + (hh + 1) * 512]
        elif nm == "ffn2":
            base = dr["W_ffn2"][idx * D:(idx + 1) * D, hh * 512:(hh + 1) * 512]
        elif nm == "We":
            base = dr["We"][idx][:, hh * 512:(hh + 1) * 512]
        else:
            base = dr[nm][:, hh * 512:(hh + 1) * 512]
        return base.rearrange("(c p) m -> p c m", p=128)

    wload = {}
    slot_free = [None] * 2
    wb_done = {}
    st_hu = [0]

    def hu_sb(hu, kc, mc):
        return wslot[hu % 2][:, kc * 512 + mc * 128: kc * 512 + (mc + 1) * 128]

    def ensure_hu(t, hu):
        while st_hu[0] <= hu:
            cur = st_hu[0]
            s = cur % 2
            aft = [slot_free[s]]
            if t == 0:
                aft += [wb_done.get(cur - 2), tok_sm]
                tokn = p.dma_g("gpsimd",
                               (lambda e, cur=cur, s=s: [e.dma_start(
                                   out=wslot[s][:].rearrange("p (c m) -> p c m", m=512),
                                   in_=src_ap(cur // 2, cur % 2))]), 1, after=aft)
                wload[(t, cur)] = tokn
                wb_done[cur] = p.dma_g("sync",
                                       (lambda e, cur=cur, s=s: [e.dma_start(
                                           out=wcache[:, cur * 4096:(cur + 1) * 4096],
                                           in_=wslot[s][:])]), 1, after=[tokn])
            else:
                q = "sync" if cur % 2 == 0 else "scalar"
                wload[(t, cur)] = p.dma_g(q,
                                          (lambda e, cur=cur, s=s: [e.dma_start(
                                              out=wslot[s][:],
                                              in_=wcache[:, cur * 4096:(cur + 1) * 4096])]),
                                          1, after=aft)
            st_hu[0] += 1

    dbg_after = []

    def dump_bf(t, name, bft, after):
        if debug and t == dbg_tile and name in dbg:
            if os.environ.get("DK_PRINT_SEQ"):
                print("DUMP", name, "seq", p.seq)
            dbg_after.append(p.dma_g("gpsimd",
                                     (lambda e: [e.dma_start(out=dbg[name][:], in_=bft[:])]),
                                     1, after=after))

    def dump_f(t, name, ft, after):
        if debug and t == dbg_tile and name in dbg:
            if os.environ.get("DK_PRINT_SEQ"):
                print("DUMP", name, "seq", p.seq)
            dbg_after.append(p.dma_g("gpsimd",
                                     (lambda e: [e.dma_start(out=dbg[name][:], in_=ft[:])]),
                                     1, after=after))

    tok_store_last = None
    tmb_free = [None, None]

    for t in range(NT):
        st_hu[0] = 0
        ensure_hu(t, 1)

        tok_u1 = None
        for tc in range(TC):
            buf = tmb[tc % 2]
            tok_ld = p.dma_g("sync",
                             (lambda e, t=t, tc=tc, buf=buf: [e.dma_start(
                                 out=buf[:],
                                 in_=x_in[t * TILE + tc * 128: t * TILE + (tc + 1) * 128, :])]),
                             1, after=[tmb_free[0], tmb_free[1]])

            if debug and t == dbg_tile and tc == 0 and os.environ.get("DK_DUMP_BUF"):
                if os.environ.get("DK_PRINT_SEQ"):
                    print("DUMP_BUF seq", p.seq)
                dbg_after.append(p.dma_g("gpsimd",
                                         (lambda e, buf=buf: [e.dma_start(
                                             out=dbg["d_u1"][:, 0:1024], in_=buf[:])]),
                                         1, after=[tok_ld]))

            def p0s(e, tc=tc, buf=buf):
                ins = None
                for hh in range(2):
                    e.activation(out=xsq_fm[:], in_=buf[:, hh * 512:(hh + 1) * 512],
                                 func=AF.Copy, accum_out=stat_h[:, hh:hh + 1])
                    ins = e.activation(out=xsq_fm[:], in_=buf[:, hh * 512:(hh + 1) * 512],
                                       func=AF.Square, accum_out=stat_h[:, 2 + hh:3 + hh])
                return ins
            tok_s = p.g("scalar", p0s, after=[tok_ld, tok_c0])

            def p0r(e, tc=tc):
                e.tensor_add(out=stat_s[:, 0:1], in0=stat_h[:, 0:1], in1=stat_h[:, 1:2])
                e.tensor_add(out=stat_q[:, 0:1], in0=stat_h[:, 2:3], in1=stat_h[:, 3:4])
                e.tensor_scalar_mul(out=stat_m[:, 0:1], in0=stat_s[:, 0:1], scalar1=1.0 / D)
                e.tensor_scalar_mul(out=stat_q[:, 0:1], in0=stat_q[:, 0:1], scalar1=1.0 / D)
                e.tensor_tensor(out=stat_i[:, 0:1], in0=stat_m[:, 0:1], in1=stat_m[:, 0:1],
                                op=OP.mult)
                e.tensor_tensor(out=stat_q[:, 0:1], in0=stat_q[:, 0:1],
                                in1=stat_i[:, 0:1], op=OP.subtract)
                return e.tensor_scalar_add(out=stat_q[:, 0:1], in0=stat_q[:, 0:1],
                                           scalar1=1e-5)
            tok_v = p.g("vector", p0r, after=[tok_s])
            tok_q = p.g("scalar", (lambda e: e.activation(
                out=stat_q[:, 0:1], in_=stat_q[:, 0:1], func=AF.Sqrt,
                scale=1.0)), after=[tok_v])

            def p0a(e, tc=tc, buf=buf):
                e.reciprocal(out=stat_i[:, 0:1], in_=stat_q[:, 0:1])
                return e.tensor_scalar(out=buf[:], in0=buf[:],
                                       scalar1=stat_m[:, 0:1],
                                       scalar2=stat_i[:, 0:1],
                                       op0=OP.subtract, op1=OP.mult)
            tok_ap = p.g("vector", p0a, after=[tok_q])
            if debug and t == dbg_tile and tc == 0 and os.environ.get("DK_DUMP_BUF") == "3":
                if os.environ.get("DK_PRINT_SEQ"):
                    print("DUMP_BUF3 seq", p.seq)
                dbg_after.append(p.dma_g("gpsimd",
                                         (lambda e: [e.dma_start(out=dbg["d_u1"][:, 0:4], in_=stat_h[:, 0:4], allow_slow_non_contiguous=True),
                                                     e.dma_start(out=dbg["d_u1"][:, 4:5], in_=stat_m[:, 0:1], allow_slow_non_contiguous=True),
                                                     e.dma_start(out=dbg["d_u1"][:, 5:6], in_=stat_q[:, 0:1], allow_slow_non_contiguous=True),
                                                     e.dma_start(out=dbg["d_u1"][:, 6:7], in_=stat_i[:, 0:1], allow_slow_non_contiguous=True)]),
                                         4, after=[tok_ap]))
            if debug and t == dbg_tile and tc == 0 and os.environ.get("DK_DUMP_BUF") == "2":
                if os.environ.get("DK_PRINT_SEQ"):
                    print("DUMP_BUF2 seq", p.seq)
                dbg_after.append(p.dma_g("gpsimd",
                                         (lambda e, buf=buf: [e.dma_start(
                                             out=dbg["d_u1"][:, 0:1024], in_=buf[:])]),
                                         1, after=[tok_ap]))
            for gi in range(2):
                bk = (tc % 2) * 2 + gi
                def tpg(e, tc=tc, gi=gi, bk=bk, buf=buf):
                    ins = None
                    for pc4 in range(4):
                        pc = gi * 4 + pc4
                        ins = e.matmul(out=BK(bk)[:, pc4 * 128:(pc4 + 1) * 128],
                                       lhsT=buf[:, pc * 128:(pc + 1) * 128],
                                       rhs=id_f32[:], start=(pc4 == 0), stop=(pc4 == 3))
                    return ins
                tk = p.g("tensor", tpg, after=[tok_ap, tok_c0], wbank=[bk])
                def tpe(e, tc=tc, gi=gi, bk=bk):
                    ins = None
                    for pc4 in range(4):
                        pc = gi * 4 + pc4
                        ins = e.tensor_copy(out=fmc(x_fm, pc, tc),
                                            in_=BK(bk)[:, pc4 * 128:(pc4 + 1) * 128])
                    return ins
                tok_u1 = p.g("vector", tpe, after=[tk], rbank=[bk])
            tmb_free[tc % 2] = tok_u1
        dump_f(t, "d_u1", x_fm, [tok_u1])

        tk_prev = None
        for pc in range(PC):
            sq_tok = p.g("scalar", (lambda e, pc=pc: e.activation(
                out=xsq_fm[:], in_=fm(x_fm, pc), func=AF.Square)),
                after=[tok_u1 if pc == 0 else tk_prev])
            def lna_mm(e, pc=pc):
                e.matmul(out=BK(4)[0:1, 0:512], lhsT=w1b1[:, pc:pc + 1], rhs=fm(x_fm, pc),
                         start=(pc == 0), stop=(pc == PC - 1))
                e.matmul(out=BK(5)[0:1, 0:512], lhsT=w1sq[:, pc:pc + 1], rhs=xsq_fm[:],
                         start=(pc == 0), stop=(pc == PC - 1))
                return e.matmul(out=BK(6)[0:1, 0:512], lhsT=VC["ln_pre_w"][:, pc:pc + 1],
                                rhs=fm(x_fm, pc), start=(pc == 0), stop=(pc == PC - 1))
            tk_prev = p.g("tensor", lna_mm, after=[sq_tok, tok_u1],
                          wbank=[4, 5, 6] if pc == 0 else [])
        def lna_rows(e):
            e.tensor_scalar(out=lnmu, in0=BK(6)[0:1, 0:512],
                            scalar1=sb1[0:1, 0:1], scalar2=1.0 / D, op0=OP.add, op1=OP.mult)
            e.tensor_scalar_mul(out=xsq_fm[0:1, :], in0=BK(4)[0:1, 0:512], scalar1=2.0)
            e.tensor_tensor(out=xsq_fm[0:1, :], in0=xsq_fm[0:1, :], in1=BK(5)[0:1, 0:512],
                            op=OP.add)
            e.tensor_scalar(out=xsq_fm[0:1, :], in0=xsq_fm[0:1, :],
                            scalar1=sb1[0:1, 1:2], scalar2=1.0 / D, op0=OP.add, op1=OP.mult)
            e.tensor_tensor(out=s2w_f[0:1, :], in0=lnmu, in1=lnmu, op=OP.mult)
            e.tensor_tensor(out=lniv, in0=xsq_fm[0:1, :], in1=s2w_f[0:1, :],
                            op=OP.subtract)
            return e.tensor_scalar_add(out=lniv, in0=lniv, scalar1=1e-5)
        tok_lr = p.g("vector", lna_rows, after=[tk_prev, tok_sb1e], rbank=[4, 5, 6])
        _lr = [tok_lr]
        if debug and t == dbg_tile and os.environ.get("DK_DUMP_LNA"):
            _d = p.dma_g("gpsimd",
                         (lambda e: [e.dma_start(out=dbg["d_u1"][4:5, 0:512], in_=lniv)]),
                         1, after=[tok_lr])
            dbg_after.append(_d); _lr.append(_d)
        tok_lq = p.g("scalar", lambda e: e.activation(out=lniv, in_=lniv,
                                                      func=AF.Sqrt, scale=1.0),
                     after=_lr)
        _lq = [tok_lq]
        if debug and t == dbg_tile and os.environ.get("DK_DUMP_LNA"):
            _d = p.dma_g("gpsimd",
                         (lambda e: [e.dma_start(out=dbg["d_u1"][5:6, 0:512], in_=lniv)]),
                         1, after=[tok_lq])
            dbg_after.append(_d); _lq.append(_d)
        def lna_i(e):
            e.reciprocal(out=lniv, in_=lniv)
            e.tensor_tensor(out=s2w_f[0:1, :], in0=lnmu, in1=lniv, op=OP.mult)
            return e.tensor_scalar_mul(out=s2w_f[0:1, :], in0=s2w_f[0:1, :], scalar1=-1.0)
        tok_li = p.g("vector", lna_i, after=_lq)

        if debug and t == dbg_tile and os.environ.get("DK_DUMP_LNA"):
            if os.environ.get("DK_PRINT_SEQ"):
                print("DUMP_LNA seq", p.seq)
            dbg_after.append(p.dma_g("gpsimd",
                                     (lambda e: [e.dma_start(out=dbg["d_u1"][0:1, 0:512], in_=lnmu),
                                                 e.dma_start(out=dbg["d_u1"][1:2, 0:512], in_=lniv),
                                                 e.dma_start(out=dbg["d_u1"][2:3, 0:512], in_=s2w_f[0:1, :]),
                                                 e.dma_start(out=dbg["d_u1"][3:4, 0:2], in_=sb1[0:1, :]),
                                                 e.dma_start(out=dbg["d_u1"][:, 8:10], in_=scr_red[:, 0:2], allow_slow_non_contiguous=True),
                                                 e.dma_start(out=dbg["d_u1"][:, 16:16 + PC], in_=gscr[:, 0:PC], allow_slow_non_contiguous=True)]),
                                     6, after=[tok_li]))
        def lna_bc(e):
            e.matmul(out=BK(5), lhsT=ones_r128[:], rhs=lniv, start=True, stop=True)
            return e.matmul(out=BK(6), lhsT=ones_r128[:], rhs=s2w_f[0:1, :],
                            start=True, stop=True)
        tok_lbc = p.g("tensor", lna_bc, after=[tok_li], wbank=[5, 6])
        tok_xn_last = None
        for pc in range(PC):
            def lna_xn(e, pc=pc):
                e.tensor_scalar(out=xsq_fm[:], in0=BK(5),
                                scalar1=w2b1c[:, pc:pc + 1],
                                scalar2=VC["ln_attn_b"][:, pc:pc + 1],
                                op0=OP.mult, op1=OP.add)
                e.scalar_tensor_tensor(out=xsq_fm[:], in0=BK(6),
                                       scalar=VC["ln_attn_w"][:, pc:pc + 1],
                                       in1=xsq_fm[:], op0=OP.mult, op1=OP.add)
                e.tensor_tensor(out=bt2_f[:], in0=fm(x_fm, pc), in1=BK(5), op=OP.mult)
                return e.scalar_tensor_tensor(out=fm(xn_bf, pc), in0=bt2_f[:],
                                              scalar=w1w2c[:, pc:pc + 1],
                                              in1=xsq_fm[:], op0=OP.mult, op1=OP.add)
            tok_xn_last = p.g("vector", (lambda e, pc=pc: lna_xn(e, pc)),
                              after=[tok_lbc], rbank=[5, 6])
        dump_bf(t, "d_xn", xn_bf, [tok_xn_last])

        def p_dx(e):
            ins = None
            for pc in range(PC):
                e.tensor_sub(out=fm(dx_bf, pc)[:, 1:TILE], in0=fm(xn_bf, pc)[:, 0:TILE - 1],
                             in1=fm(xn_bf, pc)[:, 1:TILE])
                e.tensor_sub(out=fm(dx_bf, pc)[:, 0:1], in0=xn_halo[:, pc:pc + 1],
                             in1=fm(xn_bf, pc)[:, 0:1])
                ins = e.tensor_copy(out=xn_halo[:, pc:pc + 1],
                                    in_=fm(xn_bf, pc)[:, TILE - 1:TILE])
            return ins
        tok_dx = p.g("vector", p_dx, after=[tok_xn_last])

        def dense_mm(hu0, rhs_of, evac_fn, evac_eng, gname=None, after0=()):
            ensure_hu(t, hu0 + 1)
            mm_toks = []
            for kc in range(PC):
                mm_after = []
                if gname is not None:
                    def mkmix(e, kc=kc):
                        return e.scalar_tensor_tensor(
                            out=mixp[kc % 2][:], in0=fm(dx_bf, kc),
                            scalar=VC[gname][:, kc:kc + 1], in1=fm(xn_bf, kc),
                            op0=OP.mult, op1=OP.add)
                    aft = [tok_dx] + list(after0)
                    if kc >= 2:
                        aft.append(mm_toks[kc - 2])
                    mm_after.append(p.g("vector", mkmix, after=aft))
                def mmk(e, kc=kc, hu0=hu0):
                    ins = None
                    rhs = mixp[kc % 2][:] if gname is not None else rhs_of(kc)
                    for mc in range(8):
                        ins = e.matmul(out=BK(mc), lhsT=hu_sb(hu0 + mc // 4, kc, mc % 4),
                                       rhs=rhs, start=(kc == 0), stop=(kc == PC - 1))
                    return ins
                mm_after += [wload[(t, hu0)], wload[(t, hu0 + 1)]] + list(after0)
                mm_toks.append(p.g("tensor", mmk, after=mm_after,
                                   wbank=list(range(8)) if kc == 0 else []))
            def evall(e):
                ins = None
                for mc in range(8):
                    ins = evac_fn(e, mc, BK(mc))
                return ins
            ev = p.g(evac_eng, evall, after=[mm_toks[-1]], rbank=list(range(8)))
            slot_free[hu0 % 2] = mm_toks[-1]
            slot_free[(hu0 + 1) % 2] = mm_toks[-1]
            return mm_toks[-1], ev

        mm_r, ev_r_t = dense_mm(0, None, (lambda e, mc, bk: e.tensor_copy(
            out=fm(r_bf, mc), in_=bk)), "vector", gname="x_r")

        def ev_k(e, mc, bk):
            e.tensor_copy(out=fm(km_bf, mc), in_=bk)
            return e.tensor_scalar_mul(out=fm(kk_bf, mc), in0=bk,
                                       scalar1=VC["k_k"][:, mc:mc + 1])
        mm_k, ev_k_t = dense_mm(2, None, ev_k, "vector", gname="x_k")

        mm_v, ev_v_t = dense_mm(4, None, (lambda e, mc, bk: e.tensor_copy(
            out=fm(v_bf, mc), in_=bk)), "vector", gname="x_v")

        def build_mix_full(gname, aft):
            def mk(e):
                ins = None
                for kc in range(PC):
                    ins = e.scalar_tensor_tensor(
                        out=fm(o_bf, kc), in0=fm(dx_bf, kc),
                        scalar=VC[gname][:, kc:kc + 1], in1=fm(xn_bf, kc),
                        op0=OP.mult, op1=OP.add)
                return ins
            return p.g("vector", mk, after=aft)

        tok_mixw = build_mix_full("x_w", [ev_v_t])
        def w1mm(e):
            ins = None
            for kc in range(PC):
                ins = e.matmul(out=BK(0)[0:64, :], lhsT=w1_sb[:, kc * 64:(kc + 1) * 64],
                               rhs=fm(o_bf, kc), start=(kc == 0), stop=(kc == PC - 1))
            return ins
        tok_w1 = p.g("tensor", w1mm, after=[tok_mixw], wbank=[0])
        tok_w1e = p.g("scalar", lambda e: e.activation(out=t1_bf[:], in_=BK(0)[0:64, :],
                                                       func=AF.Tanh),
                      after=[tok_w1], rbank=[0])
        tok_we = None
        for wv in range(2):
            def w2mm(e, wv=wv):
                ins = None
                for mc4 in range(4):
                    mc = wv * 4 + mc4
                    ins = e.matmul(out=BK(4 + mc4), lhsT=w2_sb[:, mc * 128:(mc + 1) * 128],
                                   rhs=t1_bf[:], start=True, stop=True)
                return ins
            mt = p.g("tensor", w2mm, after=[tok_w1e], wbank=[4, 5, 6, 7])
            def w2ev(e, wv=wv):
                ins = None
                for mc4 in range(4):
                    mc = wv * 4 + mc4
                    ins = e.activation(out=fm(e_bf, mc), in_=BK(4 + mc4),
                                       func=AF.Sigmoid, bias=VC["w0"][:, mc:mc + 1])
                return ins
            tok_we = p.g("scalar", w2ev, after=[mt], rbank=[4, 5, 6, 7])
        tok_mixa = build_mix_full("x_a", [tok_w1])
        def a1mm(e):
            ins = None
            for kc in range(PC):
                ins = e.matmul(out=BK(1)[0:64, :], lhsT=a1_sb[:, kc * 64:(kc + 1) * 64],
                               rhs=fm(o_bf, kc), start=(kc == 0), stop=(kc == PC - 1))
            return ins
        tok_a1 = p.g("tensor", a1mm, after=[tok_mixa], wbank=[1])
        tok_a1e = p.g("scalar", lambda e: e.copy(out=t1_bf[:], in_=BK(1)[0:64, :]),
                      after=[tok_a1, tok_we], rbank=[1])
        tok_ae = None
        for wv in range(2):
            def a2mm(e, wv=wv):
                ins = None
                for mc4 in range(4):
                    mc = wv * 4 + mc4
                    ins = e.matmul(out=BK(mc4), lhsT=a2_sb[:, mc * 128:(mc + 1) * 128],
                                   rhs=t1_bf[:], start=True, stop=True)
                return ins
            mt = p.g("tensor", a2mm, after=[tok_a1e], wbank=[0, 1, 2, 3])
            def a2ev(e, wv=wv):
                ins = None
                for mc4 in range(4):
                    mc = wv * 4 + mc4
                    ins = e.activation(out=fm(b_bf, mc), in_=BK(mc4),
                                       func=AF.Sigmoid, bias=VC["a0"][:, mc:mc + 1])
                return ins
            tok_ae = p.g("scalar", a2ev, after=[mt], rbank=[0, 1, 2, 3])
        tok_mixg = build_mix_full("x_g", [tok_a1])
        def g1mm(e):
            ins = None
            for kc in range(PC):
                ins = e.matmul(out=BK(2), lhsT=g1_sb[:, kc * 128:(kc + 1) * 128],
                               rhs=fm(o_bf, kc), start=(kc == 0), stop=(kc == PC - 1))
            return ins
        tok_g1 = p.g("tensor", g1mm, after=[tok_mixg], wbank=[2])
        tok_g1e = p.g("scalar", lambda e: e.activation(out=sg_bf[:], in_=BK(2),
                                                       func=AF.Sigmoid),
                      after=[tok_g1], rbank=[2])
        tok_ge = None
        for wv in range(2):
            def g2mm(e, wv=wv):
                ins = None
                for mc4 in range(4):
                    mc = wv * 4 + mc4
                    ins = e.matmul(out=BK(4 + mc4), lhsT=g2_sb[:, mc * 128:(mc + 1) * 128],
                                   rhs=sg_bf[:], start=True, stop=True)
                return ins
            mt = p.g("tensor", g2mm, after=[tok_g1e], wbank=[4, 5, 6, 7])
            def g2ev(e, wv=wv):
                ins = None
                for mc4 in range(4):
                    mc = wv * 4 + mc4
                    ins = e.copy(out=fm(g_bf, mc), in_=BK(4 + mc4))
                return ins
            tok_ge = p.g("scalar", g2ev, after=[mt], rbank=[4, 5, 6, 7])

        mm_sq_tok = None
        for kc in range(PC):
            sq_t = p.g("vector", (lambda e, kc=kc: e.tensor_mul(
                out=pp_bf[:], in0=fm(kk_bf, kc), in1=fm(kk_bf, kc))),
                after=[ev_k_t] if kc == 0 else [mm_sq_tok])
            mm_sq_tok = p.g("tensor", (lambda e, kc=kc: e.matmul(
                out=BK(3)[0:16, :], lhsT=seg16b[:, kc * 16:(kc + 1) * 16],
                rhs=pp_bf[:], start=(kc == 0), stop=(kc == PC - 1))),
                after=[sq_t], wbank=[3] if kc == 0 else [])
        tok_kn1 = p.g("scalar", lambda e: e.activation(out=hrow[:, :], in_=BK(3)[0:16, :],
                                                       func=AF.Sqrt),
                      after=[mm_sq_tok], rbank=[3])
        def kkn2(e):
            e.tensor_scalar_max(out=hrow[:, :], in0=hrow[:, :], scalar1=1e-12)
            return e.reciprocal(out=hrow2[:, :], in_=hrow[:, :])
        tok_kn2 = p.g("vector", kkn2, after=[tok_kn1])
        tok_kkn = None
        for pc in range(PC):
            mt = p.g("tensor", (lambda e, pc=pc: e.matmul(
                out=BK(2), lhsT=bc16[0:16, pc * 128:(pc + 1) * 128],
                rhs=hrow2[0:16, :], start=True, stop=True)),
                after=[tok_kn2], wbank=[2])
            def kkap(e, pc=pc):
                e.tensor_tensor(out=fm(kk_bf, pc), in0=fm(kk_bf, pc), in1=BK(2), op=OP.mult)
                e.tensor_scalar(out=pp_bf[:], in0=fm(b_bf, pc),
                                scalar1=VC["k_a"][:, pc:pc + 1],
                                scalar2=VC["omka"][:, pc:pc + 1], op0=OP.mult, op1=OP.add)
                e.tensor_tensor(out=fm(km_bf, pc), in0=fm(km_bf, pc), in1=pp_bf[:], op=OP.mult)
                return e.tensor_tensor(out=fm(b_bf, pc), in0=fm(kk_bf, pc), in1=fm(b_bf, pc),
                                       op=OP.mult)
            tok_kkn = p.g("vector", (lambda e, pc=pc: kkap(e, pc)), after=[mt, tok_ae],
                          rbank=[2])
        dump_bf(t, "d_r", r_bf, [ev_r_t])
        dump_bf(t, "d_e", e_bf, [tok_we])
        dump_bf(t, "d_kk", kk_bf, [tok_kkn])
        dump_bf(t, "d_kmod", km_bf, [tok_kkn])
        dump_bf(t, "d_v", v_bf, [ev_v_t])
        dump_bf(t, "d_b", b_bf, [tok_kkn])
        dump_bf(t, "d_g", g_bf, [tok_ge])

        def tilG(nm, j, c):
            return TIL[nm][:, j * TILE + c * 128: j * TILE + (c + 1) * 128]

        def e2G(j, c):
            return E2G[:, j * TILE + c * 128: j * TILE + (c + 1) * 128]

        tok_grp = tok_kkn
        tok_oev = None
        for G in range(2):
            til_last = None
            for j in range(4):
                pc = 4 * G + j
                tk_cs = p.g("vector", (lambda e, pc=pc: e.tensor_tensor_scan(
                    out=cs_f[:], data0=rscan[:], data1=fm(e_bf, pc), initial=0.0,
                    op0=OP.mult, op1=OP.add)),
                    after=[tok_we, tok_grp, til_last])
                def mkcse(e, pc=pc):
                    e.tensor_tensor(out=cse_f[:], in0=cs_f[:], in1=fm(e_bf, pc),
                                    op=OP.subtract)
                    return e.tensor_copy(
                        out=csL_f[:, 0:TC],
                        in_=AP(cs_f.ap().tensor, 127, [[TILE, 128], [128, TC]]))
                tk_c2 = p.g("vector", mkcse, after=[tk_cs])
                def mkexp1(e, pc=pc, j=j):
                    e.activation(out=dec[0][:], in_=cs_f[:], func=AF.Exp, scale=-SC)
                    e.activation(out=dec[1][:], in_=cse_f[:], func=AF.Exp, scale=-SC)
                    return e.activation(out=DLG[:, j * TC:(j + 1) * TC],
                                        in_=csL_f[:, 0:TC], func=AF.Exp, scale=-SC)
                tk_ex = p.g("scalar", mkexp1, after=[tk_c2])
                def mktil1(e, pc=pc, j=j):
                    sl = slice(j * TILE, (j + 1) * TILE)
                    e.tensor_mul(out=TIL["rt"][:, sl], in0=fm(r_bf, pc), in1=dec[0][:])
                    e.scalar_tensor_tensor(out=TIL["at"][:, sl], in0=fm(kk_bf, pc),
                                           scalar=-1.0, in1=dec[1][:],
                                           op0=OP.mult, op1=OP.mult)
                    ins = None
                    for cq in range(TC):
                        ins = e.tensor_scalar(
                            out=cse_f[:, cq * 128:(cq + 1) * 128],
                            in0=cs_f[:, cq * 128:(cq + 1) * 128],
                            scalar1=csL_f[:, cq:cq + 1], scalar2=None, op0=OP.subtract)
                    return ins
                tk_t1 = p.g("vector", mktil1, after=[tk_ex])
                def mkexp2(e, pc=pc, j=j):
                    e.activation(out=dec[0][:], in_=cs_f[:], func=AF.Exp, scale=SC)
                    return e.activation(out=E2G[:, j * TILE:(j + 1) * TILE], in_=cse_f[:],
                                        func=AF.Exp, scale=SC)
                tk_e2 = p.g("scalar", mkexp2, after=[tk_t1])
                def mktil2(e, pc=pc, j=j):
                    sl = slice(j * TILE, (j + 1) * TILE)
                    e.tensor_mul(out=TIL["bt"][:, sl], in0=fm(b_bf, pc), in1=dec[0][:])
                    return e.tensor_mul(out=TIL["kt"][:, sl], in0=fm(km_bf, pc), in1=dec[0][:])
                til_last = p.g("vector", mktil2, after=[tk_e2])
            tok_til = til_last

            tok_state_ev = None
            for c in range(TC):
                def ghmm(e, G=G, c=c):
                    ins = None
                    for q, (dst, sa, sb) in enumerate(
                            [(PSA, "bt", "at"), (PSB, "at", "bt"), (PSC, "kt", "at")]):
                        for hf in range(2):
                            for j in range(4):
                                sl = 4 * hf + j
                                ins = e.matmul(
                                    out=dst[:, sl * 128:(sl + 1) * 128],
                                    lhsT=tilG(sa, j, c)[hf * 64:(hf + 1) * 64, :],
                                    rhs=tilG(sb, j, c)[hf * 64:(hf + 1) * 64, :],
                                    start=(j == 0), stop=(j == 3),
                                    tile_position=(64 * hf, 0))
                    return ins
                tok_gh = p.g("tensor", ghmm, after=[tok_til],
                             wbank=[0, 1, 2, 3, 4, 5])
                def ghev(e):
                    e.tensor_tensor(out=h8(X_sb), in0=ps_h8(PSA), in1=mask8(msk_su),
                                    op=OP.mult)
                    e.tensor_tensor(out=h8(XT_sb), in0=ps_h8(PSB), in1=mask8(msk_sl),
                                    op=OP.mult)
                    return e.tensor_tensor(out=h8(Hm_sb), in0=ps_h8(PSC), in1=mask8(msk_su),
                                           op=OP.mult)
                tok_ghe = p.g("vector", ghev, after=[tok_gh], rbank=[0, 1, 2, 3, 4, 5])
                def vtmm(e, G=G, c=c):
                    ins = None
                    for hf in range(2):
                        for j in range(4):
                            ins = e.matmul(
                                out=BK(6 + hf)[:, j * 64:(j + 1) * 64],
                                lhsT=fmc(v_bf, 4 * G + j, c)[hf * 64:(hf + 1) * 64, :],
                                rhs=id_bf[hf * 64:(hf + 1) * 64, hf * 64:hf * 64 + 64],
                                start=(j == 0), stop=(j == 3),
                                tile_position=(64 * hf, 0))
                    return ins
                tok_vt = p.g("tensor", vtmm, after=[], wbank=[6, 7])
                def vtev(e):
                    e.copy(out=vtm_sb[:, 0:256], in_=BK(6)[:, 0:256])
                    return e.copy(out=vtm_sb[:, 256:512], in_=BK(7)[:, 0:256])
                tok_vte = p.g("scalar", vtev, after=[tok_vt], rbank=[6, 7])
                def rhsA(e, G=G, c=c):
                    ins = None
                    for hf in range(2):
                        for j in range(4):
                            ins = e.matmul(
                                out=BK(hf)[hf * 64:(hf + 1) * 64, j * 128:(j + 1) * 128],
                                lhsT=sT_bf[hf * 64:(hf + 1) * 64,
                                           (4 * G + j) * 64:(4 * G + j + 1) * 64],
                                rhs=tilG("at", j, c)[hf * 64:(hf + 1) * 64, :],
                                start=(j == 0), stop=(j == 3),
                                tile_position=(64 * hf, 64 * hf))
                    return ins
                tok_rhsA = p.g("tensor", rhsA, after=[tok_state_ev, tok_til],
                               wbank=[0, 1])
                def rhs0(e):
                    e.copy(out=pp_bf[0:64, :], in_=BK(0)[0:64, 0:512])
                    return e.copy(out=pp_bf[64:128, :], in_=BK(1)[64:128, 0:512])
                tok_rhs0 = p.g("scalar", rhs0, after=[tok_rhsA], rbank=[0, 1])
                def rhsB(e, G=G, c=c):
                    ins = None
                    for hf in range(2):
                        for j in range(4):
                            sl = 4 * hf + j
                            ins = e.matmul(
                                out=BK(2)[hf * 64:(hf + 1) * 64, j * 128:(j + 1) * 128],
                                lhsT=vtm_sb[:, sl * 64:(sl + 1) * 64],
                                rhs=Hm_sb[:, sl * 128:(sl + 1) * 128],
                                start=(j == 0), stop=(j == 3),
                                tile_position=(0, 64 * hf))
                    return ins
                tok_rhs = p.g("tensor", rhsB, after=[tok_vte, tok_ghe], wbank=[2])
                tok_rhse = p.g("vector", lambda e: e.tensor_tensor(
                    out=rhs_sb, in0=BK(2)[:, 0:512], in1=pp_bf[:], op=OP.add),
                               after=[tok_rhs0, tok_rhs], rbank=[2])
                def u0mm(e):
                    ins = None
                    for hf in range(2):
                        for j in range(4):
                            ins = e.matmul(
                                out=BK(hf)[:, j * 64:(j + 1) * 64],
                                lhsT=rhs_sb[hf * 64:(hf + 1) * 64, j * 128:(j + 1) * 128],
                                rhs=id_bf[hf * 64:(hf + 1) * 64, hf * 64:hf * 64 + 64],
                                start=(j == 0), stop=(j == 3),
                                tile_position=(64 * hf, 0))
                    return ins
                tok_u0 = p.g("tensor", u0mm, after=[tok_rhse], wbank=[0, 1])
                def u0ev(e):
                    e.copy(out=u_sb[:, 0:256], in_=BK(0)[:, 0:256])
                    return e.copy(out=u_sb[:, 256:512], in_=BK(1)[:, 0:256])
                tok_ue = p.g("scalar", u0ev, after=[tok_u0], rbank=[0, 1])
                tok_xe = None
                for ks in range(7):
                    do_sq = ks < 6
                    def ummm(e):
                        ins = None
                        for hg in range(8):
                            ins = e.matmul(out=BK(0)[:, hg * 64:(hg + 1) * 64],
                                           lhsT=X_sb[:, hg * 128:(hg + 1) * 128],
                                           rhs=u_sb[:, hg * 64:(hg + 1) * 64],
                                           start=(hg == 0), stop=(hg == 7))
                        return ins
                    tok_um = p.g("tensor", ummm, after=[tok_ue, tok_xe], wbank=[0])
                    tok_sq = None
                    if do_sq:
                        def sqmm(e, ks=ks):
                            ins = None
                            for hg in range(8):
                                ins = e.matmul(out=PSB[:, hg * 128:(hg + 1) * 128],
                                               lhsT=XT_sb[:, hg * 128:(hg + 1) * 128],
                                               rhs=X_sb[:, hg * 128:(hg + 1) * 128],
                                               start=(hg % 4 == 0), stop=(hg % 4 == 3))
                            if ks < 5:
                                for hg in range(8):
                                    ins = e.matmul(out=PSC[:, hg * 128:(hg + 1) * 128],
                                                   lhsT=X_sb[:, hg * 128:(hg + 1) * 128],
                                                   rhs=XT_sb[:, hg * 128:(hg + 1) * 128],
                                                   start=(hg % 4 == 0), stop=(hg % 4 == 3))
                            return ins
                        tok_sq = p.g("tensor", sqmm, after=[tok_xe],
                                     wbank=[2, 3] + ([4, 5] if ks < 5 else []))
                    tok_ue = p.g("vector", lambda e: e.tensor_tensor(
                        out=u_sb, in0=BK(0)[:, 0:512], in1=u_sb, op=OP.add),
                                 after=[tok_um], rbank=[0])
                    if do_sq:
                        def chev(e, ks=ks):
                            ins = e.tensor_copy(out=X_sb, in_=PSB[:])
                            if ks < 5:
                                ins = e.tensor_copy(out=XT_sb, in_=PSC[:])
                            return ins
                        tok_xe = p.g("vector", chev, after=[tok_sq],
                                     rbank=[2, 3] + ([4, 5] if ks < 5 else []))
                def m12mm(e, G=G, c=c):
                    ins = None
                    for q, (dst, sa) in enumerate([(PSB, "bt"), (PSC, "kt")]):
                        for hf in range(2):
                            for j in range(4):
                                sl = 4 * hf + j
                                ins = e.matmul(
                                    out=dst[:, sl * 128:(sl + 1) * 128],
                                    lhsT=tilG(sa, j, c)[hf * 64:(hf + 1) * 64, :],
                                    rhs=tilG("rt", j, c)[hf * 64:(hf + 1) * 64, :],
                                    start=(j == 0), stop=(j == 3),
                                    tile_position=(64 * hf, 0))
                    return ins
                tok_m12 = p.g("tensor", m12mm, after=[tok_xe], wbank=[2, 3, 4, 5])
                def m12ev(e):
                    e.tensor_tensor(out=h8(M1_sb), in0=ps_h8(PSB), in1=mask8(msk_iu),
                                    op=OP.mult)
                    return e.tensor_tensor(out=h8(M2_sb), in0=ps_h8(PSC), in1=mask8(msk_iu),
                                           op=OP.mult)
                tok_m12e = p.g("vector", m12ev, after=[tok_m12], rbank=[2, 3, 4, 5])
                def outA(e, G=G, c=c):
                    ins = None
                    for hf in range(2):
                        for j in range(4):
                            ins = e.matmul(
                                out=BK(hf)[hf * 64:(hf + 1) * 64, j * 128:(j + 1) * 128],
                                lhsT=sT_bf[hf * 64:(hf + 1) * 64,
                                           (4 * G + j) * 64:(4 * G + j + 1) * 64],
                                rhs=tilG("rt", j, c)[hf * 64:(hf + 1) * 64, :],
                                start=(j == 0), stop=(j == 3),
                                tile_position=(64 * hf, 64 * hf))
                    return ins
                tok_outA = p.g("tensor", outA, after=[tok_state_ev, tok_til],
                               wbank=[0, 1])
                def out0(e):
                    e.copy(out=pp_bf[0:64, :], in_=BK(0)[0:64, 0:512])
                    return e.copy(out=pp_bf[64:128, :], in_=BK(1)[64:128, 0:512])
                tok_out0 = p.g("scalar", out0, after=[tok_outA], rbank=[0, 1])
                def outB(e, G=G, c=c):
                    ins = None
                    for hf in range(2):
                        for j in range(4):
                            sl = 4 * hf + j
                            e.matmul(out=BK(6)[hf * 64:(hf + 1) * 64, j * 128:(j + 1) * 128],
                                     lhsT=u_sb[:, sl * 64:(sl + 1) * 64],
                                     rhs=M1_sb[:, sl * 128:(sl + 1) * 128],
                                     start=(j == 0), stop=False, tile_position=(0, 64 * hf))
                        for j in range(4):
                            sl = 4 * hf + j
                            ins = e.matmul(out=BK(6)[hf * 64:(hf + 1) * 64,
                                                     j * 128:(j + 1) * 128],
                                           lhsT=vtm_sb[:, sl * 64:(sl + 1) * 64],
                                           rhs=M2_sb[:, sl * 128:(sl + 1) * 128],
                                           start=False, stop=(j == 3),
                                           tile_position=(0, 64 * hf))
                    return ins
                tok_out = p.g("tensor", outB, after=[tok_m12e, tok_ue], wbank=[6])
                def outev(e, G=G, c=c):
                    return e.tensor_tensor(
                        out=AP(o_bf.ap().tensor, (4 * G) * TILE + c * 128,
                               [[PC * TILE, 128], [TILE, 4], [1, 128]]),
                        in0=BK(6).rearrange("p (j x) -> p j x", j=4),
                        in1=pp_bf[:].rearrange("p (j x) -> p j x", j=4), op=OP.add)
                tok_oev = p.g("vector", outev, after=[tok_out0], rbank=[6])
                def bbkbf(e, G=G, c=c):
                    ins = None
                    for j in range(4):
                        e.tensor_mul(out=bbkb[:, j * 128:(j + 1) * 128],
                                     in0=fmc(b_bf, 4 * G + j, c), in1=e2G(j, c))
                        ins = e.tensor_mul(out=bbkb[:, 512 + j * 128: 512 + (j + 1) * 128],
                                           in0=fmc(km_bf, 4 * G + j, c), in1=e2G(j, c))
                    return ins
                tok_bk = p.g("vector", bbkbf, after=[tok_til])
                def sttr(e):
                    ins = None
                    for q in range(2):
                        for hf in range(2):
                            for j in range(4):
                                ins = e.matmul(
                                    out=BK(2 + 2 * q + hf)[:, j * 64:(j + 1) * 64],
                                    lhsT=bbkb[hf * 64:(hf + 1) * 64,
                                              q * 512 + j * 128: q * 512 + (j + 1) * 128],
                                    rhs=id_bf[hf * 64:(hf + 1) * 64, hf * 64:hf * 64 + 64],
                                    start=(j == 0), stop=(j == 3),
                                    tile_position=(64 * hf, 0))
                    return ins
                tok_st1 = p.g("tensor", sttr, after=[tok_bk], wbank=[2, 3, 4, 5])
                def sttre(e):
                    e.copy(out=btm_sb[:, 0:256], in_=BK(2)[:, 0:256])
                    e.copy(out=btm_sb[:, 256:512], in_=BK(3)[:, 0:256])
                    e.copy(out=ktm_sb[:, 0:256], in_=BK(4)[:, 0:256])
                    return e.copy(out=ktm_sb[:, 256:512], in_=BK(5)[:, 0:256])
                tok_st1e = p.g("scalar", sttre, after=[tok_st1], rbank=[2, 3, 4, 5])
                def stprod(e, G=G, c=c):
                    ins = None
                    for hf in range(2):
                        for j in range(4):
                            sl = 4 * hf + j
                            e.matmul(out=BK(6)[hf * 64:(hf + 1) * 64,
                                               j * 64:(j + 1) * 64],
                                     lhsT=btm_sb[:, sl * 64:(sl + 1) * 64],
                                     rhs=u_sb[:, sl * 64:(sl + 1) * 64],
                                     start=(j == 0), stop=False,
                                     tile_position=(0, 64 * hf))
                        for j in range(4):
                            sl = 4 * hf + j
                            ins = e.matmul(out=BK(6)[hf * 64:(hf + 1) * 64,
                                                     j * 64:(j + 1) * 64],
                                           lhsT=ktm_sb[:, sl * 64:(sl + 1) * 64],
                                           rhs=vtm_sb[:, sl * 64:(sl + 1) * 64],
                                           start=False, stop=(j == 3),
                                           tile_position=(0, 64 * hf))
                    return ins
                tok_st2 = p.g("tensor", stprod, after=[tok_st1e], wbank=[6])
                def stev(e, G=G, c=c):
                    for j in range(4):
                        e.scalar_tensor_tensor(
                            out=sT[:, (4 * G + j) * 64:(4 * G + j + 1) * 64],
                            in0=sT[:, (4 * G + j) * 64:(4 * G + j + 1) * 64],
                            scalar=DLG[:, j * TC + c: j * TC + c + 1],
                            in1=BK(6)[:, j * 64:(j + 1) * 64],
                            op0=OP.mult, op1=OP.add)
                    return e.tensor_copy(
                        out=sT_bf[:, (4 * G) * 64:(4 * G + 4) * 64],
                        in_=sT[:, (4 * G) * 64:(4 * G + 4) * 64])
                tok_state_ev = p.g("vector", stev, after=[tok_st2], rbank=[6])
            tok_grp = tok_state_ev
        tok_wkv = tok_grp
        dump_bf(t, "d_o", o_bf, [tok_wkv, tok_oev])

        gs_tok = None
        for pc in range(PC):
            gs_tok = p.g("tensor", (lambda e, pc=pc: e.matmul(
                out=BK(0)[0:16, :], lhsT=seg16b[:, pc * 16:(pc + 1) * 16],
                rhs=fm(o_bf, pc), start=(pc == 0), stop=(pc == PC - 1))),
                after=[tok_wkv, tok_oev], wbank=[0] if pc == 0 else [])
        gq_tok = None
        for pc in range(PC):
            sq2 = p.g("vector", (lambda e, pc=pc: e.tensor_mul(
                out=pp_bf[:], in0=fm(o_bf, pc), in1=fm(o_bf, pc))),
                after=[tok_wkv, tok_oev] if pc == 0 else [gq_tok])
            gq_tok = p.g("tensor", (lambda e, pc=pc: e.matmul(
                out=BK(1)[0:16, :], lhsT=seg16b[:, pc * 16:(pc + 1) * 16],
                rhs=pp_bf[:], start=(pc == 0), stop=(pc == PC - 1))),
                after=[sq2], wbank=[1] if pc == 0 else [])
        def gnrows(e):
            e.tensor_scalar_mul(out=hrow3[:, :], in0=BK(0)[0:16, :], scalar1=1.0 / N)
            e.tensor_scalar_mul(out=hrow2[:, :], in0=BK(1)[0:16, :], scalar1=1.0 / N)
            e.tensor_tensor(out=hrow[:, :], in0=hrow3[:, :], in1=hrow3[:, :], op=OP.mult)
            e.tensor_tensor(out=hrow2[:, :], in0=hrow2[:, :], in1=hrow[:, :],
                            op=OP.subtract)
            return e.tensor_scalar_add(out=hrow2[:, :], in0=hrow2[:, :], scalar1=0.00064)
        tok_gr = p.g("vector", gnrows, after=[gs_tok, gq_tok], rbank=[0, 1])
        tok_gq = p.g("scalar", lambda e: e.activation(out=hrow2[:, :], in_=hrow2[:, :],
                                                      func=AF.Sqrt, scale=1.0),
                     after=[tok_gr])
        tok_gi = p.g("vector", lambda e: e.reciprocal(out=hrow2[:, :], in_=hrow2[:, :]),
                     after=[tok_gq])
        rk_tok = None
        mt_prev = tok_gi
        for pc in range(PC):
            mt = p.g("vector", (lambda e, pc=pc: e.scalar_tensor_tensor(
                out=pp_bf[:], in0=fm(r_bf, pc), scalar=VC["r_k"][:, pc:pc + 1],
                in1=fm(km_bf, pc), op0=OP.mult, op1=OP.mult)), after=[mt_prev])
            rk_tok = p.g("tensor", (lambda e, pc=pc: e.matmul(
                out=BK(2)[0:16, :], lhsT=seg16b[:, pc * 16:(pc + 1) * 16],
                rhs=pp_bf[:], start=(pc == 0), stop=(pc == PC - 1))),
                after=[mt], wbank=[2] if pc == 0 else [])
            mt_prev = rk_tok
        tok_rkr = p.g("vector", lambda e: e.tensor_copy(out=hrow[:, :], in_=BK(2)[0:16, :]),
                      after=[rk_tok], rbank=[2])
        tok_og = None
        for pc in range(PC):
            def gnbc(e, pc=pc):
                e.matmul(out=BK(3), lhsT=bc16[0:16, pc * 128:(pc + 1) * 128],
                         rhs=hrow2[0:16, :], start=True, stop=True)
                e.matmul(out=BK(4), lhsT=bc16[0:16, pc * 128:(pc + 1) * 128],
                         rhs=hrow3[0:16, :], start=True, stop=True)
                return e.matmul(out=BK(5), lhsT=bc16[0:16, pc * 128:(pc + 1) * 128],
                                rhs=hrow[0:16, :], start=True, stop=True)
            mt = p.g("tensor", gnbc, after=[tok_gi, tok_rkr], wbank=[3, 4, 5])
            def gnap(e, pc=pc):
                e.tensor_sub(out=s2w_f[:], in0=fm(o_bf, pc), in1=BK(4))
                e.tensor_tensor(out=s2w_f[:], in0=s2w_f[:], in1=BK(3), op=OP.mult)
                e.tensor_scalar(out=s2w_f[:], in0=s2w_f[:],
                                scalar1=VC["gn_w"][:, pc:pc + 1],
                                scalar2=VC["gn_b"][:, pc:pc + 1], op0=OP.mult, op1=OP.add)
                e.tensor_tensor(out=bt2_f[:], in0=BK(5), in1=fm(v_bf, pc), op=OP.mult)
                e.tensor_tensor(out=s2w_f[:], in0=s2w_f[:], in1=bt2_f[:], op=OP.add)
                return e.tensor_tensor(out=fm(kk_bf, pc), in0=s2w_f[:], in1=fm(g_bf, pc),
                                       op=OP.mult)
            tok_og = p.g("vector", (lambda e, pc=pc: gnap(e, pc)), after=[mt],
                         rbank=[3, 4, 5])

        def ev_wo(e, mc, bk):
            e.scalar_tensor_tensor(out=fm(x_fm, mc), in0=fm(x_fm, mc),
                                   scalar=VC["ln_pre_w"][:, mc:mc + 1], in1=bk,
                                   op0=OP.mult, op1=OP.add)
            return e.tensor_scalar_add(out=fm(x_fm, mc), in0=fm(x_fm, mc),
                                       scalar1=VC["ln_pre_b"][:, mc:mc + 1])
        mm_wo, tok_x1 = dense_mm(6, (lambda kc: fm(kk_bf, kc)), ev_wo, "vector",
                                 after0=[tok_og])
        dump_f(t, "d_x1", x_fm, [tok_x1])

        tk_prev = None
        for pc in range(PC):
            sq_tok = p.g("scalar", (lambda e, pc=pc: e.activation(
                out=xsq_fm[:], in_=fm(x_fm, pc), func=AF.Square)),
                after=[tok_x1 if pc == 0 else tk_prev])
            def lnf_mm(e, pc=pc):
                e.matmul(out=BK(0)[0:1, :], lhsT=ones_col[:], rhs=fm(x_fm, pc),
                         start=(pc == 0), stop=(pc == PC - 1))
                return e.matmul(out=BK(3)[0:1, :], lhsT=ones_col[:], rhs=xsq_fm[:],
                                start=(pc == 0), stop=(pc == PC - 1))
            tk_prev = p.g("tensor", lnf_mm, after=[sq_tok, tok_x1],
                          wbank=[0, 3] if pc == 0 else [])
        def lnf_rows(e):
            e.tensor_scalar_mul(out=lnmu, in0=BK(0)[0:1, :], scalar1=1.0 / D)
            e.tensor_scalar_mul(out=lniv, in0=BK(3)[0:1, :], scalar1=1.0 / D)
            e.tensor_tensor(out=s2w_f[0:1, :], in0=lnmu, in1=lnmu,
                            op=OP.mult)
            e.tensor_tensor(out=lniv, in0=lniv, in1=s2w_f[0:1, :],
                            op=OP.subtract)
            return e.tensor_scalar_add(out=lniv, in0=lniv, scalar1=1e-5)
        tok_fr = p.g("vector", lnf_rows, after=[tk_prev], rbank=[0, 3])
        tok_fq = p.g("scalar", lambda e: e.activation(out=lniv, in_=lniv,
                                                      func=AF.Sqrt, scale=1.0),
                     after=[tok_fr])
        def lnf_i(e):
            e.reciprocal(out=lniv, in_=lniv)
            e.tensor_tensor(out=s2w_f[0:1, :], in0=lnmu, in1=lniv,
                            op=OP.mult)
            return e.tensor_scalar_mul(out=s2w_f[0:1, :], in0=s2w_f[0:1, :], scalar1=-1.0)
        tok_fi = p.g("vector", lnf_i, after=[tok_fq])
        def lnf_bc(e):
            e.matmul(out=BK(1), lhsT=ones_r128[:], rhs=lniv, start=True, stop=True)
            return e.matmul(out=BK(2), lhsT=ones_r128[:], rhs=s2w_f[0:1, :],
                            start=True, stop=True)
        tok_fbc = p.g("tensor", lnf_bc, after=[tok_fi], wbank=[1, 2])
        tok_xf = None
        for pc in range(PC):
            def lnf_xf(e, pc=pc):
                e.tensor_tensor(out=bt2_f[:], in0=fm(x_fm, pc), in1=BK(1), op=OP.mult)
                e.tensor_tensor(out=bt2_f[:], in0=bt2_f[:], in1=BK(2), op=OP.add)
                return e.tensor_scalar(out=fm(xn_bf, pc), in0=bt2_f[:],
                                       scalar1=VC["ln_ffn_w"][:, pc:pc + 1],
                                       scalar2=VC["ln_ffn_b"][:, pc:pc + 1],
                                       op0=OP.mult, op1=OP.add)
            tok_xf = p.g("vector", (lambda e, pc=pc: lnf_xf(e, pc)),
                         after=[tok_fbc], rbank=[1, 2])
        dump_bf(t, "d_xf", xn_bf, [tok_xf])
        def p_kf(e):
            ins = None
            for pc in range(PC):
                e.tensor_sub(out=fm(dx_bf, pc)[:, 1:TILE], in0=fm(xn_bf, pc)[:, 0:TILE - 1],
                             in1=fm(xn_bf, pc)[:, 1:TILE])
                e.tensor_sub(out=fm(dx_bf, pc)[:, 0:1], in0=xf_halo[:, pc:pc + 1],
                             in1=fm(xn_bf, pc)[:, 0:1])
                e.tensor_copy(out=xf_halo[:, pc:pc + 1], in_=fm(xn_bf, pc)[:, TILE - 1:TILE])
                ins = e.scalar_tensor_tensor(out=fm(dx_bf, pc), in0=fm(dx_bf, pc),
                                             scalar=VC["ffn_x_k"][:, pc:pc + 1],
                                             in1=fm(xn_bf, pc), op0=OP.mult, op1=OP.add)
            return ins
        tok_kf = p.g("vector", p_kf, after=[tok_xf])

        tok_h2 = None
        for ms in range(4):
            hu0 = 8 + 2 * ms
            def ev_f1(e, mc, bk, ms=ms):
                return e.activation(out=h2c(ms * 8 + mc), in_=bk, func=AF.Relu)
            mmt, evt = dense_mm(hu0, (lambda kc: fm(dx_bf, kc)), ev_f1, "scalar",
                                after0=[tok_kf] if ms == 0 else [tok_kf, tok_h2])
            def sq_f1(e, ms=ms):
                ins = None
                for mc in range(8):
                    q = ms * 8 + mc
                    ins = e.tensor_mul(out=h2c(q), in0=h2c(q), in1=h2c(q))
                return ins
            tok_h2 = p.g("vector", sq_f1, after=[evt])
        mm_f2 = None
        for ksl in range(4):
            hu0 = 16 + 2 * ksl
            ensure_hu(t, hu0 + 1)
            def f2mm(e, ksl=ksl, hu0=hu0):
                ins = None
                for kc in range(PC):
                    for mc in range(8):
                        ins = e.matmul(out=BK(mc), lhsT=hu_sb(hu0 + mc // 4, kc, mc % 4),
                                       rhs=h2c(ksl * 8 + kc),
                                       start=(ksl == 0 and kc == 0),
                                       stop=(ksl == 3 and kc == PC - 1))
                return ins
            aft = [tok_h2, wload[(t, hu0)], wload[(t, hu0 + 1)]]
            mm_f2 = p.g("tensor", f2mm, after=aft,
                        wbank=list(range(8)) if ksl == 0 else [])
            slot_free[hu0 % 2] = mm_f2
            slot_free[(hu0 + 1) % 2] = mm_f2
        def ev_f2(e):
            ins = None
            for mc in range(8):
                ins = e.tensor_tensor(out=fm(x_fm, mc), in0=fm(x_fm, mc), in1=BK(mc),
                                      op=OP.add)
            return ins
        tok_x2 = p.g("vector", ev_f2, after=[mm_f2], rbank=list(range(8)))
        dump_f(t, "d_x2", x_fm, [tok_x2])

        tok_xo = tok_x2
        for tc in range(TC):
            buf = tmb[tc % 2]
            tok_ld = p.dma_g("sync",
                             (lambda e, t=t, tc=tc, buf=buf: [e.dma_start(
                                 out=buf[:],
                                 in_=x_in[t * TILE + tc * 128: t * TILE + (tc + 1) * 128, :])]),
                             1, after=[tmb_free[0], tmb_free[1]])
            for gi in range(2):
                bk = (tc % 2) * 2 + gi
                def tpg2(e, tc=tc, gi=gi, bk=bk, buf=buf):
                    ins = None
                    for pc4 in range(4):
                        pc = gi * 4 + pc4
                        ins = e.matmul(out=BK(bk)[:, pc4 * 128:(pc4 + 1) * 128],
                                       lhsT=buf[:, pc * 128:(pc + 1) * 128],
                                       rhs=id_f32[:], start=(pc4 == 0), stop=(pc4 == 3))
                    return ins
                tk = p.g("tensor", tpg2, after=[tok_ld, tok_x2], wbank=[bk])
                def tpe2(e, tc=tc, gi=gi, bk=bk):
                    ins = None
                    for pc4 in range(4):
                        pc = gi * 4 + pc4
                        ins = e.tensor_tensor(out=fmc(x_fm, pc, tc), in0=fmc(x_fm, pc, tc),
                                              in1=BK(bk)[:, pc4 * 128:(pc4 + 1) * 128],
                                              op=OP.add)
                    return ins
                tok_xo = p.g("vector", tpe2, after=[tk], rbank=[bk])
                tmb_free[tc % 2] = tok_xo

        tk_prev = None
        for pc in range(PC):
            sq_tok = p.g("scalar", (lambda e, pc=pc: e.activation(
                out=xsq_fm[:], in_=fm(x_fm, pc), func=AF.Square)),
                after=[tok_xo if pc == 0 else tk_prev])
            def ln3_mm(e, pc=pc):
                e.matmul(out=BK(4)[0:1, :], lhsT=ones_col[:], rhs=fm(x_fm, pc),
                         start=(pc == 0), stop=(pc == PC - 1))
                return e.matmul(out=BK(7)[0:1, :], lhsT=ones_col[:], rhs=xsq_fm[:],
                                start=(pc == 0), stop=(pc == PC - 1))
            tk_prev = p.g("tensor", ln3_mm, after=[sq_tok, tok_xo],
                          wbank=[4, 7] if pc == 0 else [])
        def ln3_rows(e):
            e.tensor_scalar_mul(out=lnmu, in0=BK(4)[0:1, :], scalar1=1.0 / D)
            e.tensor_scalar_mul(out=lniv, in0=BK(7)[0:1, :], scalar1=1.0 / D)
            e.tensor_tensor(out=s2w_f[0:1, :], in0=lnmu, in1=lnmu,
                            op=OP.mult)
            e.tensor_tensor(out=lniv, in0=lniv, in1=s2w_f[0:1, :],
                            op=OP.subtract)
            return e.tensor_scalar_add(out=lniv, in0=lniv, scalar1=1e-5)
        tok_3r = p.g("vector", ln3_rows, after=[tk_prev], rbank=[4, 7])
        tok_3q = p.g("scalar", lambda e: e.activation(out=lniv, in_=lniv,
                                                      func=AF.Sqrt, scale=1.0),
                     after=[tok_3r])
        def ln3_i(e):
            e.reciprocal(out=lniv, in_=lniv)
            e.tensor_tensor(out=s2w_f[0:1, :], in0=lnmu, in1=lniv,
                            op=OP.mult)
            return e.tensor_scalar_mul(out=s2w_f[0:1, :], in0=s2w_f[0:1, :], scalar1=-1.0)
        tok_3i = p.g("vector", ln3_i, after=[tok_3q])
        def ln3_bc(e):
            e.matmul(out=BK(5), lhsT=ones_r128[:], rhs=lniv, start=True, stop=True)
            return e.matmul(out=BK(6), lhsT=ones_r128[:], rhs=s2w_f[0:1, :],
                            start=True, stop=True)
        tok_3bc = p.g("tensor", ln3_bc, after=[tok_3i], wbank=[5, 6])
        tok_x3 = None
        for pc in range(PC):
            def ln3_ap(e, pc=pc):
                e.tensor_tensor(out=bt2_f[:], in0=fm(x_fm, pc), in1=BK(5), op=OP.mult)
                e.tensor_tensor(out=fm(x_fm, pc), in0=bt2_f[:], in1=BK(6), op=OP.add)
                return e.tensor_copy(out=fm(xn_bf, pc), in_=fm(x_fm, pc))
            tok_x3 = p.g("vector", (lambda e, pc=pc: ln3_ap(e, pc)),
                         after=[tok_3bc], rbank=[5, 6])
        dump_f(t, "d_x3", x_fm, [tok_x3])

        def gt1(e):
            ins = None
            for kc in range(PC):
                for q in range(2):
                    ins = e.matmul(out=BK(q), lhsT=wg1_sb[:, kc * 256 + q * 128:
                                                         kc * 256 + (q + 1) * 128],
                                   rhs=fm(xn_bf, kc), start=(kc == 0), stop=(kc == PC - 1))
            return ins
        tok_gt1 = p.g("tensor", gt1, after=[tok_x3], wbank=[0, 1])
        def gt1e(e):
            e.activation(out=hg_bf[:, 0:TILE], in_=BK(0), func=AF.Gelu,
                         bias=bg1c[:, 0:1])
            return e.activation(out=hg_bf[:, TILE:2 * TILE], in_=BK(1), func=AF.Gelu,
                                bias=bg1c[:, 1:2])
        tok_gt1e = p.g("scalar", gt1e, after=[tok_gt1], rbank=[0, 1])
        def gt2(e):
            e.matmul(out=BK(2)[0:40, :], lhsT=wg2_sb[:, 0:40], rhs=hg_bf[:, 0:TILE],
                     start=True, stop=False)
            return e.matmul(out=BK(2)[0:40, :], lhsT=wg2_sb[:, 40:80],
                            rhs=hg_bf[:, TILE:2 * TILE], start=False, stop=True)
        tok_gt2 = p.g("tensor", gt2, after=[tok_gt1e], wbank=[2])
        tok_lg = p.g("vector", lambda e: e.tensor_scalar_add(
            out=lgt_f[:], in0=BK(2)[0:20, :], scalar1=bg2c[0:20, :]),
                     after=[tok_gt2], rbank=[2])
        def lgT(e):
            ins = None
            for tc in range(TC):
                ins = e.matmul(out=BK(3)[:, tc * 20:(tc + 1) * 20],
                               lhsT=lgt_f[:, tc * 128:(tc + 1) * 128],
                               rhs=id_f32[0:20, 0:20], start=(tc == 0), stop=(tc == 3))
            return ins
        tok_lgT = p.g("tensor", lgT, after=[tok_lg], wbank=[3])
        tok_lgTe = p.g("vector", lambda e: e.tensor_copy(out=lgtT[:], in_=BK(3)[:, 0:TC * NE]),
                       after=[tok_lgT], rbank=[3])
        def lgv(tt):
            return tt[:].rearrange("p (c k) -> p c k", c=TC)
        def growc(i):
            return AP(grow.ap().tensor, i * TC, [[16, 128], [1, TC], [0, NE]])
        def top2(e):
            e.tensor_reduce(out=grow[:, 0:TC], in_=lgv(lgtT), axis=mybir.AxisListType.X,
                            op=OP.max)
            for cq in range(TC):
                e.tensor_scalar(out=gscr[:, cq * NE:(cq + 1) * NE],
                                in0=lgtT[:, cq * NE:(cq + 1) * NE],
                                scalar1=grow[:, cq:cq + 1], scalar2=None, op0=OP.subtract)
            return e.tensor_copy(out=gscr2[:], in_=gscr[:])
        tok_t2a = p.g("vector", top2, after=[tok_lgTe])
        tok_t2b = p.g("scalar", lambda e: e.activation(out=gscr2[:], in_=gscr2[:],
                                                       func=AF.Exp),
                      after=[tok_t2a])
        def top2c(e):
            e.tensor_reduce(out=grow[:, TC:2 * TC], in_=lgv(gscr2),
                            axis=mybir.AxisListType.X, op=OP.add)
            e.reciprocal(out=grow[:, 2 * TC:3 * TC], in_=grow[:, TC:2 * TC])
            for cq in range(TC):
                e.tensor_scalar(out=gscr[:, cq * NE:(cq + 1) * NE],
                                in0=lgtT[:, cq * NE:(cq + 1) * NE],
                                scalar1=grow[:, cq:cq + 1], scalar2=None, op0=OP.is_ge)
            e.scalar_tensor_tensor(out=lgv(gscr), in0=lgv(gscr), scalar=-1e30,
                                   in1=lgv(lgtT), op0=OP.mult, op1=OP.add)
            e.tensor_reduce(out=grow[:, 3 * TC:4 * TC], in_=lgv(gscr),
                            axis=mybir.AxisListType.X, op=OP.max)
            for cq in range(TC):
                e.tensor_scalar(out=gscr[:, cq * NE:(cq + 1) * NE],
                                in0=lgtT[:, cq * NE:(cq + 1) * NE],
                                scalar1=grow[:, 3 * TC + cq:3 * TC + cq + 1], scalar2=None, op0=OP.is_ge)
            e.tensor_tensor(out=gscr[:], in0=gscr[:], in1=gscr2[:], op=OP.mult)
            ins = None
            for cq in range(TC):
                ins = e.tensor_scalar(out=gscr2[:, cq * NE:(cq + 1) * NE],
                                      in0=gscr[:, cq * NE:(cq + 1) * NE],
                                      scalar1=grow[:, 2 * TC + cq:2 * TC + cq + 1],
                                      scalar2=None, op0=OP.mult)
            return ins
        tok_t2c = p.g("vector", top2c, after=[tok_t2b])
        def gTb(e):
            ins = None
            for tc in range(TC):
                ins = e.matmul(out=BK(4)[0:20, tc * 128:(tc + 1) * 128],
                               lhsT=gscr2[:, tc * 20:(tc + 1) * 20],
                               rhs=id_f32[:], start=(tc == 0), stop=(tc == 3))
            return ins
        tok_gTb = p.g("tensor", gTb, after=[tok_t2c], wbank=[4])
        tok_gbe = p.g("scalar", lambda e: e.copy(out=gates_bf[:], in_=BK(4)[0:20, :]),
                      after=[tok_gTb], rbank=[4])
        if debug and t == dbg_tile:
            dbg_after.append(p.dma_g("gpsimd",
                                     (lambda e: [e.dma_start(out=dbg["d_gates"][:],
                                                             in_=gates_bf[:])]),
                                     1, after=[tok_gbe]))

        tok_selmm = None
        tok_selev = None
        for ee in range(NE):
            tok_msel = p.g("vector", (lambda e, ee=ee: e.memset(sel2[ee % 2], 1.0)),
                           after=[tok_gbe] if ee < 2 else [tok_selmm])
            tok_sel = p.g("gpsimd", (lambda e, ee=ee: e.affine_select(
                out=sel2[ee % 2], in_=sel2[ee % 2], pattern=[[0, 128]],
                compare_op=OP.is_equal, fill=0.0, base=-ee, channel_multiplier=1)),
                after=[tok_msel])
            def selmm(e, ee=ee):
                return e.matmul(out=BK(ee % 2), lhsT=sel2[ee % 2],
                                rhs=gates_bf, start=True, stop=True)
            tok_selmm = p.g("tensor", selmm, after=[tok_sel, tok_gbe], wbank=[ee % 2])
            eng_ev = "vector" if ee % 2 == 0 else "scalar"
            def selev(e, ee=ee):
                if ee % 2 == 0:
                    return e.tensor_copy(out=gbc_t(ee), in_=BK(ee % 2))
                return e.copy(out=gbc_t(ee), in_=BK(ee % 2))
            tok_selev = p.g(eng_ev, selev, after=[tok_selmm], rbank=[ee % 2])

        def bemm(e):
            ins = None
            for mc in range(8):
                ins = e.matmul(out=BK(mc), lhsT=be_sb[:, mc * 128:(mc + 1) * 128],
                               rhs=gates_bf[:], start=True, stop=False)
            return ins
        tok_y0 = p.g("tensor", bemm, after=[tok_gbe], wbank=list(range(8)))

        tok_gx = None
        mm_moe = tok_y0
        for ee in range(NE):
            def gxf(e, ee=ee):
                ins = None
                for pc in range(PC):
                    ins = e.tensor_tensor(out=fm(kk_bf, pc), in0=fm(xn_bf, pc),
                                          in1=gbc_t(ee), op=OP.mult)
                return ins
            tok_gx = p.g("vector", gxf, after=[tok_selev, mm_moe])
            hu0 = 24 + 2 * ee
            ensure_hu(t, hu0 + 1)
            def moemm(e, ee=ee, hu0=hu0):
                ins = None
                for kc in range(PC):
                    for mc in range(8):
                        ins = e.matmul(out=BK(mc), lhsT=hu_sb(hu0 + mc // 4, kc, mc % 4),
                                       rhs=fm(kk_bf, kc), start=False,
                                       stop=(ee == NE - 1 and kc == PC - 1))
                return ins
            mm_moe = p.g("tensor", moemm,
                         after=[tok_gx, wload[(t, hu0)], wload[(t, hu0 + 1)]])
            slot_free[hu0 % 2] = mm_moe
            slot_free[(hu0 + 1) % 2] = mm_moe
        def ev_moe(e):
            ins = None
            for mc in range(8):
                ins = e.tensor_tensor(out=fm(x_fm, mc), in0=fm(x_fm, mc), in1=BK(mc),
                                      op=OP.add)
            return ins
        tok_x4 = p.g("vector", ev_moe, after=[mm_moe], rbank=list(range(8)))

        store_toks = []
        for tc in range(TC):
            buf = tmb[tc % 2]
            for gi in range(2):
                bk = (tc % 2) * 2 + gi
                def tpo(e, tc=tc, gi=gi, bk=bk):
                    ins = None
                    for pc4 in range(4):
                        pc = gi * 4 + pc4
                        ins = e.matmul(out=BK(bk)[:, pc4 * 128:(pc4 + 1) * 128],
                                       lhsT=fmc(x_fm, pc, tc), rhs=id_f32[:],
                                       start=(pc4 == 0), stop=(pc4 == 3))
                    return ins
                tk = p.g("tensor", tpo, after=[tok_x4, tmb_free[0], tmb_free[1]], wbank=[bk])
                def tpoe(e, tc=tc, gi=gi, bk=bk, buf=buf):
                    ins = None
                    for pc4 in range(4):
                        pc = gi * 4 + pc4
                        ins = e.tensor_copy(out=buf[:, pc * 128:(pc + 1) * 128],
                                            in_=BK(bk)[:, pc4 * 128:(pc4 + 1) * 128])
                    return ins
                tok_tpe = p.g("vector", tpoe, after=[tk], rbank=[bk])
            def p8s(e, tc=tc, buf=buf):
                for hh in range(2):
                    e.activation(out=xsq_fm[:], in_=buf[:, hh * 512:(hh + 1) * 512],
                                 func=AF.Copy, accum_out=stat_h[:, 4 * tc + hh:4 * tc + hh + 1])
                    ins = e.activation(out=xsq_fm[:], in_=buf[:, hh * 512:(hh + 1) * 512],
                                       func=AF.Square,
                                       accum_out=stat_h[:, 4 * tc + 2 + hh:4 * tc + 3 + hh])
                return ins
            tok_s8 = p.g("scalar", p8s, after=[tok_tpe])
            def p8r(e, tc=tc):
                e.tensor_add(out=stat_s[:, tc:tc + 1], in0=stat_h[:, 4 * tc:4 * tc + 1],
                             in1=stat_h[:, 4 * tc + 1:4 * tc + 2])
                e.tensor_add(out=stat_q[:, tc:tc + 1], in0=stat_h[:, 4 * tc + 2:4 * tc + 3],
                             in1=stat_h[:, 4 * tc + 3:4 * tc + 4])
                e.tensor_scalar_mul(out=stat_m[:, tc:tc + 1], in0=stat_s[:, tc:tc + 1],
                                    scalar1=1.0 / D)
                e.tensor_scalar_mul(out=stat_q[:, tc:tc + 1], in0=stat_q[:, tc:tc + 1],
                                    scalar1=1.0 / D)
                e.tensor_tensor(out=stat_i[:, tc:tc + 1], in0=stat_m[:, tc:tc + 1],
                                in1=stat_m[:, tc:tc + 1], op=OP.mult)
                e.tensor_tensor(out=stat_q[:, tc:tc + 1], in0=stat_q[:, tc:tc + 1],
                                in1=stat_i[:, tc:tc + 1], op=OP.subtract)
                return e.tensor_scalar_add(out=stat_q[:, tc:tc + 1],
                                           in0=stat_q[:, tc:tc + 1], scalar1=1e-5)
            tok_r8 = p.g("vector", p8r, after=[tok_s8])
            tok_q8 = p.g("scalar", (lambda e, tc=tc: e.activation(
                out=stat_q[:, tc:tc + 1], in_=stat_q[:, tc:tc + 1], func=AF.Sqrt,
                scale=1.0)), after=[tok_r8])
            def p8a(e, tc=tc, buf=buf):
                e.reciprocal(out=stat_i[:, tc:tc + 1], in_=stat_q[:, tc:tc + 1])
                return e.tensor_scalar(out=buf[:], in0=buf[:],
                                       scalar1=stat_m[:, tc:tc + 1],
                                       scalar2=stat_i[:, tc:tc + 1],
                                       op0=OP.subtract, op1=OP.mult)
            tok_a8 = p.g("vector", p8a, after=[tok_q8])
            tok_st8 = p.dma_g("sync",
                              (lambda e, t=t, tc=tc, buf=buf: [e.dma_start(
                                  out=y_out[t * TILE + tc * 128: t * TILE + (tc + 1) * 128, :],
                                  in_=buf[:])]), 1, after=[tok_a8])
            store_toks.append(tok_st8)
            tmb_free[tc % 2] = tok_st8
        tok_store_last = store_toks[-1]

    p.g("vector", lambda e: e.tensor_copy(out=pp_bf[0:1, 0:1], in_=pp_bf[0:1, 0:1]),
        after=[tok_store_last] + list(dbg_after))

    cut = int(os.environ.get("DK_CUT", str(p.seq + 10)))
    if os.environ.get("DK_PRINT_SEQ"):
        print("TOTAL_SEQ", p.seq)
    with nc.Block() as block:
        class VecGuard:
            def __init__(self, eng, sp):
                self.e = eng
                self.sp = sp
                self.prev_out = set()
                self.prev_in = set()

            @staticmethod
            def _names(v):
                if isinstance(v, bass.AP):
                    return {v.name}
                return set()

            def wait_ge(self, *a, **k):
                return self.e.wait_ge(*a, **k)

            def __getattr__(self, nm):
                real = getattr(self.e, nm)
                if not callable(real):
                    return real
                def wrap(*args, **kw):
                    outs, ins = set(), set()
                    if nm == "memset" and args:
                        outs |= self._names(args[0])
                        rest = args[1:]
                    else:
                        rest = args
                    for k, v in kw.items():
                        if k in ("out", "accum_out"):
                            outs |= self._names(v)
                        else:
                            ins |= self._names(v)
                    for v in rest:
                        ins |= self._names(v)
                    if (self.prev_out & (outs | ins)) or (self.prev_in & outs):
                        self.e.tensor_copy(out=self.sp, in_=self.sp)
                    self.prev_out, self.prev_in = outs, ins
                    return real(*args, **kw)
                return wrap

        def emit(ename):
            def run(eng):
                if ename == "vector":
                    eng = VecGuard(eng, vspace[:, 0:1])
                for (waits, fn, n_dma, seq) in p.groups[ename]:
                    if seq >= cut:
                        break
                    for s, v in waits.items():
                        if s.startswith("dma_"):
                            eng.wait_ge(dsem[s[4:]], v)
                        else:
                            eng.wait_ge(sem[s], v)
                    res = fn(eng)
                    if n_dma is not None:
                        assert len(res) == n_dma, \
                            f"{ename} dma group: declared {n_dma}, issued {len(res)}"
                        for ins in res:
                            ins.then_inc(dsem[ename], 16)
                    else:
                        res.then_inc(sem[ename], 1)
            return run
        block.sync(emit("sync"))
        block.scalar(emit("scalar"))
        block.vector(emit("vector"))
        block.tensor(emit("tensor"))
        block.gpsimd(emit("gpsimd"))

    es.close()
    return nc



def kernel(**inputs):
    x = np.ascontiguousarray(np.asarray(inputs["L0_attn_x"], dtype=np.float32))
    nc = build_nc(debug=False)
    base = {}
    ren = {"W_ffn1": "L0_attn_W_ffn1", "W_ffn2": "L0_attn_W_ffn2",
           "wg1": "L0_ffn_wg1", "bg1": "L0_ffn_bg1", "wg2": "L0_ffn_wg2",
           "bg2": "L0_ffn_bg2", "We": "L0_ffn_We", "be": "L0_ffn_be"}
    names = (WN1D + ["r_k", "w1", "w2", "a1", "a2", "g1", "g2",
                     "W_r", "W_k", "W_v", "W_o", "W_ffn1", "W_ffn2",
                     "wg1", "bg1", "wg2", "bg2", "We", "be"])
    for nm in names:
        key = ren.get(nm, "L0_attn_" + nm)
        if key not in inputs:
            key = "L0_ffn_" + nm
        base[nm] = np.ascontiguousarray(np.asarray(inputs[key], dtype=np.float32))
    in_maps = []
    for b in range(8):
        m = dict(base)
        m["x"] = x[b]
        in_maps.append(m)
    res = run_bass_kernel_spmd(nc, in_maps, core_ids=list(range(8)))
    return np.stack([res.results[b]["out"] for b in range(8)], axis=0)


# version 07
# speedup vs baseline: 1.0122x; 1.0073x over previous
import contextlib
import numpy as np
import concourse.bass as bass
import concourse.mybir as mybir
from concourse.bass_utils import run_bass_kernel_spmd

F32 = mybir.dt.float32
BF16 = mybir.dt.bfloat16
AF = mybir.ActivationFunctionType
OP = mybir.AluOpType
AP = bass.AP

import os
TOK, D, TILE = 4096, 1024, 512
NT, PC, TC = TOK // TILE, D // 128, TILE // 128
NT = int(os.environ.get("DK_NT", NT))
H, N, DFF, NE = 16, 64, 4096, 20
SC = float(np.exp(-0.5))
ENGS = ("sync", "scalar", "vector", "tensor", "gpsimd")

WN1D = ["ln_pre_w", "ln_pre_b", "ln_attn_w", "ln_attn_b", "ln_ffn_w", "ln_ffn_b",
        "x_r", "x_w", "x_k", "x_v", "x_a", "x_g", "w0", "a0", "k_k", "k_a",
        "gn_w", "gn_b", "ffn_x_k"]


class Plan:

    def __init__(self):
        self.groups = {e: [] for e in ENGS}
        self.prog = {e: 0 for e in ENGS}
        self.dmac = {e: 0 for e in ENGS}
        self.seq = 0
        self.bank_w = [None] * 8
        self.bank_r = [None] * 8

    def _mk(self, eng, fn, after, is_dma, n_dma=0):
        waits = {}
        for tokn in after:
            if tokn is None:
                continue
            s, v = tokn
            if s == eng and eng != "gpsimd":
                continue
            waits[s] = max(waits.get(s, 0), v)
        import traceback
        if os.environ.get("DK_TRACE_SEQ") and self.seq == int(os.environ["DK_TRACE_SEQ"]):
            print("SEQ", self.seq, "eng", eng, "dma", is_dma)
            traceback.print_stack()
        self.groups[eng].append((waits, fn, n_dma if is_dma else None, self.seq))
        self.seq += 1
        if is_dma:
            self.dmac[eng] += 16 * n_dma
            return ("dma_" + eng, self.dmac[eng])
        self.prog[eng] += 1
        return (eng, self.prog[eng])

    def g(self, eng, fn, after=(), wbank=(), rbank=()):
        after = list(after)
        for b in wbank:
            after.append(self.bank_r[b])
            after.append(self.bank_w[b])
        for b in rbank:
            after.append(self.bank_w[b])
        tok = self._mk(eng, fn, after, False)
        for b in wbank:
            self.bank_w[b] = tok
        for b in rbank:
            self.bank_r[b] = tok
        return tok

    def dma_g(self, eng, fn, n_dma, after=()):
        return self._mk(eng, fn, list(after), True, n_dma)


def build_nc(debug=False, dbg_tile=NT - 1):
    nc = bass.Bass("TRN2", target_bir_lowering=False, debug=False,
                   detect_race_conditions=False)

    x_in = nc.dram_tensor("x", [TOK, D], F32, kind="ExternalInput").ap()
    y_out = nc.dram_tensor("out", [TOK, D], F32, kind="ExternalOutput").ap()
    dr = {}
    for nm in WN1D:
        dr[nm] = nc.dram_tensor(nm, [D], F32, kind="ExternalInput").ap()
    for nm, sh in [("r_k", [H, N]), ("w1", [D, 64]), ("w2", [64, D]),
                   ("a1", [D, 64]), ("a2", [64, D]), ("g1", [D, 128]),
                   ("g2", [128, D]), ("W_r", [D, D]), ("W_k", [D, D]),
                   ("W_v", [D, D]), ("W_o", [D, D]), ("W_ffn1", [D, DFF]),
                   ("W_ffn2", [DFF, D]), ("wg1", [D, 256]), ("bg1", [256]),
                   ("wg2", [256, 40]), ("bg2", [40]), ("We", [NE, D, D]),
                   ("be", [NE, D])]:
        dr[nm] = nc.dram_tensor(nm, sh, F32, kind="ExternalInput").ap()

    BIGW = ([("W_r", 0), ("W_k", 0), ("W_v", 0), ("W_o", 0)]
            + [("ffn1", i) for i in range(4)] + [("ffn2", i) for i in range(4)]
            + [("We", e) for e in range(NE)])
    NU = len(BIGW)
    wcache = nc.dram_tensor("wcache", [128, 2 * NU * 4096], BF16, kind="Internal").ap()

    dbg = {}
    if debug:
        for nm in ["d_u1", "d_xn", "d_r", "d_e", "d_kk", "d_kmod", "d_v", "d_b",
                   "d_g", "d_o", "d_x1", "d_xf", "d_x2", "d_x3"]:
            dbg[nm] = nc.dram_tensor(nm, [128, PC * TILE], F32, kind="ExternalOutput").ap()
        dbg["d_gates"] = nc.dram_tensor("d_gates", [20, TILE], F32, kind="ExternalOutput").ap()

    es = contextlib.ExitStack()
    E = es.enter_context

    id_bf = E(nc.sbuf_tensor("id_bf", [128, 128], BF16))
    id_f32 = E(nc.sbuf_tensor("id_f32", [128, 128], F32))
    msk_su = E(nc.sbuf_tensor("msk_su", [128, 128], BF16))
    msk_iu = E(nc.sbuf_tensor("msk_iu", [128, 128], BF16))
    msk_sl = E(nc.sbuf_tensor("msk_sl", [128, 128], BF16))
    rscan = E(nc.sbuf_tensor("rscan", [128, TILE], BF16))
    seg16b = E(nc.sbuf_tensor("seg16b", [128, PC * 16], BF16))
    bc16 = E(nc.sbuf_tensor("bc16", [16, PC * 128], F32))
    ones_col = E(nc.sbuf_tensor("ones_col", [128, 1], F32))
    cst_eps = E(nc.sbuf_tensor("cst_eps", [128, 2], F32))
    ones_r128 = E(nc.sbuf_tensor("ones_r128", [1, 128], F32))

    VC = {nm: E(nc.sbuf_tensor("c_" + nm, [128, PC], F32)) for nm in WN1D}
    VC["r_k"] = E(nc.sbuf_tensor("c_r_k", [128, PC], F32))
    VC["omka"] = E(nc.sbuf_tensor("c_omka", [128, PC], F32))
    w1w2c = E(nc.sbuf_tensor("w1w2c", [128, PC], F32))
    w2b1c = E(nc.sbuf_tensor("w2b1c", [128, PC], F32))
    bg1c = E(nc.sbuf_tensor("bg1c", [128, 2], F32))
    bg2c = E(nc.sbuf_tensor("bg2c", [40, 1], F32))
    w1sq = E(nc.sbuf_tensor("w1sq", [128, PC], F32))
    w1b1 = E(nc.sbuf_tensor("w1b1", [128, PC], F32))
    sb1 = E(nc.sbuf_tensor("sb1", [1, 2], F32))
    scr_red = E(nc.sbuf_tensor("scr_red", [128, 2], F32))

    w1_sb = E(nc.sbuf_tensor("w1_sb", [128, PC * 64], BF16))
    a1_sb = E(nc.sbuf_tensor("a1_sb", [128, PC * 64], BF16))
    w2_sb = E(nc.sbuf_tensor("w2_sb", [64, D], BF16))
    a2_sb = E(nc.sbuf_tensor("a2_sb", [64, D], BF16))
    g1_sb = E(nc.sbuf_tensor("g1_sb", [128, PC * 128], BF16))
    g2_sb = E(nc.sbuf_tensor("g2_sb", [128, D], BF16))
    wg1_sb = E(nc.sbuf_tensor("wg1_sb", [128, PC * 256], BF16))
    wg2_sb = E(nc.sbuf_tensor("wg2_sb", [128, 2 * 40], BF16))
    be_sb = E(nc.sbuf_tensor("be_sb", [NE, D], BF16))

    sT = E(nc.sbuf_tensor("sT", [128, PC * N], F32))
    sT_bf = E(nc.sbuf_tensor("sT_bf", [128, PC * N], BF16))
    xn_halo = E(nc.sbuf_tensor("xn_halo", [128, PC], BF16))
    xf_halo = E(nc.sbuf_tensor("xf_halo", [128, PC], BF16))

    NSLOT = 2
    wslot = [E(nc.sbuf_tensor(f"wslot{i}", [128, 4096], BF16)) for i in range(NSLOT)]

    tmb0 = E(nc.sbuf_tensor("tmb0", [128, D], F32))
    tmb = [tmb0, tmb0]

    x_fm = E(nc.sbuf_tensor("x_fm", [128, PC * TILE], F32))
    xsq_fm = E(nc.sbuf_tensor("xsq_fm", [128, TILE], F32))
    s2w_f = E(nc.sbuf_tensor("s2w_f", [128, TILE], F32))
    bt2_f = E(nc.sbuf_tensor("bt2_f", [128, TILE], F32))
    pass

    lniv = bt2_f[0:1, 0:TILE]
    xn_bf = E(nc.sbuf_tensor("xn_bf", [128, PC * TILE], BF16))
    dx_bf = E(nc.sbuf_tensor("dx_bf", [128, PC * TILE], BF16))
    e_bf = E(nc.sbuf_tensor("e_bf", [128, PC * TILE], BF16))
    b_bf = E(nc.sbuf_tensor("b_bf", [128, PC * TILE], BF16))
    r_bf = E(nc.sbuf_tensor("r_bf", [128, PC * TILE], BF16))
    kk_bf = E(nc.sbuf_tensor("kk_bf", [128, PC * TILE], BF16))
    km_bf = E(nc.sbuf_tensor("km_bf", [128, PC * TILE], BF16))
    v_bf = E(nc.sbuf_tensor("v_bf", [128, PC * TILE], BF16))
    g_bf = E(nc.sbuf_tensor("g_bf", [128, PC * TILE], BF16))
    o_bf = E(nc.sbuf_tensor("o_bf", [128, PC * TILE], BF16))
    mixp = [E(nc.sbuf_tensor(f"mixp{i}", [128, TILE], BF16)) for i in range(2)]
    t1_bf = E(nc.sbuf_tensor("t1_bf", [64, TILE], BF16))
    sg_bf = E(nc.sbuf_tensor("sg_bf", [128, TILE], BF16))
    pp_bf = E(nc.sbuf_tensor("pp_bf", [128, TILE], BF16))

    ARENA_COLS = 18944
    arena = E(nc.sbuf_tensor("arena", [128, ARENA_COLS], BF16))
    _ac = [0]
    def carve(n):
        a0 = _ac[0]; _ac[0] += n
        assert _ac[0] <= ARENA_COLS
        return arena[:, a0:a0 + n]
    TIL = {nm: carve(4 * TILE) for nm in ("at", "bt", "kt", "rt")}
    E2G = carve(4 * TILE)
    X_sb = carve(8 * 128)
    XT_sb = carve(8 * 128)
    Hm_sb = carve(8 * 128)
    M1_sb = carve(8 * 128)
    M2_sb = carve(8 * 128)
    u_sb = carve(8 * 64)
    vtm_sb = carve(8 * 64)
    rhs_sb = carve(4 * 128)
    btm_sb = carve(8 * 64)
    ktm_sb = carve(8 * 64)
    bbkb = carve(2 * 4 * 128)
    def h2c(q):
        return arena[:, q * 512:(q + 1) * 512]

    cs_f = E(nc.sbuf_tensor("cs_f", [128, TILE], F32))
    cse_f = E(nc.sbuf_tensor("cse_f", [128, TILE], F32))
    csL_f = E(nc.sbuf_tensor("csL_f", [128, TC], F32))
    DLG = E(nc.sbuf_tensor("DLG", [128, 4 * TC], F32))
    dec = [E(nc.sbuf_tensor(f"dec{i}", [128, TILE], BF16)) for i in range(2)]

    hrow = E(nc.sbuf_tensor("hrow", [16, TILE], F32))
    hrow2 = E(nc.sbuf_tensor("hrow2", [16, TILE], F32))
    hrow3 = E(nc.sbuf_tensor("hrow3", [16, TILE], F32))
    rows20 = E(nc.sbuf_tensor("rows20", [20, TILE + TILE], F32))
    lgt_f = rows20[0:20, 0:TILE]
    lnmu = rows20[0:1, TILE:2 * TILE]

    hg_bf = arena[:, 0:2 * TILE]
    lgtT = arena[:, 5 * TILE:5 * TILE + 160].bitcast(F32)
    gscr = arena[:, 5 * TILE + 160:5 * TILE + 320].bitcast(F32)
    gscr2 = arena[:, 5 * TILE + 320:5 * TILE + 480].bitcast(F32)
    grow = E(nc.sbuf_tensor("grow", [128, 16], F32))
    vspace = E(nc.sbuf_tensor("vspace", [128, 1], F32))
    gates_bf = arena[0:20, 2 * TILE:3 * TILE]
    sel2 = [arena[0:20, 15 * TILE + i * 128:15 * TILE + (i + 1) * 128] for i in range(2)]
    GBC0 = 16 * TILE
    def gbc_t(ee):
        return arena[:, GBC0 + ee * TILE:GBC0 + (ee + 1) * TILE]

    stat_s = E(nc.sbuf_tensor("stat_s", [128, TC], F32))
    stat_q = E(nc.sbuf_tensor("stat_q", [128, TC], F32))
    stat_h = E(nc.sbuf_tensor("stat_h", [128, 4 * TC], F32))
    stat_m = E(nc.sbuf_tensor("stat_m", [128, TC], F32))
    stat_i = E(nc.sbuf_tensor("stat_i", [128, TC], F32))

    PSA = E(nc.psum_tensor("PSA", [128, 1024], F32))
    PSB = E(nc.psum_tensor("PSB", [128, 1024], F32))
    PSC = E(nc.psum_tensor("PSC", [128, 1024], F32))
    PSD = E(nc.psum_tensor("PSD", [128, 1024], F32))
    PST = [PSA, PSB, PSC, PSD]
    def BK(i):
        return PST[i // 2][:, (i % 2) * 512:(i % 2) * 512 + 512]

    sem = {e: E(nc.semaphore("sem_" + e)) for e in ENGS}
    dsem = {e: E(nc.semaphore("dsem_" + e)) for e in ENGS}


    p = Plan()

    def fm(tt, pc):
        return tt[:, pc * TILE:(pc + 1) * TILE]

    def fmc(tt, pc, c):
        return tt[:, pc * TILE + c * 128: pc * TILE + (c + 1) * 128]

    def mask8(m):
        return AP(m.ap().tensor, 0, [[128, 128], [0, 8], [1, 128]])

    def h8(tt):
        return tt.rearrange("p (h j) -> p h j", h=8)

    def ps_h8(pst):
        return pst[:].rearrange("p (h j) -> p h j", h=8)

    def init_memsets(e):
        e.memset(id_bf[:], 1.0)
        e.memset(id_f32[:], 1.0)
        e.memset(msk_su[:], 1.0)
        e.memset(msk_iu[:], 1.0)
        e.memset(msk_sl[:], 1.0)
        e.memset(rscan[:], 1.0)
        for c in range(TC):
            e.memset(rscan[:, c * 128:c * 128 + 1], 0.0)
        e.memset(seg16b[:], 1.0)
        e.memset(bc16[:], 1.0)
        e.memset(ones_col[:], 1.0)
        e.memset(cst_eps[:, 0:1], 1e-5)
        e.memset(cst_eps[:, 1:2], 0.00064)
        e.memset(ones_r128[:], 1.0)
        e.memset(sT[:], 0.0)
        e.memset(sT_bf[:], 0.0)
        e.memset(xn_halo[:], 0.0)
        return e.memset(xf_halo[:], 0.0)

    tok_ms = p.g("vector", init_memsets)
    _asel = [
        (id_bf, OP.is_equal, 1, -1), (id_f32, OP.is_equal, 1, -1),
        (msk_su, OP.is_gt, 1, -1), (msk_iu, OP.is_ge, 1, -1),
        (msk_sl, OP.is_gt, -1, 1),
    ]
    for _t, _op, _st, _cm in _asel:
        p.g("gpsimd", (lambda e, _t=_t, _op=_op, _st=_st, _cm=_cm: e.affine_select(
            out=_t[:], in_=_t[:], pattern=[[_st, 128]], compare_op=_op,
            fill=0.0, base=0, channel_multiplier=_cm)), after=[tok_ms])
    p.g("gpsimd", (lambda e: e.affine_select(
        out=bc16[:].rearrange("r (b o) -> r b o", o=64),
        in_=bc16[:].rearrange("r (b o) -> r b o", o=64),
        pattern=[[1, 16], [0, 64]], compare_op=OP.is_equal,
        fill=0.0, base=0, channel_multiplier=-1)), after=[tok_ms])
    p.g("gpsimd", (lambda e: e.affine_select(
        out=seg16b[0:64, :].rearrange("p (c j) -> p c j", j=16),
        in_=seg16b[0:64, :].rearrange("p (c j) -> p c j", j=16),
        pattern=[[-2, PC], [1, 16]], compare_op=OP.is_equal,
        fill=0.0, base=0, channel_multiplier=0)), after=[tok_ms])
    p.g("gpsimd", (lambda e: e.affine_select(
        out=seg16b[64:128, :].rearrange("p (c j) -> p c j", j=16),
        in_=seg16b[64:128, :].rearrange("p (c j) -> p c j", j=16),
        pattern=[[-2, PC], [1, 16]], compare_op=OP.is_equal,
        fill=0.0, base=-1, channel_multiplier=0)), after=[tok_ms])
    tok_c0 = ("gpsimd", p.prog["gpsimd"])

    VSTAGE_LIST = WN1D + ["r_k", "bg1"]
    vstage = x_fm[0:8, 0:len(VSTAGE_LIST) * 128]

    def load_smalls(e):
        ins = []
        for vi, nm in enumerate(VSTAGE_LIST):
            if nm == "r_k":
                src_ = dr["r_k"].rearrange("h n -> (h n)").rearrange("(c p) -> c p", p=128)
            elif nm == "bg1":
                src_ = dr["bg1"].rearrange("(c p) -> c p", p=128)
            else:
                src_ = dr[nm].rearrange("(c p) -> c p", p=128)
            rows = src_.shape[0]
            ins.append(e.dma_start(out=vstage[0:rows, vi * 128:(vi + 1) * 128], in_=src_))
        ins.append(e.dma_start(out=bg2c[:], in_=AP(dr["bg2"].tensor, 0, [[1, 40], [1, 1]])))
        ins.append(e.dma_start(out=w1_sb[:].rearrange("p (c m) -> p c m", m=64),
                   in_=dr["w1"].rearrange("(c p) m -> p c m", p=128)))
        ins.append(e.dma_start(out=a1_sb[:].rearrange("p (c m) -> p c m", m=64),
                   in_=dr["a1"].rearrange("(c p) m -> p c m", p=128)))
        ins.append(e.dma_start(out=w2_sb[:], in_=dr["w2"]))
        ins.append(e.dma_start(out=a2_sb[:], in_=dr["a2"]))
        ins.append(e.dma_start(out=g1_sb[:].rearrange("p (c m) -> p c m", m=128),
                   in_=dr["g1"].rearrange("(c p) m -> p c m", p=128)))
        ins.append(e.dma_start(out=g2_sb[:], in_=dr["g2"]))
        ins.append(e.dma_start(out=wg1_sb[:].rearrange("p (c m) -> p c m", m=256),
                   in_=dr["wg1"].rearrange("(c p) m -> p c m", p=128)))
        ins.append(e.dma_start(out=wg2_sb[:].rearrange("p (c m) -> p c m", m=40),
                   in_=dr["wg2"].rearrange("(c p) m -> p c m", p=128)))
        ins.append(e.dma_start(out=be_sb[:], in_=dr["be"]))
        return ins

    N_SMALL = len(WN1D) + 2 + 1 + 9
    tok_sm = p.dma_g("gpsimd", load_smalls, N_SMALL)

    def vtr_mm(e):
        ins = None
        for vi, nm in enumerate(VSTAGE_LIST):
            rows = 2 if nm == "bg1" else 8
            ins = e.matmul(out=BK(0)[:, vi * 8: vi * 8 + rows],
                           lhsT=vstage[0:rows, vi * 128:(vi + 1) * 128],
                           rhs=id_f32[0:rows, 0:rows],
                           start=(vi == 0), stop=(vi == len(VSTAGE_LIST) - 1))
        return ins
    tok_vt0 = p.g("tensor", vtr_mm, after=[tok_sm, tok_c0], wbank=[0])

    def vtr_ev(e):
        ins = None
        for vi, nm in enumerate(VSTAGE_LIST):
            if nm == "bg1":
                ins = e.tensor_copy(out=bg1c[:], in_=BK(0)[:, vi * 8: vi * 8 + 2])
            else:
                dst = VC["r_k"] if nm == "r_k" else VC[nm]
                ins = e.tensor_copy(out=dst[:], in_=BK(0)[:, vi * 8: vi * 8 + 8])
        return ins
    tok_vt0e = p.g("vector", vtr_ev, after=[tok_vt0], rbank=[0])

    def derive(e):
        e.tensor_scalar(out=VC["omka"][:], in0=VC["k_a"][:], scalar1=-1.0,
                        scalar2=1.0, op0=OP.mult, op1=OP.add)
        e.tensor_mul(out=w1w2c[:], in0=VC["ln_pre_w"][:], in1=VC["ln_attn_w"][:])
        e.tensor_mul(out=w2b1c[:], in0=VC["ln_pre_b"][:], in1=VC["ln_attn_w"][:])
        e.tensor_mul(out=w1sq[:], in0=VC["ln_pre_w"][:], in1=VC["ln_pre_w"][:])
        e.tensor_mul(out=w1b1[:], in0=VC["ln_pre_w"][:], in1=VC["ln_pre_b"][:])
        e.reduce_sum(out=scr_red[:, 0:1], in_=VC["ln_pre_b"][:], axis=mybir.AxisListType.X)
        e.tensor_tensor(out=gscr[:, 0:PC], in0=VC["ln_pre_b"][:], in1=VC["ln_pre_b"][:],
                        op=OP.mult)
        return e.reduce_sum(out=scr_red[:, 1:2], in_=gscr[:, 0:PC], axis=mybir.AxisListType.X)
    tok_dv = p.g("vector", derive, after=[tok_vt0e])
    tok_sb1 = p.g("tensor", lambda e: e.matmul(out=BK(0)[0:1, 0:2], lhsT=ones_col[:],
                                               rhs=scr_red[:], start=True, stop=True),
                  after=[tok_dv], wbank=[0])
    tok_sb1e = p.g("vector", lambda e: e.tensor_copy(out=sb1[:], in_=BK(0)[0:1, 0:2]),
                   after=[tok_sb1], rbank=[0])

    def src_ap(u, hh):
        nm, idx = BIGW[u]
        if nm == "ffn1":
            base = dr["W_ffn1"][:, idx * D + hh * 512: idx * D + (hh + 1) * 512]
        elif nm == "ffn2":
            base = dr["W_ffn2"][idx * D:(idx + 1) * D, hh * 512:(hh + 1) * 512]
        elif nm == "We":
            base = dr["We"][idx][:, hh * 512:(hh + 1) * 512]
        else:
            base = dr[nm][:, hh * 512:(hh + 1) * 512]
        return base.rearrange("(c p) m -> p c m", p=128)

    wload = {}
    slot_free = [None] * 2
    wb_done = {}
    st_hu = [0]

    def hu_sb(hu, kc, mc):
        return wslot[hu % 2][:, kc * 512 + mc * 128: kc * 512 + (mc + 1) * 128]

    def ensure_hu(t, hu):
        while st_hu[0] <= hu:
            cur = st_hu[0]
            s = cur % 2
            aft = [slot_free[s]]
            if t == 0:
                aft += [wb_done.get(cur - 2), tok_sm]
                tokn = p.dma_g("gpsimd",
                               (lambda e, cur=cur, s=s: [e.dma_start(
                                   out=wslot[s][:].rearrange("p (c m) -> p c m", m=512),
                                   in_=src_ap(cur // 2, cur % 2))]), 1, after=aft)
                wload[(t, cur)] = tokn
                wb_done[cur] = p.dma_g("sync",
                                       (lambda e, cur=cur, s=s: [e.dma_start(
                                           out=wcache[:, cur * 4096:(cur + 1) * 4096],
                                           in_=wslot[s][:])]), 1, after=[tokn])
            else:
                q = "sync" if cur % 2 == 0 else "scalar"
                wload[(t, cur)] = p.dma_g(q,
                                          (lambda e, cur=cur, s=s: [e.dma_start(
                                              out=wslot[s][:],
                                              in_=wcache[:, cur * 4096:(cur + 1) * 4096])]),
                                          1, after=aft)
            st_hu[0] += 1

    dbg_after = []

    def dump_bf(t, name, bft, after):
        if debug and t == dbg_tile and name in dbg:
            if os.environ.get("DK_PRINT_SEQ"):
                print("DUMP", name, "seq", p.seq)
            dbg_after.append(p.dma_g("gpsimd",
                                     (lambda e: [e.dma_start(out=dbg[name][:], in_=bft[:])]),
                                     1, after=after))

    def dump_f(t, name, ft, after):
        if debug and t == dbg_tile and name in dbg:
            if os.environ.get("DK_PRINT_SEQ"):
                print("DUMP", name, "seq", p.seq)
            dbg_after.append(p.dma_g("gpsimd",
                                     (lambda e: [e.dma_start(out=dbg[name][:], in_=ft[:])]),
                                     1, after=after))

    tok_store_last = None
    tmb_free = [None, None]

    for t in range(NT):
        st_hu[0] = 0
        ensure_hu(t, 1)

        tok_u1 = None
        for tc in range(TC):
            buf = tmb[tc % 2]
            tok_ld = p.dma_g("sync",
                             (lambda e, t=t, tc=tc, buf=buf: [e.dma_start(
                                 out=buf[:],
                                 in_=x_in[t * TILE + tc * 128: t * TILE + (tc + 1) * 128, :])]),
                             1, after=[tmb_free[0], tmb_free[1]])

            if debug and t == dbg_tile and tc == 0 and os.environ.get("DK_DUMP_BUF"):
                if os.environ.get("DK_PRINT_SEQ"):
                    print("DUMP_BUF seq", p.seq)
                dbg_after.append(p.dma_g("gpsimd",
                                         (lambda e, buf=buf: [e.dma_start(
                                             out=dbg["d_u1"][:, 0:1024], in_=buf[:])]),
                                         1, after=[tok_ld]))

            def p0s(e, tc=tc, buf=buf):
                ins = None
                for hh in range(2):
                    e.activation(out=xsq_fm[:], in_=buf[:, hh * 512:(hh + 1) * 512],
                                 func=AF.Copy, accum_out=stat_h[:, hh:hh + 1])
                    ins = e.activation(out=xsq_fm[:], in_=buf[:, hh * 512:(hh + 1) * 512],
                                       func=AF.Square, accum_out=stat_h[:, 2 + hh:3 + hh])
                return ins
            tok_s = p.g("scalar", p0s, after=[tok_ld, tok_c0])

            def p0r(e, tc=tc):
                e.tensor_add(out=stat_s[:, 0:1], in0=stat_h[:, 0:1], in1=stat_h[:, 1:2])
                e.tensor_add(out=stat_q[:, 0:1], in0=stat_h[:, 2:3], in1=stat_h[:, 3:4])
                e.tensor_scalar_mul(out=stat_m[:, 0:1], in0=stat_s[:, 0:1], scalar1=1.0 / D)
                e.tensor_scalar_mul(out=stat_q[:, 0:1], in0=stat_q[:, 0:1], scalar1=1.0 / D)
                e.tensor_tensor(out=stat_i[:, 0:1], in0=stat_m[:, 0:1], in1=stat_m[:, 0:1],
                                op=OP.mult)
                e.tensor_tensor(out=stat_q[:, 0:1], in0=stat_q[:, 0:1],
                                in1=stat_i[:, 0:1], op=OP.subtract)
                return e.tensor_scalar_add(out=stat_q[:, 0:1], in0=stat_q[:, 0:1],
                                           scalar1=1e-5)
            tok_v = p.g("vector", p0r, after=[tok_s])
            tok_q = p.g("scalar", (lambda e: e.activation(
                out=stat_q[:, 0:1], in_=stat_q[:, 0:1], func=AF.Sqrt,
                scale=1.0)), after=[tok_v])

            def p0a(e, tc=tc, buf=buf):
                e.reciprocal(out=stat_i[:, 0:1], in_=stat_q[:, 0:1])
                return e.tensor_scalar(out=buf[:], in0=buf[:],
                                       scalar1=stat_m[:, 0:1],
                                       scalar2=stat_i[:, 0:1],
                                       op0=OP.subtract, op1=OP.mult)
            tok_ap = p.g("vector", p0a, after=[tok_q])
            if debug and t == dbg_tile and tc == 0 and os.environ.get("DK_DUMP_BUF") == "3":
                if os.environ.get("DK_PRINT_SEQ"):
                    print("DUMP_BUF3 seq", p.seq)
                dbg_after.append(p.dma_g("gpsimd",
                                         (lambda e: [e.dma_start(out=dbg["d_u1"][:, 0:4], in_=stat_h[:, 0:4], allow_slow_non_contiguous=True),
                                                     e.dma_start(out=dbg["d_u1"][:, 4:5], in_=stat_m[:, 0:1], allow_slow_non_contiguous=True),
                                                     e.dma_start(out=dbg["d_u1"][:, 5:6], in_=stat_q[:, 0:1], allow_slow_non_contiguous=True),
                                                     e.dma_start(out=dbg["d_u1"][:, 6:7], in_=stat_i[:, 0:1], allow_slow_non_contiguous=True)]),
                                         4, after=[tok_ap]))
            if debug and t == dbg_tile and tc == 0 and os.environ.get("DK_DUMP_BUF") == "2":
                if os.environ.get("DK_PRINT_SEQ"):
                    print("DUMP_BUF2 seq", p.seq)
                dbg_after.append(p.dma_g("gpsimd",
                                         (lambda e, buf=buf: [e.dma_start(
                                             out=dbg["d_u1"][:, 0:1024], in_=buf[:])]),
                                         1, after=[tok_ap]))
            for gi in range(2):
                bk = (tc % 2) * 2 + gi
                def tpg(e, tc=tc, gi=gi, bk=bk, buf=buf):
                    ins = None
                    for pc4 in range(4):
                        pc = gi * 4 + pc4
                        ins = e.matmul(out=BK(bk)[:, pc4 * 128:(pc4 + 1) * 128],
                                       lhsT=buf[:, pc * 128:(pc + 1) * 128],
                                       rhs=id_f32[:], start=(pc4 == 0), stop=(pc4 == 3))
                    return ins
                tk = p.g("tensor", tpg, after=[tok_ap, tok_c0], wbank=[bk])
                def tpe(e, tc=tc, gi=gi, bk=bk):
                    ins = None
                    for pc4 in range(4):
                        pc = gi * 4 + pc4
                        ins = e.tensor_copy(out=fmc(x_fm, pc, tc),
                                            in_=BK(bk)[:, pc4 * 128:(pc4 + 1) * 128])
                    return ins
                tok_u1 = p.g("vector", tpe, after=[tk], rbank=[bk])
            tmb_free[tc % 2] = tok_u1
        dump_f(t, "d_u1", x_fm, [tok_u1])

        tk_prev = None
        for pc in range(PC):
            sq_tok = p.g("scalar", (lambda e, pc=pc: e.activation(
                out=xsq_fm[:], in_=fm(x_fm, pc), func=AF.Square)),
                after=[tok_u1 if pc == 0 else tk_prev])
            def lna_mm(e, pc=pc):
                e.matmul(out=BK(4)[0:1, 0:512], lhsT=w1b1[:, pc:pc + 1], rhs=fm(x_fm, pc),
                         start=(pc == 0), stop=(pc == PC - 1))
                e.matmul(out=BK(5)[0:1, 0:512], lhsT=w1sq[:, pc:pc + 1], rhs=xsq_fm[:],
                         start=(pc == 0), stop=(pc == PC - 1))
                return e.matmul(out=BK(6)[0:1, 0:512], lhsT=VC["ln_pre_w"][:, pc:pc + 1],
                                rhs=fm(x_fm, pc), start=(pc == 0), stop=(pc == PC - 1))
            tk_prev = p.g("tensor", lna_mm, after=[sq_tok, tok_u1],
                          wbank=[4, 5, 6] if pc == 0 else [])
        def lna_rows(e):
            e.tensor_scalar(out=lnmu, in0=BK(6)[0:1, 0:512],
                            scalar1=sb1[0:1, 0:1], scalar2=1.0 / D, op0=OP.add, op1=OP.mult)
            e.tensor_scalar_mul(out=xsq_fm[0:1, :], in0=BK(4)[0:1, 0:512], scalar1=2.0)
            e.tensor_tensor(out=xsq_fm[0:1, :], in0=xsq_fm[0:1, :], in1=BK(5)[0:1, 0:512],
                            op=OP.add)
            e.tensor_scalar(out=xsq_fm[0:1, :], in0=xsq_fm[0:1, :],
                            scalar1=sb1[0:1, 1:2], scalar2=1.0 / D, op0=OP.add, op1=OP.mult)
            e.tensor_tensor(out=s2w_f[0:1, :], in0=lnmu, in1=lnmu, op=OP.mult)
            e.tensor_tensor(out=lniv, in0=xsq_fm[0:1, :], in1=s2w_f[0:1, :],
                            op=OP.subtract)
            return e.tensor_scalar_add(out=lniv, in0=lniv, scalar1=1e-5)
        tok_lr = p.g("vector", lna_rows, after=[tk_prev, tok_sb1e], rbank=[4, 5, 6])
        _lr = [tok_lr]
        if debug and t == dbg_tile and os.environ.get("DK_DUMP_LNA"):
            _d = p.dma_g("gpsimd",
                         (lambda e: [e.dma_start(out=dbg["d_u1"][4:5, 0:512], in_=lniv)]),
                         1, after=[tok_lr])
            dbg_after.append(_d); _lr.append(_d)
        tok_lq = p.g("scalar", lambda e: e.activation(out=lniv, in_=lniv,
                                                      func=AF.Sqrt, scale=1.0),
                     after=_lr)
        _lq = [tok_lq]
        if debug and t == dbg_tile and os.environ.get("DK_DUMP_LNA"):
            _d = p.dma_g("gpsimd",
                         (lambda e: [e.dma_start(out=dbg["d_u1"][5:6, 0:512], in_=lniv)]),
                         1, after=[tok_lq])
            dbg_after.append(_d); _lq.append(_d)
        def lna_i(e):
            e.reciprocal(out=lniv, in_=lniv)
            e.tensor_tensor(out=s2w_f[0:1, :], in0=lnmu, in1=lniv, op=OP.mult)
            return e.tensor_scalar_mul(out=s2w_f[0:1, :], in0=s2w_f[0:1, :], scalar1=-1.0)
        tok_li = p.g("vector", lna_i, after=_lq)

        if debug and t == dbg_tile and os.environ.get("DK_DUMP_LNA"):
            if os.environ.get("DK_PRINT_SEQ"):
                print("DUMP_LNA seq", p.seq)
            dbg_after.append(p.dma_g("gpsimd",
                                     (lambda e: [e.dma_start(out=dbg["d_u1"][0:1, 0:512], in_=lnmu),
                                                 e.dma_start(out=dbg["d_u1"][1:2, 0:512], in_=lniv),
                                                 e.dma_start(out=dbg["d_u1"][2:3, 0:512], in_=s2w_f[0:1, :]),
                                                 e.dma_start(out=dbg["d_u1"][3:4, 0:2], in_=sb1[0:1, :]),
                                                 e.dma_start(out=dbg["d_u1"][:, 8:10], in_=scr_red[:, 0:2], allow_slow_non_contiguous=True),
                                                 e.dma_start(out=dbg["d_u1"][:, 16:16 + PC], in_=gscr[:, 0:PC], allow_slow_non_contiguous=True)]),
                                     6, after=[tok_li]))
        def lna_bc(e):
            e.matmul(out=BK(5), lhsT=ones_r128[:], rhs=lniv, start=True, stop=True)
            return e.matmul(out=BK(6), lhsT=ones_r128[:], rhs=s2w_f[0:1, :],
                            start=True, stop=True)
        tok_lbc = p.g("tensor", lna_bc, after=[tok_li], wbank=[5, 6])
        tok_xn_last = None
        for pc in range(PC):
            def lna_xn(e, pc=pc):
                e.tensor_scalar(out=xsq_fm[:], in0=BK(5),
                                scalar1=w2b1c[:, pc:pc + 1],
                                scalar2=VC["ln_attn_b"][:, pc:pc + 1],
                                op0=OP.mult, op1=OP.add)
                e.scalar_tensor_tensor(out=xsq_fm[:], in0=BK(6),
                                       scalar=VC["ln_attn_w"][:, pc:pc + 1],
                                       in1=xsq_fm[:], op0=OP.mult, op1=OP.add)
                e.tensor_tensor(out=bt2_f[:], in0=fm(x_fm, pc), in1=BK(5), op=OP.mult)
                return e.scalar_tensor_tensor(out=fm(xn_bf, pc), in0=bt2_f[:],
                                              scalar=w1w2c[:, pc:pc + 1],
                                              in1=xsq_fm[:], op0=OP.mult, op1=OP.add)
            tok_xn_last = p.g("vector", (lambda e, pc=pc: lna_xn(e, pc)),
                              after=[tok_lbc], rbank=[5, 6])
        dump_bf(t, "d_xn", xn_bf, [tok_xn_last])

        def p_dx(e):
            ins = None
            for pc in range(PC):
                e.tensor_sub(out=fm(dx_bf, pc)[:, 1:TILE], in0=fm(xn_bf, pc)[:, 0:TILE - 1],
                             in1=fm(xn_bf, pc)[:, 1:TILE])
                e.tensor_sub(out=fm(dx_bf, pc)[:, 0:1], in0=xn_halo[:, pc:pc + 1],
                             in1=fm(xn_bf, pc)[:, 0:1])
                ins = e.tensor_copy(out=xn_halo[:, pc:pc + 1],
                                    in_=fm(xn_bf, pc)[:, TILE - 1:TILE])
            return ins
        tok_dx = p.g("vector", p_dx, after=[tok_xn_last])

        def dense_mm(hu0, rhs_of, evac_fn, evac_eng, gname=None, after0=()):
            ensure_hu(t, hu0 + 1)
            mm_toks = []
            for kc in range(PC):
                mm_after = []
                if gname is not None:
                    def mkmix(e, kc=kc):
                        return e.scalar_tensor_tensor(
                            out=mixp[kc % 2][:], in0=fm(dx_bf, kc),
                            scalar=VC[gname][:, kc:kc + 1], in1=fm(xn_bf, kc),
                            op0=OP.mult, op1=OP.add)
                    aft = [tok_dx] + list(after0)
                    if kc >= 2:
                        aft.append(mm_toks[kc - 2])
                    mm_after.append(p.g("vector", mkmix, after=aft))
                def mmk(e, kc=kc, hu0=hu0):
                    ins = None
                    rhs = mixp[kc % 2][:] if gname is not None else rhs_of(kc)
                    for mc in range(8):
                        ins = e.matmul(out=BK(mc), lhsT=hu_sb(hu0 + mc // 4, kc, mc % 4),
                                       rhs=rhs, start=(kc == 0), stop=(kc == PC - 1))
                    return ins
                mm_after += [wload[(t, hu0)], wload[(t, hu0 + 1)]] + list(after0)
                mm_toks.append(p.g("tensor", mmk, after=mm_after,
                                   wbank=list(range(8)) if kc == 0 else []))
            def evall(e):
                ins = None
                for mc in range(8):
                    ins = evac_fn(e, mc, BK(mc))
                return ins
            ev = p.g(evac_eng, evall, after=[mm_toks[-1]], rbank=list(range(8)))
            slot_free[hu0 % 2] = mm_toks[-1]
            slot_free[(hu0 + 1) % 2] = mm_toks[-1]
            return mm_toks[-1], ev

        mm_r, ev_r_t = dense_mm(0, None, (lambda e, mc, bk: e.tensor_copy(
            out=fm(r_bf, mc), in_=bk)), "vector", gname="x_r")

        def ev_k(e, mc, bk):
            e.tensor_copy(out=fm(km_bf, mc), in_=bk)
            return e.tensor_scalar_mul(out=fm(kk_bf, mc), in0=bk,
                                       scalar1=VC["k_k"][:, mc:mc + 1])
        mm_k, ev_k_t = dense_mm(2, None, ev_k, "vector", gname="x_k")

        mm_v, ev_v_t = dense_mm(4, None, (lambda e, mc, bk: e.tensor_copy(
            out=fm(v_bf, mc), in_=bk)), "vector", gname="x_v")

        def build_mix_full(gname, aft):
            def mk(e):
                ins = None
                for kc in range(PC):
                    ins = e.scalar_tensor_tensor(
                        out=fm(o_bf, kc), in0=fm(dx_bf, kc),
                        scalar=VC[gname][:, kc:kc + 1], in1=fm(xn_bf, kc),
                        op0=OP.mult, op1=OP.add)
                return ins
            return p.g("vector", mk, after=aft)

        tok_mixw = build_mix_full("x_w", [ev_v_t])
        def w1mm(e):
            ins = None
            for kc in range(PC):
                ins = e.matmul(out=BK(0)[0:64, :], lhsT=w1_sb[:, kc * 64:(kc + 1) * 64],
                               rhs=fm(o_bf, kc), start=(kc == 0), stop=(kc == PC - 1))
            return ins
        tok_w1 = p.g("tensor", w1mm, after=[tok_mixw], wbank=[0])
        tok_w1e = p.g("scalar", lambda e: e.activation(out=t1_bf[:], in_=BK(0)[0:64, :],
                                                       func=AF.Tanh),
                      after=[tok_w1], rbank=[0])
        tok_we = None
        for wv in range(2):
            def w2mm(e, wv=wv):
                ins = None
                for mc4 in range(4):
                    mc = wv * 4 + mc4
                    ins = e.matmul(out=BK(4 + mc4), lhsT=w2_sb[:, mc * 128:(mc + 1) * 128],
                                   rhs=t1_bf[:], start=True, stop=True)
                return ins
            mt = p.g("tensor", w2mm, after=[tok_w1e], wbank=[4, 5, 6, 7])
            def w2ev(e, wv=wv):
                ins = None
                for mc4 in range(4):
                    mc = wv * 4 + mc4
                    ins = e.activation(out=fm(e_bf, mc), in_=BK(4 + mc4),
                                       func=AF.Sigmoid, bias=VC["w0"][:, mc:mc + 1])
                return ins
            tok_we = p.g("scalar", w2ev, after=[mt], rbank=[4, 5, 6, 7])
        tok_mixa = build_mix_full("x_a", [tok_w1])
        def a1mm(e):
            ins = None
            for kc in range(PC):
                ins = e.matmul(out=BK(1)[0:64, :], lhsT=a1_sb[:, kc * 64:(kc + 1) * 64],
                               rhs=fm(o_bf, kc), start=(kc == 0), stop=(kc == PC - 1))
            return ins
        tok_a1 = p.g("tensor", a1mm, after=[tok_mixa], wbank=[1])
        tok_a1e = p.g("scalar", lambda e: e.copy(out=t1_bf[:], in_=BK(1)[0:64, :]),
                      after=[tok_a1, tok_we], rbank=[1])
        tok_ae = None
        for wv in range(2):
            def a2mm(e, wv=wv):
                ins = None
                for mc4 in range(4):
                    mc = wv * 4 + mc4
                    ins = e.matmul(out=BK(mc4), lhsT=a2_sb[:, mc * 128:(mc + 1) * 128],
                                   rhs=t1_bf[:], start=True, stop=True)
                return ins
            mt = p.g("tensor", a2mm, after=[tok_a1e], wbank=[0, 1, 2, 3])
            def a2ev(e, wv=wv):
                ins = None
                for mc4 in range(4):
                    mc = wv * 4 + mc4
                    ins = e.activation(out=fm(b_bf, mc), in_=BK(mc4),
                                       func=AF.Sigmoid, bias=VC["a0"][:, mc:mc + 1])
                return ins
            tok_ae = p.g("scalar", a2ev, after=[mt], rbank=[0, 1, 2, 3])
        tok_mixg = build_mix_full("x_g", [tok_a1])
        def g1mm(e):
            ins = None
            for kc in range(PC):
                ins = e.matmul(out=BK(2), lhsT=g1_sb[:, kc * 128:(kc + 1) * 128],
                               rhs=fm(o_bf, kc), start=(kc == 0), stop=(kc == PC - 1))
            return ins
        tok_g1 = p.g("tensor", g1mm, after=[tok_mixg], wbank=[2])
        tok_g1e = p.g("scalar", lambda e: e.activation(out=sg_bf[:], in_=BK(2),
                                                       func=AF.Sigmoid),
                      after=[tok_g1], rbank=[2])
        tok_ge = None
        for wv in range(2):
            def g2mm(e, wv=wv):
                ins = None
                for mc4 in range(4):
                    mc = wv * 4 + mc4
                    ins = e.matmul(out=BK(4 + mc4), lhsT=g2_sb[:, mc * 128:(mc + 1) * 128],
                                   rhs=sg_bf[:], start=True, stop=True)
                return ins
            mt = p.g("tensor", g2mm, after=[tok_g1e], wbank=[4, 5, 6, 7])
            def g2ev(e, wv=wv):
                ins = None
                for mc4 in range(4):
                    mc = wv * 4 + mc4
                    ins = e.copy(out=fm(g_bf, mc), in_=BK(4 + mc4))
                return ins
            tok_ge = p.g("scalar", g2ev, after=[mt], rbank=[4, 5, 6, 7])

        mm_sq_tok = None
        for kc in range(PC):
            sq_t = p.g("vector", (lambda e, kc=kc: e.tensor_mul(
                out=pp_bf[:], in0=fm(kk_bf, kc), in1=fm(kk_bf, kc))),
                after=[ev_k_t] if kc == 0 else [mm_sq_tok])
            mm_sq_tok = p.g("tensor", (lambda e, kc=kc: e.matmul(
                out=BK(3)[0:16, :], lhsT=seg16b[:, kc * 16:(kc + 1) * 16],
                rhs=pp_bf[:], start=(kc == 0), stop=(kc == PC - 1))),
                after=[sq_t], wbank=[3] if kc == 0 else [])
        tok_kn1 = p.g("scalar", lambda e: e.activation(out=hrow[:, :], in_=BK(3)[0:16, :],
                                                       func=AF.Sqrt),
                      after=[mm_sq_tok], rbank=[3])
        def kkn2(e):
            e.tensor_scalar_max(out=hrow[:, :], in0=hrow[:, :], scalar1=1e-12)
            return e.reciprocal(out=hrow2[:, :], in_=hrow[:, :])
        tok_kn2 = p.g("vector", kkn2, after=[tok_kn1])
        tok_kkn = None
        for pc in range(PC):
            mt = p.g("tensor", (lambda e, pc=pc: e.matmul(
                out=BK(2), lhsT=bc16[0:16, pc * 128:(pc + 1) * 128],
                rhs=hrow2[0:16, :], start=True, stop=True)),
                after=[tok_kn2], wbank=[2])
            def kkap(e, pc=pc):
                e.tensor_tensor(out=fm(kk_bf, pc), in0=fm(kk_bf, pc), in1=BK(2), op=OP.mult)
                e.tensor_scalar(out=pp_bf[:], in0=fm(b_bf, pc),
                                scalar1=VC["k_a"][:, pc:pc + 1],
                                scalar2=VC["omka"][:, pc:pc + 1], op0=OP.mult, op1=OP.add)
                e.tensor_tensor(out=fm(km_bf, pc), in0=fm(km_bf, pc), in1=pp_bf[:], op=OP.mult)
                return e.tensor_tensor(out=fm(b_bf, pc), in0=fm(kk_bf, pc), in1=fm(b_bf, pc),
                                       op=OP.mult)
            tok_kkn = p.g("vector", (lambda e, pc=pc: kkap(e, pc)), after=[mt, tok_ae],
                          rbank=[2])
        dump_bf(t, "d_r", r_bf, [ev_r_t])
        dump_bf(t, "d_e", e_bf, [tok_we])
        dump_bf(t, "d_kk", kk_bf, [tok_kkn])
        dump_bf(t, "d_kmod", km_bf, [tok_kkn])
        dump_bf(t, "d_v", v_bf, [ev_v_t])
        dump_bf(t, "d_b", b_bf, [tok_kkn])
        dump_bf(t, "d_g", g_bf, [tok_ge])

        def tilG(nm, j, c):
            return TIL[nm][:, j * TILE + c * 128: j * TILE + (c + 1) * 128]

        def e2G(j, c):
            return E2G[:, j * TILE + c * 128: j * TILE + (c + 1) * 128]

        tok_grp = tok_kkn
        tok_oev = None
        for G in range(2):
            til_last = None
            for j in range(4):
                pc = 4 * G + j
                tk_cs = p.g("vector", (lambda e, pc=pc: e.tensor_tensor_scan(
                    out=cs_f[:], data0=rscan[:], data1=fm(e_bf, pc), initial=0.0,
                    op0=OP.mult, op1=OP.add)),
                    after=[tok_we, tok_grp, til_last])
                def mkcse(e, pc=pc):
                    e.tensor_tensor(out=cse_f[:], in0=cs_f[:], in1=fm(e_bf, pc),
                                    op=OP.subtract)
                    return e.tensor_copy(
                        out=csL_f[:, 0:TC],
                        in_=AP(cs_f.ap().tensor, 127, [[TILE, 128], [128, TC]]))
                tk_c2 = p.g("vector", mkcse, after=[tk_cs])
                def mkexp1(e, pc=pc, j=j):
                    e.activation(out=dec[0][:], in_=cs_f[:], func=AF.Exp, scale=-SC)
                    e.activation(out=dec[1][:], in_=cse_f[:], func=AF.Exp, scale=-SC)
                    return e.activation(out=DLG[:, j * TC:(j + 1) * TC],
                                        in_=csL_f[:, 0:TC], func=AF.Exp, scale=-SC)
                tk_ex = p.g("scalar", mkexp1, after=[tk_c2])
                def mktil1(e, pc=pc, j=j):
                    sl = slice(j * TILE, (j + 1) * TILE)
                    e.tensor_mul(out=TIL["rt"][:, sl], in0=fm(r_bf, pc), in1=dec[0][:])
                    e.scalar_tensor_tensor(out=TIL["at"][:, sl], in0=fm(kk_bf, pc),
                                           scalar=-1.0, in1=dec[1][:],
                                           op0=OP.mult, op1=OP.mult)
                    ins = None
                    for cq in range(TC):
                        ins = e.tensor_scalar(
                            out=cse_f[:, cq * 128:(cq + 1) * 128],
                            in0=cs_f[:, cq * 128:(cq + 1) * 128],
                            scalar1=csL_f[:, cq:cq + 1], scalar2=None, op0=OP.subtract)
                    return ins
                tk_t1 = p.g("vector", mktil1, after=[tk_ex])
                def mkexp2(e, pc=pc, j=j):
                    e.activation(out=dec[0][:], in_=cs_f[:], func=AF.Exp, scale=SC)
                    return e.activation(out=E2G[:, j * TILE:(j + 1) * TILE], in_=cse_f[:],
                                        func=AF.Exp, scale=SC)
                tk_e2 = p.g("scalar", mkexp2, after=[tk_t1])
                def mktil2(e, pc=pc, j=j):
                    sl = slice(j * TILE, (j + 1) * TILE)
                    e.tensor_mul(out=TIL["bt"][:, sl], in0=fm(b_bf, pc), in1=dec[0][:])
                    return e.tensor_mul(out=TIL["kt"][:, sl], in0=fm(km_bf, pc), in1=dec[0][:])
                til_last = p.g("vector", mktil2, after=[tk_e2])
            tok_til = til_last

            tok_state_ev = None
            for c in range(TC):
                def ghmm(e, G=G, c=c):
                    ins = None
                    for q, (dst, sa, sb) in enumerate(
                            [(PSA, "bt", "at"), (PSB, "at", "bt"), (PSC, "kt", "at")]):
                        for hf in range(2):
                            for j in range(4):
                                sl = 4 * hf + j
                                ins = e.matmul(
                                    out=dst[:, sl * 128:(sl + 1) * 128],
                                    lhsT=tilG(sa, j, c)[hf * 64:(hf + 1) * 64, :],
                                    rhs=tilG(sb, j, c)[hf * 64:(hf + 1) * 64, :],
                                    start=(j == 0), stop=(j == 3),
                                    tile_position=(64 * hf, 0))
                    return ins
                tok_gh = p.g("tensor", ghmm, after=[tok_til],
                             wbank=[0, 1, 2, 3, 4, 5])
                tok_hme = p.g("vector", lambda e: e.tensor_tensor(
                    out=h8(Hm_sb), in0=ps_h8(PSC), in1=mask8(msk_su), op=OP.mult),
                              after=[tok_gh], rbank=[4, 5])
                def ghev(e):
                    e.tensor_tensor(out=h8(X_sb), in0=ps_h8(PSA), in1=mask8(msk_su),
                                    op=OP.mult)
                    return e.tensor_tensor(out=h8(XT_sb), in0=ps_h8(PSB), in1=mask8(msk_sl),
                                           op=OP.mult)
                tok_ghe = p.g("vector", ghev, after=[tok_gh], rbank=[0, 1, 2, 3])
                def vtmm(e, G=G, c=c):
                    ins = None
                    for hf in range(2):
                        for j in range(4):
                            ins = e.matmul(
                                out=BK(6 + hf)[:, j * 64:(j + 1) * 64],
                                lhsT=fmc(v_bf, 4 * G + j, c)[hf * 64:(hf + 1) * 64, :],
                                rhs=id_bf[hf * 64:(hf + 1) * 64, hf * 64:hf * 64 + 64],
                                start=(j == 0), stop=(j == 3),
                                tile_position=(64 * hf, 0))
                    return ins
                tok_vt = p.g("tensor", vtmm, after=[], wbank=[6, 7])
                def vtev(e):
                    e.copy(out=vtm_sb[:, 0:256], in_=BK(6)[:, 0:256])
                    return e.copy(out=vtm_sb[:, 256:512], in_=BK(7)[:, 0:256])
                tok_vte = p.g("scalar", vtev, after=[tok_vt], rbank=[6, 7])
                def rhsA(e, G=G, c=c):
                    ins = None
                    for hf in range(2):
                        for j in range(4):
                            ins = e.matmul(
                                out=BK(hf)[hf * 64:(hf + 1) * 64, j * 128:(j + 1) * 128],
                                lhsT=sT_bf[hf * 64:(hf + 1) * 64,
                                           (4 * G + j) * 64:(4 * G + j + 1) * 64],
                                rhs=tilG("at", j, c)[hf * 64:(hf + 1) * 64, :],
                                start=(j == 0), stop=(j == 3),
                                tile_position=(64 * hf, 64 * hf))
                    return ins
                tok_rhsA = p.g("tensor", rhsA, after=[tok_state_ev, tok_til],
                               wbank=[0, 1])
                def rhs0(e):
                    e.copy(out=pp_bf[0:64, :], in_=BK(0)[0:64, 0:512])
                    return e.copy(out=pp_bf[64:128, :], in_=BK(1)[64:128, 0:512])
                tok_rhs0 = p.g("scalar", rhs0, after=[tok_rhsA], rbank=[0, 1])
                def rhsB(e, G=G, c=c):
                    ins = None
                    for hf in range(2):
                        for j in range(4):
                            sl = 4 * hf + j
                            ins = e.matmul(
                                out=BK(2)[hf * 64:(hf + 1) * 64, j * 128:(j + 1) * 128],
                                lhsT=vtm_sb[:, sl * 64:(sl + 1) * 64],
                                rhs=Hm_sb[:, sl * 128:(sl + 1) * 128],
                                start=(j == 0), stop=(j == 3),
                                tile_position=(0, 64 * hf))
                    return ins
                tok_rhs = p.g("tensor", rhsB, after=[tok_vte, tok_hme], wbank=[2])
                tok_rhse = p.g("vector", lambda e: e.tensor_tensor(
                    out=rhs_sb, in0=BK(2)[:, 0:512], in1=pp_bf[:], op=OP.add),
                               after=[tok_rhs0, tok_rhs], rbank=[2])
                def u0mm(e):
                    ins = None
                    for hf in range(2):
                        for j in range(4):
                            ins = e.matmul(
                                out=BK(hf)[:, j * 64:(j + 1) * 64],
                                lhsT=rhs_sb[hf * 64:(hf + 1) * 64, j * 128:(j + 1) * 128],
                                rhs=id_bf[hf * 64:(hf + 1) * 64, hf * 64:hf * 64 + 64],
                                start=(j == 0), stop=(j == 3),
                                tile_position=(64 * hf, 0))
                    return ins
                tok_u0 = p.g("tensor", u0mm, after=[tok_rhse], wbank=[0, 1])
                def u0ev(e):
                    e.copy(out=u_sb[:, 0:256], in_=BK(0)[:, 0:256])
                    return e.copy(out=u_sb[:, 256:512], in_=BK(1)[:, 0:256])
                tok_ue = p.g("scalar", u0ev, after=[tok_u0], rbank=[0, 1])
                tok_xe = None
                for ks in range(7):
                    do_sq = ks < 6
                    def ummm(e):
                        ins = None
                        for hg in range(8):
                            ins = e.matmul(out=BK(0)[:, hg * 64:(hg + 1) * 64],
                                           lhsT=X_sb[:, hg * 128:(hg + 1) * 128],
                                           rhs=u_sb[:, hg * 64:(hg + 1) * 64],
                                           start=(hg == 0), stop=(hg == 7))
                        return ins
                    tok_um = p.g("tensor", ummm,
                                 after=[tok_ue, tok_xeX if ks > 0 else tok_xe],
                                 wbank=[0])
                    tok_sq = None
                    if do_sq:
                        def sqmm(e, ks=ks):
                            ins = None
                            for hg in range(8):
                                ins = e.matmul(out=PSB[:, hg * 128:(hg + 1) * 128],
                                               lhsT=XT_sb[:, hg * 128:(hg + 1) * 128],
                                               rhs=X_sb[:, hg * 128:(hg + 1) * 128],
                                               start=(hg % 4 == 0), stop=(hg % 4 == 3))
                            if ks < 5:
                                for hg in range(8):
                                    ins = e.matmul(out=PSC[:, hg * 128:(hg + 1) * 128],
                                                   lhsT=X_sb[:, hg * 128:(hg + 1) * 128],
                                                   rhs=XT_sb[:, hg * 128:(hg + 1) * 128],
                                                   start=(hg % 4 == 0), stop=(hg % 4 == 3))
                            return ins
                        tok_sq = p.g("tensor", sqmm, after=[tok_xe],
                                     wbank=[2, 3] + ([4, 5] if ks < 5 else []))
                    tok_ue = p.g("vector", lambda e: e.tensor_tensor(
                        out=u_sb, in0=BK(0)[:, 0:512], in1=u_sb, op=OP.add),
                                 after=[tok_um], rbank=[0])
                    if do_sq:
                        tok_xeX = p.g("vector",
                                      lambda e: e.tensor_copy(out=X_sb, in_=PSB[:]),
                                      after=[tok_sq], rbank=[2, 3])
                        if ks < 5:
                            tok_xe = p.g("vector",
                                         lambda e: e.tensor_copy(out=XT_sb, in_=PSC[:]),
                                         after=[tok_sq], rbank=[4, 5])
                        else:
                            tok_xe = tok_xeX
                def m12mm(e, G=G, c=c):
                    ins = None
                    for q, (dst, sa) in enumerate([(PSB, "bt"), (PSC, "kt")]):
                        for hf in range(2):
                            for j in range(4):
                                sl = 4 * hf + j
                                ins = e.matmul(
                                    out=dst[:, sl * 128:(sl + 1) * 128],
                                    lhsT=tilG(sa, j, c)[hf * 64:(hf + 1) * 64, :],
                                    rhs=tilG("rt", j, c)[hf * 64:(hf + 1) * 64, :],
                                    start=(j == 0), stop=(j == 3),
                                    tile_position=(64 * hf, 0))
                    return ins
                tok_m12 = p.g("tensor", m12mm, after=[tok_xe], wbank=[2, 3, 4, 5])
                def m12ev(e):
                    e.tensor_tensor(out=h8(M1_sb), in0=ps_h8(PSB), in1=mask8(msk_iu),
                                    op=OP.mult)
                    return e.tensor_tensor(out=h8(M2_sb), in0=ps_h8(PSC), in1=mask8(msk_iu),
                                           op=OP.mult)
                tok_m12e = p.g("vector", m12ev, after=[tok_m12], rbank=[2, 3, 4, 5])
                def outA(e, G=G, c=c):
                    ins = None
                    for hf in range(2):
                        for j in range(4):
                            ins = e.matmul(
                                out=BK(hf)[hf * 64:(hf + 1) * 64, j * 128:(j + 1) * 128],
                                lhsT=sT_bf[hf * 64:(hf + 1) * 64,
                                           (4 * G + j) * 64:(4 * G + j + 1) * 64],
                                rhs=tilG("rt", j, c)[hf * 64:(hf + 1) * 64, :],
                                start=(j == 0), stop=(j == 3),
                                tile_position=(64 * hf, 64 * hf))
                    return ins
                tok_outA = p.g("tensor", outA, after=[tok_state_ev, tok_til],
                               wbank=[0, 1])
                def out0(e):
                    e.copy(out=pp_bf[0:64, :], in_=BK(0)[0:64, 0:512])
                    return e.copy(out=pp_bf[64:128, :], in_=BK(1)[64:128, 0:512])
                tok_out0 = p.g("scalar", out0, after=[tok_outA], rbank=[0, 1])
                def outB(e, G=G, c=c):
                    ins = None
                    for hf in range(2):
                        for j in range(4):
                            sl = 4 * hf + j
                            e.matmul(out=BK(6)[hf * 64:(hf + 1) * 64, j * 128:(j + 1) * 128],
                                     lhsT=u_sb[:, sl * 64:(sl + 1) * 64],
                                     rhs=M1_sb[:, sl * 128:(sl + 1) * 128],
                                     start=(j == 0), stop=False, tile_position=(0, 64 * hf))
                        for j in range(4):
                            sl = 4 * hf + j
                            ins = e.matmul(out=BK(6)[hf * 64:(hf + 1) * 64,
                                                     j * 128:(j + 1) * 128],
                                           lhsT=vtm_sb[:, sl * 64:(sl + 1) * 64],
                                           rhs=M2_sb[:, sl * 128:(sl + 1) * 128],
                                           start=False, stop=(j == 3),
                                           tile_position=(0, 64 * hf))
                    return ins
                tok_out = p.g("tensor", outB, after=[tok_m12e, tok_ue], wbank=[6])
                def outev(e, G=G, c=c):
                    return e.tensor_tensor(
                        out=AP(o_bf.ap().tensor, (4 * G) * TILE + c * 128,
                               [[PC * TILE, 128], [TILE, 4], [1, 128]]),
                        in0=BK(6).rearrange("p (j x) -> p j x", j=4),
                        in1=pp_bf[:].rearrange("p (j x) -> p j x", j=4), op=OP.add)
                tok_oev = p.g("vector", outev, after=[tok_out0], rbank=[6])
                def bbkbf(e, G=G, c=c):
                    ins = None
                    for j in range(4):
                        e.tensor_mul(out=bbkb[:, j * 128:(j + 1) * 128],
                                     in0=fmc(b_bf, 4 * G + j, c), in1=e2G(j, c))
                        ins = e.tensor_mul(out=bbkb[:, 512 + j * 128: 512 + (j + 1) * 128],
                                           in0=fmc(km_bf, 4 * G + j, c), in1=e2G(j, c))
                    return ins
                tok_bk = p.g("vector", bbkbf, after=[tok_til])
                def sttr(e):
                    ins = None
                    for q in range(2):
                        for hf in range(2):
                            for j in range(4):
                                ins = e.matmul(
                                    out=BK(2 + 2 * q + hf)[:, j * 64:(j + 1) * 64],
                                    lhsT=bbkb[hf * 64:(hf + 1) * 64,
                                              q * 512 + j * 128: q * 512 + (j + 1) * 128],
                                    rhs=id_bf[hf * 64:(hf + 1) * 64, hf * 64:hf * 64 + 64],
                                    start=(j == 0), stop=(j == 3),
                                    tile_position=(64 * hf, 0))
                    return ins
                tok_st1 = p.g("tensor", sttr, after=[tok_bk], wbank=[2, 3, 4, 5])
                def sttre(e):
                    e.copy(out=btm_sb[:, 0:256], in_=BK(2)[:, 0:256])
                    e.copy(out=btm_sb[:, 256:512], in_=BK(3)[:, 0:256])
                    e.copy(out=ktm_sb[:, 0:256], in_=BK(4)[:, 0:256])
                    return e.copy(out=ktm_sb[:, 256:512], in_=BK(5)[:, 0:256])
                tok_st1e = p.g("scalar", sttre, after=[tok_st1], rbank=[2, 3, 4, 5])
                def stprod(e, G=G, c=c):
                    ins = None
                    for hf in range(2):
                        for j in range(4):
                            sl = 4 * hf + j
                            e.matmul(out=BK(6)[hf * 64:(hf + 1) * 64,
                                               j * 64:(j + 1) * 64],
                                     lhsT=btm_sb[:, sl * 64:(sl + 1) * 64],
                                     rhs=u_sb[:, sl * 64:(sl + 1) * 64],
                                     start=(j == 0), stop=False,
                                     tile_position=(0, 64 * hf))
                        for j in range(4):
                            sl = 4 * hf + j
                            ins = e.matmul(out=BK(6)[hf * 64:(hf + 1) * 64,
                                                     j * 64:(j + 1) * 64],
                                           lhsT=ktm_sb[:, sl * 64:(sl + 1) * 64],
                                           rhs=vtm_sb[:, sl * 64:(sl + 1) * 64],
                                           start=False, stop=(j == 3),
                                           tile_position=(0, 64 * hf))
                    return ins
                tok_st2 = p.g("tensor", stprod, after=[tok_st1e], wbank=[6])
                def stev(e, G=G, c=c):
                    for j in range(4):
                        e.scalar_tensor_tensor(
                            out=sT[:, (4 * G + j) * 64:(4 * G + j + 1) * 64],
                            in0=sT[:, (4 * G + j) * 64:(4 * G + j + 1) * 64],
                            scalar=DLG[:, j * TC + c: j * TC + c + 1],
                            in1=BK(6)[:, j * 64:(j + 1) * 64],
                            op0=OP.mult, op1=OP.add)
                    return e.tensor_copy(
                        out=sT_bf[:, (4 * G) * 64:(4 * G + 4) * 64],
                        in_=sT[:, (4 * G) * 64:(4 * G + 4) * 64])
                tok_state_ev = p.g("vector", stev, after=[tok_st2], rbank=[6])
            tok_grp = tok_state_ev
        tok_wkv = tok_grp
        dump_bf(t, "d_o", o_bf, [tok_wkv, tok_oev])

        gs_tok = None
        for pc in range(PC):
            gs_tok = p.g("tensor", (lambda e, pc=pc: e.matmul(
                out=BK(0)[0:16, :], lhsT=seg16b[:, pc * 16:(pc + 1) * 16],
                rhs=fm(o_bf, pc), start=(pc == 0), stop=(pc == PC - 1))),
                after=[tok_wkv, tok_oev], wbank=[0] if pc == 0 else [])
        gq_tok = None
        for pc in range(PC):
            sq2 = p.g("vector", (lambda e, pc=pc: e.tensor_mul(
                out=pp_bf[:], in0=fm(o_bf, pc), in1=fm(o_bf, pc))),
                after=[tok_wkv, tok_oev] if pc == 0 else [gq_tok])
            gq_tok = p.g("tensor", (lambda e, pc=pc: e.matmul(
                out=BK(1)[0:16, :], lhsT=seg16b[:, pc * 16:(pc + 1) * 16],
                rhs=pp_bf[:], start=(pc == 0), stop=(pc == PC - 1))),
                after=[sq2], wbank=[1] if pc == 0 else [])
        def gnrows(e):
            e.tensor_scalar_mul(out=hrow3[:, :], in0=BK(0)[0:16, :], scalar1=1.0 / N)
            e.tensor_scalar_mul(out=hrow2[:, :], in0=BK(1)[0:16, :], scalar1=1.0 / N)
            e.tensor_tensor(out=hrow[:, :], in0=hrow3[:, :], in1=hrow3[:, :], op=OP.mult)
            e.tensor_tensor(out=hrow2[:, :], in0=hrow2[:, :], in1=hrow[:, :],
                            op=OP.subtract)
            return e.tensor_scalar_add(out=hrow2[:, :], in0=hrow2[:, :], scalar1=0.00064)
        tok_gr = p.g("vector", gnrows, after=[gs_tok, gq_tok], rbank=[0, 1])
        tok_gq = p.g("scalar", lambda e: e.activation(out=hrow2[:, :], in_=hrow2[:, :],
                                                      func=AF.Sqrt, scale=1.0),
                     after=[tok_gr])
        tok_gi = p.g("vector", lambda e: e.reciprocal(out=hrow2[:, :], in_=hrow2[:, :]),
                     after=[tok_gq])
        rk_tok = None
        mt_prev = tok_gi
        for pc in range(PC):
            mt = p.g("vector", (lambda e, pc=pc: e.scalar_tensor_tensor(
                out=pp_bf[:], in0=fm(r_bf, pc), scalar=VC["r_k"][:, pc:pc + 1],
                in1=fm(km_bf, pc), op0=OP.mult, op1=OP.mult)), after=[mt_prev])
            rk_tok = p.g("tensor", (lambda e, pc=pc: e.matmul(
                out=BK(2)[0:16, :], lhsT=seg16b[:, pc * 16:(pc + 1) * 16],
                rhs=pp_bf[:], start=(pc == 0), stop=(pc == PC - 1))),
                after=[mt], wbank=[2] if pc == 0 else [])
            mt_prev = rk_tok
        tok_rkr = p.g("vector", lambda e: e.tensor_copy(out=hrow[:, :], in_=BK(2)[0:16, :]),
                      after=[rk_tok], rbank=[2])
        tok_og = None
        for pc in range(PC):
            def gnbc(e, pc=pc):
                e.matmul(out=BK(3), lhsT=bc16[0:16, pc * 128:(pc + 1) * 128],
                         rhs=hrow2[0:16, :], start=True, stop=True)
                e.matmul(out=BK(4), lhsT=bc16[0:16, pc * 128:(pc + 1) * 128],
                         rhs=hrow3[0:16, :], start=True, stop=True)
                return e.matmul(out=BK(5), lhsT=bc16[0:16, pc * 128:(pc + 1) * 128],
                                rhs=hrow[0:16, :], start=True, stop=True)
            mt = p.g("tensor", gnbc, after=[tok_gi, tok_rkr], wbank=[3, 4, 5])
            def gnap(e, pc=pc):
                e.tensor_sub(out=s2w_f[:], in0=fm(o_bf, pc), in1=BK(4))
                e.tensor_tensor(out=s2w_f[:], in0=s2w_f[:], in1=BK(3), op=OP.mult)
                e.tensor_scalar(out=s2w_f[:], in0=s2w_f[:],
                                scalar1=VC["gn_w"][:, pc:pc + 1],
                                scalar2=VC["gn_b"][:, pc:pc + 1], op0=OP.mult, op1=OP.add)
                e.tensor_tensor(out=bt2_f[:], in0=BK(5), in1=fm(v_bf, pc), op=OP.mult)
                e.tensor_tensor(out=s2w_f[:], in0=s2w_f[:], in1=bt2_f[:], op=OP.add)
                return e.tensor_tensor(out=fm(kk_bf, pc), in0=s2w_f[:], in1=fm(g_bf, pc),
                                       op=OP.mult)
            tok_og = p.g("vector", (lambda e, pc=pc: gnap(e, pc)), after=[mt],
                         rbank=[3, 4, 5])

        def ev_wo(e, mc, bk):
            e.scalar_tensor_tensor(out=fm(x_fm, mc), in0=fm(x_fm, mc),
                                   scalar=VC["ln_pre_w"][:, mc:mc + 1], in1=bk,
                                   op0=OP.mult, op1=OP.add)
            return e.tensor_scalar_add(out=fm(x_fm, mc), in0=fm(x_fm, mc),
                                       scalar1=VC["ln_pre_b"][:, mc:mc + 1])
        mm_wo, tok_x1 = dense_mm(6, (lambda kc: fm(kk_bf, kc)), ev_wo, "vector",
                                 after0=[tok_og])
        dump_f(t, "d_x1", x_fm, [tok_x1])

        tk_prev = None
        for pc in range(PC):
            sq_tok = p.g("scalar", (lambda e, pc=pc: e.activation(
                out=xsq_fm[:], in_=fm(x_fm, pc), func=AF.Square)),
                after=[tok_x1 if pc == 0 else tk_prev])
            def lnf_mm(e, pc=pc):
                e.matmul(out=BK(0)[0:1, :], lhsT=ones_col[:], rhs=fm(x_fm, pc),
                         start=(pc == 0), stop=(pc == PC - 1))
                return e.matmul(out=BK(3)[0:1, :], lhsT=ones_col[:], rhs=xsq_fm[:],
                                start=(pc == 0), stop=(pc == PC - 1))
            tk_prev = p.g("tensor", lnf_mm, after=[sq_tok, tok_x1],
                          wbank=[0, 3] if pc == 0 else [])
        def lnf_rows(e):
            e.tensor_scalar_mul(out=lnmu, in0=BK(0)[0:1, :], scalar1=1.0 / D)
            e.tensor_scalar_mul(out=lniv, in0=BK(3)[0:1, :], scalar1=1.0 / D)
            e.tensor_tensor(out=s2w_f[0:1, :], in0=lnmu, in1=lnmu,
                            op=OP.mult)
            e.tensor_tensor(out=lniv, in0=lniv, in1=s2w_f[0:1, :],
                            op=OP.subtract)
            return e.tensor_scalar_add(out=lniv, in0=lniv, scalar1=1e-5)
        tok_fr = p.g("vector", lnf_rows, after=[tk_prev], rbank=[0, 3])
        tok_fq = p.g("scalar", lambda e: e.activation(out=lniv, in_=lniv,
                                                      func=AF.Sqrt, scale=1.0),
                     after=[tok_fr])
        def lnf_i(e):
            e.reciprocal(out=lniv, in_=lniv)
            e.tensor_tensor(out=s2w_f[0:1, :], in0=lnmu, in1=lniv,
                            op=OP.mult)
            return e.tensor_scalar_mul(out=s2w_f[0:1, :], in0=s2w_f[0:1, :], scalar1=-1.0)
        tok_fi = p.g("vector", lnf_i, after=[tok_fq])
        def lnf_bc(e):
            e.matmul(out=BK(1), lhsT=ones_r128[:], rhs=lniv, start=True, stop=True)
            return e.matmul(out=BK(2), lhsT=ones_r128[:], rhs=s2w_f[0:1, :],
                            start=True, stop=True)
        tok_fbc = p.g("tensor", lnf_bc, after=[tok_fi], wbank=[1, 2])
        tok_xf = None
        for pc in range(PC):
            def lnf_xf(e, pc=pc):
                e.tensor_tensor(out=bt2_f[:], in0=fm(x_fm, pc), in1=BK(1), op=OP.mult)
                e.tensor_tensor(out=bt2_f[:], in0=bt2_f[:], in1=BK(2), op=OP.add)
                return e.tensor_scalar(out=fm(xn_bf, pc), in0=bt2_f[:],
                                       scalar1=VC["ln_ffn_w"][:, pc:pc + 1],
                                       scalar2=VC["ln_ffn_b"][:, pc:pc + 1],
                                       op0=OP.mult, op1=OP.add)
            tok_xf = p.g("vector", (lambda e, pc=pc: lnf_xf(e, pc)),
                         after=[tok_fbc], rbank=[1, 2])
        dump_bf(t, "d_xf", xn_bf, [tok_xf])
        def p_kf(e):
            ins = None
            for pc in range(PC):
                e.tensor_sub(out=fm(dx_bf, pc)[:, 1:TILE], in0=fm(xn_bf, pc)[:, 0:TILE - 1],
                             in1=fm(xn_bf, pc)[:, 1:TILE])
                e.tensor_sub(out=fm(dx_bf, pc)[:, 0:1], in0=xf_halo[:, pc:pc + 1],
                             in1=fm(xn_bf, pc)[:, 0:1])
                e.tensor_copy(out=xf_halo[:, pc:pc + 1], in_=fm(xn_bf, pc)[:, TILE - 1:TILE])
                ins = e.scalar_tensor_tensor(out=fm(dx_bf, pc), in0=fm(dx_bf, pc),
                                             scalar=VC["ffn_x_k"][:, pc:pc + 1],
                                             in1=fm(xn_bf, pc), op0=OP.mult, op1=OP.add)
            return ins
        tok_kf = p.g("vector", p_kf, after=[tok_xf])

        tok_h2 = None
        for ms in range(4):
            hu0 = 8 + 2 * ms
            def ev_f1(e, mc, bk, ms=ms):
                return e.activation(out=h2c(ms * 8 + mc), in_=bk, func=AF.Relu)
            mmt, evt = dense_mm(hu0, (lambda kc: fm(dx_bf, kc)), ev_f1, "scalar",
                                after0=[tok_kf] if ms == 0 else [tok_kf, tok_h2])
            def sq_f1(e, ms=ms):
                ins = None
                for mc in range(8):
                    q = ms * 8 + mc
                    ins = e.tensor_mul(out=h2c(q), in0=h2c(q), in1=h2c(q))
                return ins
            tok_h2 = p.g("vector", sq_f1, after=[evt])
        mm_f2 = None
        for ksl in range(4):
            hu0 = 16 + 2 * ksl
            ensure_hu(t, hu0 + 1)
            def f2mm(e, ksl=ksl, hu0=hu0):
                ins = None
                for kc in range(PC):
                    for mc in range(8):
                        ins = e.matmul(out=BK(mc), lhsT=hu_sb(hu0 + mc // 4, kc, mc % 4),
                                       rhs=h2c(ksl * 8 + kc),
                                       start=(ksl == 0 and kc == 0),
                                       stop=(ksl == 3 and kc == PC - 1))
                return ins
            aft = [tok_h2, wload[(t, hu0)], wload[(t, hu0 + 1)]]
            mm_f2 = p.g("tensor", f2mm, after=aft,
                        wbank=list(range(8)) if ksl == 0 else [])
            slot_free[hu0 % 2] = mm_f2
            slot_free[(hu0 + 1) % 2] = mm_f2
        def ev_f2(e):
            ins = None
            for mc in range(8):
                ins = e.tensor_tensor(out=fm(x_fm, mc), in0=fm(x_fm, mc), in1=BK(mc),
                                      op=OP.add)
            return ins
        tok_x2 = p.g("vector", ev_f2, after=[mm_f2], rbank=list(range(8)))
        dump_f(t, "d_x2", x_fm, [tok_x2])

        tok_xo = tok_x2
        for tc in range(TC):
            buf = tmb[tc % 2]
            tok_ld = p.dma_g("sync",
                             (lambda e, t=t, tc=tc, buf=buf: [e.dma_start(
                                 out=buf[:],
                                 in_=x_in[t * TILE + tc * 128: t * TILE + (tc + 1) * 128, :])]),
                             1, after=[tmb_free[0], tmb_free[1]])
            for gi in range(2):
                bk = (tc % 2) * 2 + gi
                def tpg2(e, tc=tc, gi=gi, bk=bk, buf=buf):
                    ins = None
                    for pc4 in range(4):
                        pc = gi * 4 + pc4
                        ins = e.matmul(out=BK(bk)[:, pc4 * 128:(pc4 + 1) * 128],
                                       lhsT=buf[:, pc * 128:(pc + 1) * 128],
                                       rhs=id_f32[:], start=(pc4 == 0), stop=(pc4 == 3))
                    return ins
                tk = p.g("tensor", tpg2, after=[tok_ld, tok_x2], wbank=[bk])
                def tpe2(e, tc=tc, gi=gi, bk=bk):
                    ins = None
                    for pc4 in range(4):
                        pc = gi * 4 + pc4
                        ins = e.tensor_tensor(out=fmc(x_fm, pc, tc), in0=fmc(x_fm, pc, tc),
                                              in1=BK(bk)[:, pc4 * 128:(pc4 + 1) * 128],
                                              op=OP.add)
                    return ins
                tok_xo = p.g("vector", tpe2, after=[tk], rbank=[bk])
                tmb_free[tc % 2] = tok_xo

        tk_prev = None
        for pc in range(PC):
            sq_tok = p.g("scalar", (lambda e, pc=pc: e.activation(
                out=xsq_fm[:], in_=fm(x_fm, pc), func=AF.Square)),
                after=[tok_xo if pc == 0 else tk_prev])
            def ln3_mm(e, pc=pc):
                e.matmul(out=BK(4)[0:1, :], lhsT=ones_col[:], rhs=fm(x_fm, pc),
                         start=(pc == 0), stop=(pc == PC - 1))
                return e.matmul(out=BK(7)[0:1, :], lhsT=ones_col[:], rhs=xsq_fm[:],
                                start=(pc == 0), stop=(pc == PC - 1))
            tk_prev = p.g("tensor", ln3_mm, after=[sq_tok, tok_xo],
                          wbank=[4, 7] if pc == 0 else [])
        def ln3_rows(e):
            e.tensor_scalar_mul(out=lnmu, in0=BK(4)[0:1, :], scalar1=1.0 / D)
            e.tensor_scalar_mul(out=lniv, in0=BK(7)[0:1, :], scalar1=1.0 / D)
            e.tensor_tensor(out=s2w_f[0:1, :], in0=lnmu, in1=lnmu,
                            op=OP.mult)
            e.tensor_tensor(out=lniv, in0=lniv, in1=s2w_f[0:1, :],
                            op=OP.subtract)
            return e.tensor_scalar_add(out=lniv, in0=lniv, scalar1=1e-5)
        tok_3r = p.g("vector", ln3_rows, after=[tk_prev], rbank=[4, 7])
        tok_3q = p.g("scalar", lambda e: e.activation(out=lniv, in_=lniv,
                                                      func=AF.Sqrt, scale=1.0),
                     after=[tok_3r])
        def ln3_i(e):
            e.reciprocal(out=lniv, in_=lniv)
            e.tensor_tensor(out=s2w_f[0:1, :], in0=lnmu, in1=lniv,
                            op=OP.mult)
            return e.tensor_scalar_mul(out=s2w_f[0:1, :], in0=s2w_f[0:1, :], scalar1=-1.0)
        tok_3i = p.g("vector", ln3_i, after=[tok_3q])
        def ln3_bc(e):
            e.matmul(out=BK(5), lhsT=ones_r128[:], rhs=lniv, start=True, stop=True)
            return e.matmul(out=BK(6), lhsT=ones_r128[:], rhs=s2w_f[0:1, :],
                            start=True, stop=True)
        tok_3bc = p.g("tensor", ln3_bc, after=[tok_3i], wbank=[5, 6])
        tok_x3 = None
        for pc in range(PC):
            def ln3_ap(e, pc=pc):
                e.tensor_tensor(out=bt2_f[:], in0=fm(x_fm, pc), in1=BK(5), op=OP.mult)
                e.tensor_tensor(out=fm(x_fm, pc), in0=bt2_f[:], in1=BK(6), op=OP.add)
                return e.tensor_copy(out=fm(xn_bf, pc), in_=fm(x_fm, pc))
            tok_x3 = p.g("vector", (lambda e, pc=pc: ln3_ap(e, pc)),
                         after=[tok_3bc], rbank=[5, 6])
        dump_f(t, "d_x3", x_fm, [tok_x3])

        def gt1(e):
            ins = None
            for kc in range(PC):
                for q in range(2):
                    ins = e.matmul(out=BK(q), lhsT=wg1_sb[:, kc * 256 + q * 128:
                                                         kc * 256 + (q + 1) * 128],
                                   rhs=fm(xn_bf, kc), start=(kc == 0), stop=(kc == PC - 1))
            return ins
        tok_gt1 = p.g("tensor", gt1, after=[tok_x3], wbank=[0, 1])
        def gt1e(e):
            e.activation(out=hg_bf[:, 0:TILE], in_=BK(0), func=AF.Gelu,
                         bias=bg1c[:, 0:1])
            return e.activation(out=hg_bf[:, TILE:2 * TILE], in_=BK(1), func=AF.Gelu,
                                bias=bg1c[:, 1:2])
        tok_gt1e = p.g("scalar", gt1e, after=[tok_gt1], rbank=[0, 1])
        def gt2(e):
            e.matmul(out=BK(2)[0:40, :], lhsT=wg2_sb[:, 0:40], rhs=hg_bf[:, 0:TILE],
                     start=True, stop=False)
            return e.matmul(out=BK(2)[0:40, :], lhsT=wg2_sb[:, 40:80],
                            rhs=hg_bf[:, TILE:2 * TILE], start=False, stop=True)
        tok_gt2 = p.g("tensor", gt2, after=[tok_gt1e], wbank=[2])
        tok_lg = p.g("vector", lambda e: e.tensor_scalar_add(
            out=lgt_f[:], in0=BK(2)[0:20, :], scalar1=bg2c[0:20, :]),
                     after=[tok_gt2], rbank=[2])
        def lgT(e):
            ins = None
            for tc in range(TC):
                ins = e.matmul(out=BK(3)[:, tc * 20:(tc + 1) * 20],
                               lhsT=lgt_f[:, tc * 128:(tc + 1) * 128],
                               rhs=id_f32[0:20, 0:20], start=(tc == 0), stop=(tc == 3))
            return ins
        tok_lgT = p.g("tensor", lgT, after=[tok_lg], wbank=[3])
        tok_lgTe = p.g("vector", lambda e: e.tensor_copy(out=lgtT[:], in_=BK(3)[:, 0:TC * NE]),
                       after=[tok_lgT], rbank=[3])
        def lgv(tt):
            return tt[:].rearrange("p (c k) -> p c k", c=TC)
        def growc(i):
            return AP(grow.ap().tensor, i * TC, [[16, 128], [1, TC], [0, NE]])
        def top2(e):
            e.tensor_reduce(out=grow[:, 0:TC], in_=lgv(lgtT), axis=mybir.AxisListType.X,
                            op=OP.max)
            for cq in range(TC):
                e.tensor_scalar(out=gscr[:, cq * NE:(cq + 1) * NE],
                                in0=lgtT[:, cq * NE:(cq + 1) * NE],
                                scalar1=grow[:, cq:cq + 1], scalar2=None, op0=OP.subtract)
            return e.tensor_copy(out=gscr2[:], in_=gscr[:])
        tok_t2a = p.g("vector", top2, after=[tok_lgTe])
        tok_t2b = p.g("scalar", lambda e: e.activation(out=gscr2[:], in_=gscr2[:],
                                                       func=AF.Exp),
                      after=[tok_t2a])
        def top2c(e):
            e.tensor_reduce(out=grow[:, TC:2 * TC], in_=lgv(gscr2),
                            axis=mybir.AxisListType.X, op=OP.add)
            e.reciprocal(out=grow[:, 2 * TC:3 * TC], in_=grow[:, TC:2 * TC])
            for cq in range(TC):
                e.tensor_scalar(out=gscr[:, cq * NE:(cq + 1) * NE],
                                in0=lgtT[:, cq * NE:(cq + 1) * NE],
                                scalar1=grow[:, cq:cq + 1], scalar2=None, op0=OP.is_ge)
            e.scalar_tensor_tensor(out=lgv(gscr), in0=lgv(gscr), scalar=-1e30,
                                   in1=lgv(lgtT), op0=OP.mult, op1=OP.add)
            e.tensor_reduce(out=grow[:, 3 * TC:4 * TC], in_=lgv(gscr),
                            axis=mybir.AxisListType.X, op=OP.max)
            for cq in range(TC):
                e.tensor_scalar(out=gscr[:, cq * NE:(cq + 1) * NE],
                                in0=lgtT[:, cq * NE:(cq + 1) * NE],
                                scalar1=grow[:, 3 * TC + cq:3 * TC + cq + 1], scalar2=None, op0=OP.is_ge)
            e.tensor_tensor(out=gscr[:], in0=gscr[:], in1=gscr2[:], op=OP.mult)
            ins = None
            for cq in range(TC):
                ins = e.tensor_scalar(out=gscr2[:, cq * NE:(cq + 1) * NE],
                                      in0=gscr[:, cq * NE:(cq + 1) * NE],
                                      scalar1=grow[:, 2 * TC + cq:2 * TC + cq + 1],
                                      scalar2=None, op0=OP.mult)
            return ins
        tok_t2c = p.g("vector", top2c, after=[tok_t2b])
        def gTb(e):
            ins = None
            for tc in range(TC):
                ins = e.matmul(out=BK(4)[0:20, tc * 128:(tc + 1) * 128],
                               lhsT=gscr2[:, tc * 20:(tc + 1) * 20],
                               rhs=id_f32[:], start=(tc == 0), stop=(tc == 3))
            return ins
        tok_gTb = p.g("tensor", gTb, after=[tok_t2c], wbank=[4])
        tok_gbe = p.g("scalar", lambda e: e.copy(out=gates_bf[:], in_=BK(4)[0:20, :]),
                      after=[tok_gTb], rbank=[4])
        if debug and t == dbg_tile:
            dbg_after.append(p.dma_g("gpsimd",
                                     (lambda e: [e.dma_start(out=dbg["d_gates"][:],
                                                             in_=gates_bf[:])]),
                                     1, after=[tok_gbe]))

        tok_selmm = None
        tok_selev = None
        for ee in range(NE):
            tok_msel = p.g("vector", (lambda e, ee=ee: e.memset(sel2[ee % 2], 1.0)),
                           after=[tok_gbe] if ee < 2 else [tok_selmm])
            tok_sel = p.g("gpsimd", (lambda e, ee=ee: e.affine_select(
                out=sel2[ee % 2], in_=sel2[ee % 2], pattern=[[0, 128]],
                compare_op=OP.is_equal, fill=0.0, base=-ee, channel_multiplier=1)),
                after=[tok_msel])
            def selmm(e, ee=ee):
                return e.matmul(out=BK(ee % 2), lhsT=sel2[ee % 2],
                                rhs=gates_bf, start=True, stop=True)
            tok_selmm = p.g("tensor", selmm, after=[tok_sel, tok_gbe], wbank=[ee % 2])
            eng_ev = "vector" if ee % 2 == 0 else "scalar"
            def selev(e, ee=ee):
                if ee % 2 == 0:
                    return e.tensor_copy(out=gbc_t(ee), in_=BK(ee % 2))
                return e.copy(out=gbc_t(ee), in_=BK(ee % 2))
            tok_selev = p.g(eng_ev, selev, after=[tok_selmm], rbank=[ee % 2])

        def bemm(e):
            ins = None
            for mc in range(8):
                ins = e.matmul(out=BK(mc), lhsT=be_sb[:, mc * 128:(mc + 1) * 128],
                               rhs=gates_bf[:], start=True, stop=False)
            return ins
        tok_y0 = p.g("tensor", bemm, after=[tok_gbe], wbank=list(range(8)))

        tok_gx = None
        mm_moe = tok_y0
        for ee in range(NE):
            def gxf(e, ee=ee):
                ins = None
                for pc in range(PC):
                    ins = e.tensor_tensor(out=fm(kk_bf, pc), in0=fm(xn_bf, pc),
                                          in1=gbc_t(ee), op=OP.mult)
                return ins
            tok_gx = p.g("vector", gxf, after=[tok_selev, mm_moe])
            hu0 = 24 + 2 * ee
            ensure_hu(t, hu0 + 1)
            def moemm(e, ee=ee, hu0=hu0):
                ins = None
                for kc in range(PC):
                    for mc in range(8):
                        ins = e.matmul(out=BK(mc), lhsT=hu_sb(hu0 + mc // 4, kc, mc % 4),
                                       rhs=fm(kk_bf, kc), start=False,
                                       stop=(ee == NE - 1 and kc == PC - 1))
                return ins
            mm_moe = p.g("tensor", moemm,
                         after=[tok_gx, wload[(t, hu0)], wload[(t, hu0 + 1)]])
            slot_free[hu0 % 2] = mm_moe
            slot_free[(hu0 + 1) % 2] = mm_moe
        def ev_moe(e):
            ins = None
            for mc in range(8):
                ins = e.tensor_tensor(out=fm(x_fm, mc), in0=fm(x_fm, mc), in1=BK(mc),
                                      op=OP.add)
            return ins
        tok_x4 = p.g("vector", ev_moe, after=[mm_moe], rbank=list(range(8)))

        store_toks = []
        for tc in range(TC):
            buf = tmb[tc % 2]
            for gi in range(2):
                bk = (tc % 2) * 2 + gi
                def tpo(e, tc=tc, gi=gi, bk=bk):
                    ins = None
                    for pc4 in range(4):
                        pc = gi * 4 + pc4
                        ins = e.matmul(out=BK(bk)[:, pc4 * 128:(pc4 + 1) * 128],
                                       lhsT=fmc(x_fm, pc, tc), rhs=id_f32[:],
                                       start=(pc4 == 0), stop=(pc4 == 3))
                    return ins
                tk = p.g("tensor", tpo, after=[tok_x4, tmb_free[0], tmb_free[1]], wbank=[bk])
                def tpoe(e, tc=tc, gi=gi, bk=bk, buf=buf):
                    ins = None
                    for pc4 in range(4):
                        pc = gi * 4 + pc4
                        ins = e.tensor_copy(out=buf[:, pc * 128:(pc + 1) * 128],
                                            in_=BK(bk)[:, pc4 * 128:(pc4 + 1) * 128])
                    return ins
                tok_tpe = p.g("vector", tpoe, after=[tk], rbank=[bk])
            def p8s(e, tc=tc, buf=buf):
                for hh in range(2):
                    e.activation(out=xsq_fm[:], in_=buf[:, hh * 512:(hh + 1) * 512],
                                 func=AF.Copy, accum_out=stat_h[:, 4 * tc + hh:4 * tc + hh + 1])
                    ins = e.activation(out=xsq_fm[:], in_=buf[:, hh * 512:(hh + 1) * 512],
                                       func=AF.Square,
                                       accum_out=stat_h[:, 4 * tc + 2 + hh:4 * tc + 3 + hh])
                return ins
            tok_s8 = p.g("scalar", p8s, after=[tok_tpe])
            def p8r(e, tc=tc):
                e.tensor_add(out=stat_s[:, tc:tc + 1], in0=stat_h[:, 4 * tc:4 * tc + 1],
                             in1=stat_h[:, 4 * tc + 1:4 * tc + 2])
                e.tensor_add(out=stat_q[:, tc:tc + 1], in0=stat_h[:, 4 * tc + 2:4 * tc + 3],
                             in1=stat_h[:, 4 * tc + 3:4 * tc + 4])
                e.tensor_scalar_mul(out=stat_m[:, tc:tc + 1], in0=stat_s[:, tc:tc + 1],
                                    scalar1=1.0 / D)
                e.tensor_scalar_mul(out=stat_q[:, tc:tc + 1], in0=stat_q[:, tc:tc + 1],
                                    scalar1=1.0 / D)
                e.tensor_tensor(out=stat_i[:, tc:tc + 1], in0=stat_m[:, tc:tc + 1],
                                in1=stat_m[:, tc:tc + 1], op=OP.mult)
                e.tensor_tensor(out=stat_q[:, tc:tc + 1], in0=stat_q[:, tc:tc + 1],
                                in1=stat_i[:, tc:tc + 1], op=OP.subtract)
                return e.tensor_scalar_add(out=stat_q[:, tc:tc + 1],
                                           in0=stat_q[:, tc:tc + 1], scalar1=1e-5)
            tok_r8 = p.g("vector", p8r, after=[tok_s8])
            tok_q8 = p.g("scalar", (lambda e, tc=tc: e.activation(
                out=stat_q[:, tc:tc + 1], in_=stat_q[:, tc:tc + 1], func=AF.Sqrt,
                scale=1.0)), after=[tok_r8])
            def p8a(e, tc=tc, buf=buf):
                e.reciprocal(out=stat_i[:, tc:tc + 1], in_=stat_q[:, tc:tc + 1])
                return e.tensor_scalar(out=buf[:], in0=buf[:],
                                       scalar1=stat_m[:, tc:tc + 1],
                                       scalar2=stat_i[:, tc:tc + 1],
                                       op0=OP.subtract, op1=OP.mult)
            tok_a8 = p.g("vector", p8a, after=[tok_q8])
            tok_st8 = p.dma_g("sync",
                              (lambda e, t=t, tc=tc, buf=buf: [e.dma_start(
                                  out=y_out[t * TILE + tc * 128: t * TILE + (tc + 1) * 128, :],
                                  in_=buf[:])]), 1, after=[tok_a8])
            store_toks.append(tok_st8)
            tmb_free[tc % 2] = tok_st8
        tok_store_last = store_toks[-1]

    p.g("vector", lambda e: e.tensor_copy(out=pp_bf[0:1, 0:1], in_=pp_bf[0:1, 0:1]),
        after=[tok_store_last] + list(dbg_after))

    cut = int(os.environ.get("DK_CUT", str(p.seq + 10)))
    if os.environ.get("DK_PRINT_SEQ"):
        print("TOTAL_SEQ", p.seq)
    with nc.Block() as block:
        class VecGuard:
            def __init__(self, eng, sp):
                self.e = eng
                self.sp = sp
                self.prev_out = set()
                self.prev_in = set()

            @staticmethod
            def _names(v):
                if isinstance(v, bass.AP):
                    return {v.name}
                return set()

            def wait_ge(self, *a, **k):
                return self.e.wait_ge(*a, **k)

            def __getattr__(self, nm):
                real = getattr(self.e, nm)
                if not callable(real):
                    return real
                def wrap(*args, **kw):
                    outs, ins = set(), set()
                    if nm == "memset" and args:
                        outs |= self._names(args[0])
                        rest = args[1:]
                    else:
                        rest = args
                    for k, v in kw.items():
                        if k in ("out", "accum_out"):
                            outs |= self._names(v)
                        else:
                            ins |= self._names(v)
                    for v in rest:
                        ins |= self._names(v)
                    if (self.prev_out & (outs | ins)) or (self.prev_in & outs):
                        self.e.tensor_copy(out=self.sp, in_=self.sp)
                    self.prev_out, self.prev_in = outs, ins
                    return real(*args, **kw)
                return wrap

        def emit(ename):
            def run(eng):
                if ename == "vector":
                    eng = VecGuard(eng, vspace[:, 0:1])
                for (waits, fn, n_dma, seq) in p.groups[ename]:
                    if seq >= cut:
                        break
                    for s, v in waits.items():
                        if s.startswith("dma_"):
                            eng.wait_ge(dsem[s[4:]], v)
                        else:
                            eng.wait_ge(sem[s], v)
                    res = fn(eng)
                    if n_dma is not None:
                        assert len(res) == n_dma, \
                            f"{ename} dma group: declared {n_dma}, issued {len(res)}"
                        for ins in res:
                            ins.then_inc(dsem[ename], 16)
                    else:
                        res.then_inc(sem[ename], 1)
            return run
        block.sync(emit("sync"))
        block.scalar(emit("scalar"))
        block.vector(emit("vector"))
        block.tensor(emit("tensor"))
        block.gpsimd(emit("gpsimd"))

    es.close()
    return nc



def kernel(**inputs):
    x = np.ascontiguousarray(np.asarray(inputs["L0_attn_x"], dtype=np.float32))
    nc = build_nc(debug=False)
    base = {}
    ren = {"W_ffn1": "L0_attn_W_ffn1", "W_ffn2": "L0_attn_W_ffn2",
           "wg1": "L0_ffn_wg1", "bg1": "L0_ffn_bg1", "wg2": "L0_ffn_wg2",
           "bg2": "L0_ffn_bg2", "We": "L0_ffn_We", "be": "L0_ffn_be"}
    names = (WN1D + ["r_k", "w1", "w2", "a1", "a2", "g1", "g2",
                     "W_r", "W_k", "W_v", "W_o", "W_ffn1", "W_ffn2",
                     "wg1", "bg1", "wg2", "bg2", "We", "be"])
    for nm in names:
        key = ren.get(nm, "L0_attn_" + nm)
        if key not in inputs:
            key = "L0_ffn_" + nm
        base[nm] = np.ascontiguousarray(np.asarray(inputs[key], dtype=np.float32))
    in_maps = []
    for b in range(8):
        m = dict(base)
        m["x"] = x[b]
        in_maps.append(m)
    res = run_bass_kernel_spmd(nc, in_maps, core_ids=list(range(8)))
    return np.stack([res.results[b]["out"] for b in range(8)], axis=0)


# version 08 (final)
# speedup vs baseline: 1.0162x; 1.0039x over previous
import contextlib
import numpy as np
import concourse.bass as bass
import concourse.mybir as mybir
from concourse.bass_utils import run_bass_kernel_spmd

F32 = mybir.dt.float32
BF16 = mybir.dt.bfloat16
AF = mybir.ActivationFunctionType
OP = mybir.AluOpType
AP = bass.AP

import os
TOK, D, TILE = 4096, 1024, 512
NT, PC, TC = TOK // TILE, D // 128, TILE // 128
NT = int(os.environ.get("DK_NT", NT))
H, N, DFF, NE = 16, 64, 4096, 20
SC = float(np.exp(-0.5))
ENGS = ("sync", "scalar", "vector", "tensor", "gpsimd")

WN1D = ["ln_pre_w", "ln_pre_b", "ln_attn_w", "ln_attn_b", "ln_ffn_w", "ln_ffn_b",
        "x_r", "x_w", "x_k", "x_v", "x_a", "x_g", "w0", "a0", "k_k", "k_a",
        "gn_w", "gn_b", "ffn_x_k"]


class Plan:

    def __init__(self):
        self.groups = {e: [] for e in ENGS}
        self.prog = {e: 0 for e in ENGS}
        self.dmac = {e: 0 for e in ENGS}
        self.seq = 0
        self.bank_w = [None] * 8
        self.bank_r = [None] * 8

    def _mk(self, eng, fn, after, is_dma, n_dma=0):
        waits = {}
        for tokn in after:
            if tokn is None:
                continue
            s, v = tokn
            if s == eng and eng != "gpsimd":
                continue
            waits[s] = max(waits.get(s, 0), v)
        import traceback
        if os.environ.get("DK_TRACE_SEQ") and self.seq == int(os.environ["DK_TRACE_SEQ"]):
            print("SEQ", self.seq, "eng", eng, "dma", is_dma)
            traceback.print_stack()
        self.groups[eng].append((waits, fn, n_dma if is_dma else None, self.seq))
        self.seq += 1
        if is_dma:
            self.dmac[eng] += 16 * n_dma
            return ("dma_" + eng, self.dmac[eng])
        self.prog[eng] += 1
        return (eng, self.prog[eng])

    def g(self, eng, fn, after=(), wbank=(), rbank=()):
        after = list(after)
        for b in wbank:
            after.append(self.bank_r[b])
            after.append(self.bank_w[b])
        for b in rbank:
            after.append(self.bank_w[b])
        tok = self._mk(eng, fn, after, False)
        for b in wbank:
            self.bank_w[b] = tok
        for b in rbank:
            self.bank_r[b] = tok
        return tok

    def dma_g(self, eng, fn, n_dma, after=()):
        return self._mk(eng, fn, list(after), True, n_dma)


def build_nc(debug=False, dbg_tile=NT - 1):
    nc = bass.Bass("TRN2", target_bir_lowering=False, debug=False,
                   detect_race_conditions=False)

    x_in = nc.dram_tensor("x", [TOK, D], F32, kind="ExternalInput").ap()
    y_out = nc.dram_tensor("out", [TOK, D], F32, kind="ExternalOutput").ap()
    dr = {}
    for nm in WN1D:
        dr[nm] = nc.dram_tensor(nm, [D], F32, kind="ExternalInput").ap()
    for nm, sh in [("r_k", [H, N]), ("w1", [D, 64]), ("w2", [64, D]),
                   ("a1", [D, 64]), ("a2", [64, D]), ("g1", [D, 128]),
                   ("g2", [128, D]), ("W_r", [D, D]), ("W_k", [D, D]),
                   ("W_v", [D, D]), ("W_o", [D, D]), ("W_ffn1", [D, DFF]),
                   ("W_ffn2", [DFF, D]), ("wg1", [D, 256]), ("bg1", [256]),
                   ("wg2", [256, 40]), ("bg2", [40]), ("We", [NE, D, D]),
                   ("be", [NE, D])]:
        dr[nm] = nc.dram_tensor(nm, sh, F32, kind="ExternalInput").ap()

    BIGW = ([("W_r", 0), ("W_k", 0), ("W_v", 0), ("W_o", 0)]
            + [("ffn1", i) for i in range(4)] + [("ffn2", i) for i in range(4)]
            + [("We", e) for e in range(NE)])
    NU = len(BIGW)
    wcache = nc.dram_tensor("wcache", [128, 2 * NU * 4096], BF16, kind="Internal").ap()

    dbg = {}
    if debug:
        for nm in ["d_u1", "d_xn", "d_r", "d_e", "d_kk", "d_kmod", "d_v", "d_b",
                   "d_g", "d_o", "d_x1", "d_xf", "d_x2", "d_x3"]:
            dbg[nm] = nc.dram_tensor(nm, [128, PC * TILE], F32, kind="ExternalOutput").ap()
        dbg["d_gates"] = nc.dram_tensor("d_gates", [20, TILE], F32, kind="ExternalOutput").ap()

    es = contextlib.ExitStack()
    E = es.enter_context

    id_bf = E(nc.sbuf_tensor("id_bf", [128, 128], BF16))
    id_f32 = E(nc.sbuf_tensor("id_f32", [128, 128], F32))
    msk_su = E(nc.sbuf_tensor("msk_su", [128, 128], BF16))
    msk_iu = E(nc.sbuf_tensor("msk_iu", [128, 128], BF16))
    msk_sl = E(nc.sbuf_tensor("msk_sl", [128, 128], BF16))
    rscan = E(nc.sbuf_tensor("rscan", [128, TILE], BF16))
    seg16b = E(nc.sbuf_tensor("seg16b", [128, PC * 16], BF16))
    bc16 = E(nc.sbuf_tensor("bc16", [16, PC * 128], F32))
    ones_col = E(nc.sbuf_tensor("ones_col", [128, 1], F32))
    cst_eps = E(nc.sbuf_tensor("cst_eps", [128, 2], F32))
    ones_r128 = E(nc.sbuf_tensor("ones_r128", [1, 128], F32))

    VC = {nm: E(nc.sbuf_tensor("c_" + nm, [128, PC], F32)) for nm in WN1D}
    VC["r_k"] = E(nc.sbuf_tensor("c_r_k", [128, PC], F32))
    VC["omka"] = E(nc.sbuf_tensor("c_omka", [128, PC], F32))
    w1w2c = E(nc.sbuf_tensor("w1w2c", [128, PC], F32))
    w2b1c = E(nc.sbuf_tensor("w2b1c", [128, PC], F32))
    bg1c = E(nc.sbuf_tensor("bg1c", [128, 2], F32))
    bg2c = E(nc.sbuf_tensor("bg2c", [40, 1], F32))
    w1sq = E(nc.sbuf_tensor("w1sq", [128, PC], F32))
    w1b1 = E(nc.sbuf_tensor("w1b1", [128, PC], F32))
    sb1 = E(nc.sbuf_tensor("sb1", [1, 2], F32))
    scr_red = E(nc.sbuf_tensor("scr_red", [128, 2], F32))

    w1_sb = E(nc.sbuf_tensor("w1_sb", [128, PC * 64], BF16))
    a1_sb = E(nc.sbuf_tensor("a1_sb", [128, PC * 64], BF16))
    w2_sb = E(nc.sbuf_tensor("w2_sb", [64, D], BF16))
    a2_sb = E(nc.sbuf_tensor("a2_sb", [64, D], BF16))
    g1_sb = E(nc.sbuf_tensor("g1_sb", [128, PC * 128], BF16))
    g2_sb = E(nc.sbuf_tensor("g2_sb", [128, D], BF16))
    wg1_sb = E(nc.sbuf_tensor("wg1_sb", [128, PC * 256], BF16))
    wg2_sb = E(nc.sbuf_tensor("wg2_sb", [128, 2 * 40], BF16))
    be_sb = E(nc.sbuf_tensor("be_sb", [NE, D], BF16))

    sT = E(nc.sbuf_tensor("sT", [128, PC * N], F32))
    sT_bf = E(nc.sbuf_tensor("sT_bf", [128, PC * N], BF16))
    xn_halo = E(nc.sbuf_tensor("xn_halo", [128, PC], BF16))
    xf_halo = E(nc.sbuf_tensor("xf_halo", [128, PC], BF16))

    NSLOT = 2
    wslot = [E(nc.sbuf_tensor(f"wslot{i}", [128, 4096], BF16)) for i in range(NSLOT)]

    tmb0 = E(nc.sbuf_tensor("tmb0", [128, D], F32))
    tmb = [tmb0, tmb0]

    x_fm = E(nc.sbuf_tensor("x_fm", [128, PC * TILE], F32))
    xsq_fm = E(nc.sbuf_tensor("xsq_fm", [128, TILE], F32))
    s2w_f = E(nc.sbuf_tensor("s2w_f", [128, TILE], F32))
    bt2_f = E(nc.sbuf_tensor("bt2_f", [128, TILE], F32))
    pass

    lniv = bt2_f[0:1, 0:TILE]
    xn_bf = E(nc.sbuf_tensor("xn_bf", [128, PC * TILE], BF16))
    dx_bf = E(nc.sbuf_tensor("dx_bf", [128, PC * TILE], BF16))
    e_bf = E(nc.sbuf_tensor("e_bf", [128, PC * TILE], BF16))
    b_bf = E(nc.sbuf_tensor("b_bf", [128, PC * TILE], BF16))
    r_bf = E(nc.sbuf_tensor("r_bf", [128, PC * TILE], BF16))
    kk_bf = E(nc.sbuf_tensor("kk_bf", [128, PC * TILE], BF16))
    km_bf = E(nc.sbuf_tensor("km_bf", [128, PC * TILE], BF16))
    v_bf = E(nc.sbuf_tensor("v_bf", [128, PC * TILE], BF16))
    g_bf = E(nc.sbuf_tensor("g_bf", [128, PC * TILE], BF16))
    o_bf = E(nc.sbuf_tensor("o_bf", [128, PC * TILE], BF16))
    mixp = [E(nc.sbuf_tensor(f"mixp{i}", [128, TILE], BF16)) for i in range(2)]
    t1_bf = E(nc.sbuf_tensor("t1_bf", [64, TILE], BF16))
    sg_bf = E(nc.sbuf_tensor("sg_bf", [128, TILE], BF16))
    pp_bf = E(nc.sbuf_tensor("pp_bf", [128, TILE], BF16))

    ARENA_COLS = 18944
    arena = E(nc.sbuf_tensor("arena", [128, ARENA_COLS], BF16))
    _ac = [0]
    def carve(n):
        a0 = _ac[0]; _ac[0] += n
        assert _ac[0] <= ARENA_COLS
        return arena[:, a0:a0 + n]
    TIL = {nm: carve(4 * TILE) for nm in ("at", "bt", "kt", "rt")}
    E2G = carve(4 * TILE)
    X_sb = carve(8 * 128)
    XT_sb = carve(8 * 128)
    Hm_sb = carve(8 * 128)
    M1_sb = carve(8 * 128)
    M2_sb = carve(8 * 128)
    u_sb = carve(8 * 64)
    vtm_sb = carve(8 * 64)
    rhs_sb = carve(4 * 128)
    btm_sb = carve(8 * 64)
    ktm_sb = carve(8 * 64)
    bbkb = carve(2 * 4 * 128)
    def h2c(q):
        return arena[:, q * 512:(q + 1) * 512]

    cs_f = E(nc.sbuf_tensor("cs_f", [128, TILE], F32))
    cse_f = E(nc.sbuf_tensor("cse_f", [128, TILE], F32))
    csL_f = E(nc.sbuf_tensor("csL_f", [128, TC], F32))
    DLG = E(nc.sbuf_tensor("DLG", [128, 4 * TC], F32))
    dec = [E(nc.sbuf_tensor(f"dec{i}", [128, TILE], BF16)) for i in range(2)]

    hrow = E(nc.sbuf_tensor("hrow", [16, TILE], F32))
    hrow2 = E(nc.sbuf_tensor("hrow2", [16, TILE], F32))
    hrow3 = E(nc.sbuf_tensor("hrow3", [16, TILE], F32))
    rows20 = E(nc.sbuf_tensor("rows20", [20, TILE + TILE], F32))
    lgt_f = rows20[0:20, 0:TILE]
    lnmu = rows20[0:1, TILE:2 * TILE]

    hg_bf = arena[:, 0:2 * TILE]
    lgtT = arena[:, 5 * TILE:5 * TILE + 160].bitcast(F32)
    gscr = arena[:, 5 * TILE + 160:5 * TILE + 320].bitcast(F32)
    gscr2 = arena[:, 5 * TILE + 320:5 * TILE + 480].bitcast(F32)
    grow = E(nc.sbuf_tensor("grow", [128, 16], F32))
    vspace = E(nc.sbuf_tensor("vspace", [128, 1], F32))
    gates_bf = arena[0:20, 2 * TILE:3 * TILE]
    sel2 = [arena[0:20, 15 * TILE + i * 128:15 * TILE + (i + 1) * 128] for i in range(2)]
    GBC0 = 16 * TILE
    def gbc_t(ee):
        return arena[:, GBC0 + ee * TILE:GBC0 + (ee + 1) * TILE]

    stat_s = E(nc.sbuf_tensor("stat_s", [128, TC], F32))
    stat_q = E(nc.sbuf_tensor("stat_q", [128, TC], F32))
    stat_h = E(nc.sbuf_tensor("stat_h", [128, 4 * TC], F32))
    stat_m = E(nc.sbuf_tensor("stat_m", [128, TC], F32))
    stat_i = E(nc.sbuf_tensor("stat_i", [128, TC], F32))

    PSA = E(nc.psum_tensor("PSA", [128, 1024], F32))
    PSB = E(nc.psum_tensor("PSB", [128, 1024], F32))
    PSC = E(nc.psum_tensor("PSC", [128, 1024], F32))
    PSD = E(nc.psum_tensor("PSD", [128, 1024], F32))
    PST = [PSA, PSB, PSC, PSD]
    def BK(i):
        return PST[i // 2][:, (i % 2) * 512:(i % 2) * 512 + 512]

    sem = {e: E(nc.semaphore("sem_" + e)) for e in ENGS}
    dsem = {e: E(nc.semaphore("dsem_" + e)) for e in ENGS}


    p = Plan()

    def fm(tt, pc):
        return tt[:, pc * TILE:(pc + 1) * TILE]

    def fmc(tt, pc, c):
        return tt[:, pc * TILE + c * 128: pc * TILE + (c + 1) * 128]

    def mask8(m):
        return AP(m.ap().tensor, 0, [[128, 128], [0, 8], [1, 128]])

    def h8(tt):
        return tt.rearrange("p (h j) -> p h j", h=8)

    def ps_h8(pst):
        return pst[:].rearrange("p (h j) -> p h j", h=8)

    def init_memsets(e):
        e.memset(id_bf[:], 1.0)
        e.memset(id_f32[:], 1.0)
        e.memset(msk_su[:], 1.0)
        e.memset(msk_iu[:], 1.0)
        e.memset(msk_sl[:], 1.0)
        e.memset(rscan[:], 1.0)
        for c in range(TC):
            e.memset(rscan[:, c * 128:c * 128 + 1], 0.0)
        e.memset(seg16b[:], 1.0)
        e.memset(bc16[:], 1.0)
        e.memset(ones_col[:], 1.0)
        e.memset(cst_eps[:, 0:1], 1e-5)
        e.memset(cst_eps[:, 1:2], 0.00064)
        e.memset(ones_r128[:], 1.0)
        e.memset(sT[:], 0.0)
        e.memset(sT_bf[:], 0.0)
        e.memset(xn_halo[:], 0.0)
        return e.memset(xf_halo[:], 0.0)

    tok_ms = p.g("vector", init_memsets)
    _asel = [
        (id_bf, OP.is_equal, 1, -1), (id_f32, OP.is_equal, 1, -1),
        (msk_su, OP.is_gt, 1, -1), (msk_iu, OP.is_ge, 1, -1),
        (msk_sl, OP.is_gt, -1, 1),
    ]
    for _t, _op, _st, _cm in _asel:
        p.g("gpsimd", (lambda e, _t=_t, _op=_op, _st=_st, _cm=_cm: e.affine_select(
            out=_t[:], in_=_t[:], pattern=[[_st, 128]], compare_op=_op,
            fill=0.0, base=0, channel_multiplier=_cm)), after=[tok_ms])
    p.g("gpsimd", (lambda e: e.affine_select(
        out=bc16[:].rearrange("r (b o) -> r b o", o=64),
        in_=bc16[:].rearrange("r (b o) -> r b o", o=64),
        pattern=[[1, 16], [0, 64]], compare_op=OP.is_equal,
        fill=0.0, base=0, channel_multiplier=-1)), after=[tok_ms])
    p.g("gpsimd", (lambda e: e.affine_select(
        out=seg16b[0:64, :].rearrange("p (c j) -> p c j", j=16),
        in_=seg16b[0:64, :].rearrange("p (c j) -> p c j", j=16),
        pattern=[[-2, PC], [1, 16]], compare_op=OP.is_equal,
        fill=0.0, base=0, channel_multiplier=0)), after=[tok_ms])
    p.g("gpsimd", (lambda e: e.affine_select(
        out=seg16b[64:128, :].rearrange("p (c j) -> p c j", j=16),
        in_=seg16b[64:128, :].rearrange("p (c j) -> p c j", j=16),
        pattern=[[-2, PC], [1, 16]], compare_op=OP.is_equal,
        fill=0.0, base=-1, channel_multiplier=0)), after=[tok_ms])
    tok_c0 = ("gpsimd", p.prog["gpsimd"])

    VSTAGE_LIST = WN1D + ["r_k", "bg1"]
    vstage = x_fm[0:8, 0:len(VSTAGE_LIST) * 128]

    def load_smalls(e):
        ins = []
        for vi, nm in enumerate(VSTAGE_LIST):
            if nm == "r_k":
                src_ = dr["r_k"].rearrange("h n -> (h n)").rearrange("(c p) -> c p", p=128)
            elif nm == "bg1":
                src_ = dr["bg1"].rearrange("(c p) -> c p", p=128)
            else:
                src_ = dr[nm].rearrange("(c p) -> c p", p=128)
            rows = src_.shape[0]
            ins.append(e.dma_start(out=vstage[0:rows, vi * 128:(vi + 1) * 128], in_=src_))
        ins.append(e.dma_start(out=bg2c[:], in_=AP(dr["bg2"].tensor, 0, [[1, 40], [1, 1]])))
        ins.append(e.dma_start(out=w1_sb[:].rearrange("p (c m) -> p c m", m=64),
                   in_=dr["w1"].rearrange("(c p) m -> p c m", p=128)))
        ins.append(e.dma_start(out=a1_sb[:].rearrange("p (c m) -> p c m", m=64),
                   in_=dr["a1"].rearrange("(c p) m -> p c m", p=128)))
        ins.append(e.dma_start(out=w2_sb[:], in_=dr["w2"]))
        ins.append(e.dma_start(out=a2_sb[:], in_=dr["a2"]))
        ins.append(e.dma_start(out=g1_sb[:].rearrange("p (c m) -> p c m", m=128),
                   in_=dr["g1"].rearrange("(c p) m -> p c m", p=128)))
        ins.append(e.dma_start(out=g2_sb[:], in_=dr["g2"]))
        ins.append(e.dma_start(out=wg1_sb[:].rearrange("p (c m) -> p c m", m=256),
                   in_=dr["wg1"].rearrange("(c p) m -> p c m", p=128)))
        ins.append(e.dma_start(out=wg2_sb[:].rearrange("p (c m) -> p c m", m=40),
                   in_=dr["wg2"].rearrange("(c p) m -> p c m", p=128)))
        ins.append(e.dma_start(out=be_sb[:], in_=dr["be"]))
        return ins

    N_SMALL = len(WN1D) + 2 + 1 + 9
    tok_sm = p.dma_g("gpsimd", load_smalls, N_SMALL)

    def vtr_mm(e):
        ins = None
        for vi, nm in enumerate(VSTAGE_LIST):
            rows = 2 if nm == "bg1" else 8
            ins = e.matmul(out=BK(0)[:, vi * 8: vi * 8 + rows],
                           lhsT=vstage[0:rows, vi * 128:(vi + 1) * 128],
                           rhs=id_f32[0:rows, 0:rows],
                           start=(vi == 0), stop=(vi == len(VSTAGE_LIST) - 1))
        return ins
    tok_vt0 = p.g("tensor", vtr_mm, after=[tok_sm, tok_c0], wbank=[0])

    def vtr_ev(e):
        ins = None
        for vi, nm in enumerate(VSTAGE_LIST):
            if nm == "bg1":
                ins = e.tensor_copy(out=bg1c[:], in_=BK(0)[:, vi * 8: vi * 8 + 2])
            else:
                dst = VC["r_k"] if nm == "r_k" else VC[nm]
                ins = e.tensor_copy(out=dst[:], in_=BK(0)[:, vi * 8: vi * 8 + 8])
        return ins
    tok_vt0e = p.g("vector", vtr_ev, after=[tok_vt0], rbank=[0])

    def derive(e):
        e.tensor_scalar(out=VC["omka"][:], in0=VC["k_a"][:], scalar1=-1.0,
                        scalar2=1.0, op0=OP.mult, op1=OP.add)
        e.tensor_mul(out=w1w2c[:], in0=VC["ln_pre_w"][:], in1=VC["ln_attn_w"][:])
        e.tensor_mul(out=w2b1c[:], in0=VC["ln_pre_b"][:], in1=VC["ln_attn_w"][:])
        e.tensor_mul(out=w1sq[:], in0=VC["ln_pre_w"][:], in1=VC["ln_pre_w"][:])
        e.tensor_mul(out=w1b1[:], in0=VC["ln_pre_w"][:], in1=VC["ln_pre_b"][:])
        e.reduce_sum(out=scr_red[:, 0:1], in_=VC["ln_pre_b"][:], axis=mybir.AxisListType.X)
        e.tensor_tensor(out=gscr[:, 0:PC], in0=VC["ln_pre_b"][:], in1=VC["ln_pre_b"][:],
                        op=OP.mult)
        return e.reduce_sum(out=scr_red[:, 1:2], in_=gscr[:, 0:PC], axis=mybir.AxisListType.X)
    tok_dv = p.g("vector", derive, after=[tok_vt0e])
    tok_sb1 = p.g("tensor", lambda e: e.matmul(out=BK(0)[0:1, 0:2], lhsT=ones_col[:],
                                               rhs=scr_red[:], start=True, stop=True),
                  after=[tok_dv], wbank=[0])
    tok_sb1e = p.g("vector", lambda e: e.tensor_copy(out=sb1[:], in_=BK(0)[0:1, 0:2]),
                   after=[tok_sb1], rbank=[0])

    def src_ap(u, hh):
        nm, idx = BIGW[u]
        if nm == "ffn1":
            base = dr["W_ffn1"][:, idx * D + hh * 512: idx * D + (hh + 1) * 512]
        elif nm == "ffn2":
            base = dr["W_ffn2"][idx * D:(idx + 1) * D, hh * 512:(hh + 1) * 512]
        elif nm == "We":
            base = dr["We"][idx][:, hh * 512:(hh + 1) * 512]
        else:
            base = dr[nm][:, hh * 512:(hh + 1) * 512]
        return base.rearrange("(c p) m -> p c m", p=128)

    wload = {}
    slot_free = [None] * 2
    wb_done = {}
    st_hu = [0]

    def hu_sb(hu, kc, mc):
        return wslot[hu % 2][:, kc * 512 + mc * 128: kc * 512 + (mc + 1) * 128]

    def ensure_hu(t, hu):
        while st_hu[0] <= hu:
            cur = st_hu[0]
            s = cur % 2
            aft = [slot_free[s]]
            if t == 0:
                aft += [wb_done.get(cur - 2), tok_sm]
                tokn = p.dma_g("gpsimd",
                               (lambda e, cur=cur, s=s: [e.dma_start(
                                   out=wslot[s][:].rearrange("p (c m) -> p c m", m=512),
                                   in_=src_ap(cur // 2, cur % 2))]), 1, after=aft)
                wload[(t, cur)] = tokn
                wb_done[cur] = p.dma_g("sync",
                                       (lambda e, cur=cur, s=s: [e.dma_start(
                                           out=wcache[:, cur * 4096:(cur + 1) * 4096],
                                           in_=wslot[s][:])]), 1, after=[tokn])
            else:
                q = "sync" if cur % 2 == 0 else "scalar"
                wload[(t, cur)] = p.dma_g(q,
                                          (lambda e, cur=cur, s=s: [e.dma_start(
                                              out=wslot[s][:],
                                              in_=wcache[:, cur * 4096:(cur + 1) * 4096])]),
                                          1, after=aft)
            st_hu[0] += 1

    dbg_after = []

    def dump_bf(t, name, bft, after):
        if debug and t == dbg_tile and name in dbg:
            if os.environ.get("DK_PRINT_SEQ"):
                print("DUMP", name, "seq", p.seq)
            dbg_after.append(p.dma_g("gpsimd",
                                     (lambda e: [e.dma_start(out=dbg[name][:], in_=bft[:])]),
                                     1, after=after))

    def dump_f(t, name, ft, after):
        if debug and t == dbg_tile and name in dbg:
            if os.environ.get("DK_PRINT_SEQ"):
                print("DUMP", name, "seq", p.seq)
            dbg_after.append(p.dma_g("gpsimd",
                                     (lambda e: [e.dma_start(out=dbg[name][:], in_=ft[:])]),
                                     1, after=after))

    tok_store_last = None
    tmb_free = [None, None]

    for t in range(NT):
        st_hu[0] = 0
        ensure_hu(t, 1)

        tok_u1 = None
        for tc in range(TC):
            buf = tmb[tc % 2]
            tok_ld = p.dma_g("sync",
                             (lambda e, t=t, tc=tc, buf=buf: [e.dma_start(
                                 out=buf[:],
                                 in_=x_in[t * TILE + tc * 128: t * TILE + (tc + 1) * 128, :])]),
                             1, after=[tmb_free[0], tmb_free[1]])

            if debug and t == dbg_tile and tc == 0 and os.environ.get("DK_DUMP_BUF"):
                if os.environ.get("DK_PRINT_SEQ"):
                    print("DUMP_BUF seq", p.seq)
                dbg_after.append(p.dma_g("gpsimd",
                                         (lambda e, buf=buf: [e.dma_start(
                                             out=dbg["d_u1"][:, 0:1024], in_=buf[:])]),
                                         1, after=[tok_ld]))

            def p0s(e, tc=tc, buf=buf):
                ins = None
                for hh in range(2):
                    e.activation(out=xsq_fm[:], in_=buf[:, hh * 512:(hh + 1) * 512],
                                 func=AF.Copy, accum_out=stat_h[:, hh:hh + 1])
                    ins = e.activation(out=xsq_fm[:], in_=buf[:, hh * 512:(hh + 1) * 512],
                                       func=AF.Square, accum_out=stat_h[:, 2 + hh:3 + hh])
                return ins
            tok_s = p.g("scalar", p0s, after=[tok_ld, tok_c0])

            def p0r(e, tc=tc):
                e.tensor_add(out=stat_s[:, 0:1], in0=stat_h[:, 0:1], in1=stat_h[:, 1:2])
                e.tensor_add(out=stat_q[:, 0:1], in0=stat_h[:, 2:3], in1=stat_h[:, 3:4])
                e.tensor_scalar_mul(out=stat_m[:, 0:1], in0=stat_s[:, 0:1], scalar1=1.0 / D)
                e.tensor_scalar_mul(out=stat_q[:, 0:1], in0=stat_q[:, 0:1], scalar1=1.0 / D)
                e.tensor_tensor(out=stat_i[:, 0:1], in0=stat_m[:, 0:1], in1=stat_m[:, 0:1],
                                op=OP.mult)
                e.tensor_tensor(out=stat_q[:, 0:1], in0=stat_q[:, 0:1],
                                in1=stat_i[:, 0:1], op=OP.subtract)
                return e.tensor_scalar_add(out=stat_q[:, 0:1], in0=stat_q[:, 0:1],
                                           scalar1=1e-5)
            tok_v = p.g("vector", p0r, after=[tok_s])
            tok_q = p.g("scalar", (lambda e: e.activation(
                out=stat_q[:, 0:1], in_=stat_q[:, 0:1], func=AF.Sqrt,
                scale=1.0)), after=[tok_v])

            def p0a(e, tc=tc, buf=buf):
                e.reciprocal(out=stat_i[:, 0:1], in_=stat_q[:, 0:1])
                return e.tensor_scalar(out=buf[:], in0=buf[:],
                                       scalar1=stat_m[:, 0:1],
                                       scalar2=stat_i[:, 0:1],
                                       op0=OP.subtract, op1=OP.mult)
            tok_ap = p.g("vector", p0a, after=[tok_q])
            if debug and t == dbg_tile and tc == 0 and os.environ.get("DK_DUMP_BUF") == "3":
                if os.environ.get("DK_PRINT_SEQ"):
                    print("DUMP_BUF3 seq", p.seq)
                dbg_after.append(p.dma_g("gpsimd",
                                         (lambda e: [e.dma_start(out=dbg["d_u1"][:, 0:4], in_=stat_h[:, 0:4], allow_slow_non_contiguous=True),
                                                     e.dma_start(out=dbg["d_u1"][:, 4:5], in_=stat_m[:, 0:1], allow_slow_non_contiguous=True),
                                                     e.dma_start(out=dbg["d_u1"][:, 5:6], in_=stat_q[:, 0:1], allow_slow_non_contiguous=True),
                                                     e.dma_start(out=dbg["d_u1"][:, 6:7], in_=stat_i[:, 0:1], allow_slow_non_contiguous=True)]),
                                         4, after=[tok_ap]))
            if debug and t == dbg_tile and tc == 0 and os.environ.get("DK_DUMP_BUF") == "2":
                if os.environ.get("DK_PRINT_SEQ"):
                    print("DUMP_BUF2 seq", p.seq)
                dbg_after.append(p.dma_g("gpsimd",
                                         (lambda e, buf=buf: [e.dma_start(
                                             out=dbg["d_u1"][:, 0:1024], in_=buf[:])]),
                                         1, after=[tok_ap]))
            for gi in range(2):
                bk = (tc % 2) * 2 + gi
                def tpg(e, tc=tc, gi=gi, bk=bk, buf=buf):
                    ins = None
                    for pc4 in range(4):
                        pc = gi * 4 + pc4
                        ins = e.matmul(out=BK(bk)[:, pc4 * 128:(pc4 + 1) * 128],
                                       lhsT=buf[:, pc * 128:(pc + 1) * 128],
                                       rhs=id_f32[:], start=(pc4 == 0), stop=(pc4 == 3))
                    return ins
                tk = p.g("tensor", tpg, after=[tok_ap, tok_c0], wbank=[bk])
                def tpe(e, tc=tc, gi=gi, bk=bk):
                    ins = None
                    for pc4 in range(4):
                        pc = gi * 4 + pc4
                        ins = e.tensor_copy(out=fmc(x_fm, pc, tc),
                                            in_=BK(bk)[:, pc4 * 128:(pc4 + 1) * 128])
                    return ins
                tok_u1 = p.g("vector", tpe, after=[tk], rbank=[bk])
            tmb_free[tc % 2] = tok_u1
        dump_f(t, "d_u1", x_fm, [tok_u1])

        tk_prev = None
        for pc in range(PC):
            sq_tok = p.g("scalar", (lambda e, pc=pc: e.activation(
                out=xsq_fm[:], in_=fm(x_fm, pc), func=AF.Square)),
                after=[tok_u1 if pc == 0 else tk_prev])
            def lna_mm(e, pc=pc):
                e.matmul(out=BK(4)[0:1, 0:512], lhsT=w1b1[:, pc:pc + 1], rhs=fm(x_fm, pc),
                         start=(pc == 0), stop=(pc == PC - 1))
                e.matmul(out=BK(5)[0:1, 0:512], lhsT=w1sq[:, pc:pc + 1], rhs=xsq_fm[:],
                         start=(pc == 0), stop=(pc == PC - 1))
                return e.matmul(out=BK(6)[0:1, 0:512], lhsT=VC["ln_pre_w"][:, pc:pc + 1],
                                rhs=fm(x_fm, pc), start=(pc == 0), stop=(pc == PC - 1))
            tk_prev = p.g("tensor", lna_mm, after=[sq_tok, tok_u1],
                          wbank=[4, 5, 6] if pc == 0 else [])
        def lna_rows(e):
            e.tensor_scalar(out=lnmu, in0=BK(6)[0:1, 0:512],
                            scalar1=sb1[0:1, 0:1], scalar2=1.0 / D, op0=OP.add, op1=OP.mult)
            e.tensor_scalar_mul(out=xsq_fm[0:1, :], in0=BK(4)[0:1, 0:512], scalar1=2.0)
            e.tensor_tensor(out=xsq_fm[0:1, :], in0=xsq_fm[0:1, :], in1=BK(5)[0:1, 0:512],
                            op=OP.add)
            e.tensor_scalar(out=xsq_fm[0:1, :], in0=xsq_fm[0:1, :],
                            scalar1=sb1[0:1, 1:2], scalar2=1.0 / D, op0=OP.add, op1=OP.mult)
            e.tensor_tensor(out=s2w_f[0:1, :], in0=lnmu, in1=lnmu, op=OP.mult)
            e.tensor_tensor(out=lniv, in0=xsq_fm[0:1, :], in1=s2w_f[0:1, :],
                            op=OP.subtract)
            return e.tensor_scalar_add(out=lniv, in0=lniv, scalar1=1e-5)
        tok_lr = p.g("vector", lna_rows, after=[tk_prev, tok_sb1e], rbank=[4, 5, 6])
        _lr = [tok_lr]
        if debug and t == dbg_tile and os.environ.get("DK_DUMP_LNA"):
            _d = p.dma_g("gpsimd",
                         (lambda e: [e.dma_start(out=dbg["d_u1"][4:5, 0:512], in_=lniv)]),
                         1, after=[tok_lr])
            dbg_after.append(_d); _lr.append(_d)
        tok_lq = p.g("scalar", lambda e: e.activation(out=lniv, in_=lniv,
                                                      func=AF.Sqrt, scale=1.0),
                     after=_lr)
        _lq = [tok_lq]
        if debug and t == dbg_tile and os.environ.get("DK_DUMP_LNA"):
            _d = p.dma_g("gpsimd",
                         (lambda e: [e.dma_start(out=dbg["d_u1"][5:6, 0:512], in_=lniv)]),
                         1, after=[tok_lq])
            dbg_after.append(_d); _lq.append(_d)
        def lna_i(e):
            e.reciprocal(out=lniv, in_=lniv)
            e.tensor_tensor(out=s2w_f[0:1, :], in0=lnmu, in1=lniv, op=OP.mult)
            return e.tensor_scalar_mul(out=s2w_f[0:1, :], in0=s2w_f[0:1, :], scalar1=-1.0)
        tok_li = p.g("vector", lna_i, after=_lq)

        if debug and t == dbg_tile and os.environ.get("DK_DUMP_LNA"):
            if os.environ.get("DK_PRINT_SEQ"):
                print("DUMP_LNA seq", p.seq)
            dbg_after.append(p.dma_g("gpsimd",
                                     (lambda e: [e.dma_start(out=dbg["d_u1"][0:1, 0:512], in_=lnmu),
                                                 e.dma_start(out=dbg["d_u1"][1:2, 0:512], in_=lniv),
                                                 e.dma_start(out=dbg["d_u1"][2:3, 0:512], in_=s2w_f[0:1, :]),
                                                 e.dma_start(out=dbg["d_u1"][3:4, 0:2], in_=sb1[0:1, :]),
                                                 e.dma_start(out=dbg["d_u1"][:, 8:10], in_=scr_red[:, 0:2], allow_slow_non_contiguous=True),
                                                 e.dma_start(out=dbg["d_u1"][:, 16:16 + PC], in_=gscr[:, 0:PC], allow_slow_non_contiguous=True)]),
                                     6, after=[tok_li]))
        def lna_bc(e):
            e.matmul(out=BK(5), lhsT=ones_r128[:], rhs=lniv, start=True, stop=True)
            return e.matmul(out=BK(6), lhsT=ones_r128[:], rhs=s2w_f[0:1, :],
                            start=True, stop=True)
        tok_lbc = p.g("tensor", lna_bc, after=[tok_li], wbank=[5, 6])
        tok_xn_last = None
        for pc in range(PC):
            def lna_xn(e, pc=pc):
                e.tensor_scalar(out=xsq_fm[:], in0=BK(5),
                                scalar1=w2b1c[:, pc:pc + 1],
                                scalar2=VC["ln_attn_b"][:, pc:pc + 1],
                                op0=OP.mult, op1=OP.add)
                e.scalar_tensor_tensor(out=xsq_fm[:], in0=BK(6),
                                       scalar=VC["ln_attn_w"][:, pc:pc + 1],
                                       in1=xsq_fm[:], op0=OP.mult, op1=OP.add)
                e.tensor_tensor(out=bt2_f[:], in0=fm(x_fm, pc), in1=BK(5), op=OP.mult)
                return e.scalar_tensor_tensor(out=fm(xn_bf, pc), in0=bt2_f[:],
                                              scalar=w1w2c[:, pc:pc + 1],
                                              in1=xsq_fm[:], op0=OP.mult, op1=OP.add)
            tok_xn_last = p.g("vector", (lambda e, pc=pc: lna_xn(e, pc)),
                              after=[tok_lbc], rbank=[5, 6])
        dump_bf(t, "d_xn", xn_bf, [tok_xn_last])

        def p_dx(e):
            ins = None
            for pc in range(PC):
                e.tensor_sub(out=fm(dx_bf, pc)[:, 1:TILE], in0=fm(xn_bf, pc)[:, 0:TILE - 1],
                             in1=fm(xn_bf, pc)[:, 1:TILE])
                e.tensor_sub(out=fm(dx_bf, pc)[:, 0:1], in0=xn_halo[:, pc:pc + 1],
                             in1=fm(xn_bf, pc)[:, 0:1])
                ins = e.tensor_copy(out=xn_halo[:, pc:pc + 1],
                                    in_=fm(xn_bf, pc)[:, TILE - 1:TILE])
            return ins
        tok_dx = p.g("vector", p_dx, after=[tok_xn_last])

        def dense_mm(hu0, rhs_of, evac_fn, evac_eng, gname=None, after0=()):
            ensure_hu(t, hu0 + 1)
            mm_toks = []
            for kc in range(PC):
                mm_after = []
                if gname is not None:
                    def mkmix(e, kc=kc):
                        return e.scalar_tensor_tensor(
                            out=mixp[kc % 2][:], in0=fm(dx_bf, kc),
                            scalar=VC[gname][:, kc:kc + 1], in1=fm(xn_bf, kc),
                            op0=OP.mult, op1=OP.add)
                    aft = [tok_dx] + list(after0)
                    if kc >= 2:
                        aft.append(mm_toks[kc - 2])
                    mm_after.append(p.g("vector", mkmix, after=aft))
                def mmk(e, kc=kc, hu0=hu0):
                    ins = None
                    rhs = mixp[kc % 2][:] if gname is not None else rhs_of(kc)
                    for mc in range(8):
                        ins = e.matmul(out=BK(mc), lhsT=hu_sb(hu0 + mc // 4, kc, mc % 4),
                                       rhs=rhs, start=(kc == 0), stop=(kc == PC - 1))
                    return ins
                mm_after += [wload[(t, hu0)], wload[(t, hu0 + 1)]] + list(after0)
                mm_toks.append(p.g("tensor", mmk, after=mm_after,
                                   wbank=list(range(8)) if kc == 0 else []))
            def evall(e):
                ins = None
                for mc in range(8):
                    ins = evac_fn(e, mc, BK(mc))
                return ins
            ev = p.g(evac_eng, evall, after=[mm_toks[-1]], rbank=list(range(8)))
            slot_free[hu0 % 2] = mm_toks[-1]
            slot_free[(hu0 + 1) % 2] = mm_toks[-1]
            return mm_toks[-1], ev

        mm_r, ev_r_t = dense_mm(0, None, (lambda e, mc, bk: e.tensor_copy(
            out=fm(r_bf, mc), in_=bk)), "vector", gname="x_r")

        def ev_k(e, mc, bk):
            e.tensor_copy(out=fm(km_bf, mc), in_=bk)
            return e.tensor_scalar_mul(out=fm(kk_bf, mc), in0=bk,
                                       scalar1=VC["k_k"][:, mc:mc + 1])
        mm_k, ev_k_t = dense_mm(2, None, ev_k, "vector", gname="x_k")

        mm_v, ev_v_t = dense_mm(4, None, (lambda e, mc, bk: e.tensor_copy(
            out=fm(v_bf, mc), in_=bk)), "vector", gname="x_v")

        def build_mix_full(gname, aft):
            def mk(e):
                ins = None
                for kc in range(PC):
                    ins = e.scalar_tensor_tensor(
                        out=fm(o_bf, kc), in0=fm(dx_bf, kc),
                        scalar=VC[gname][:, kc:kc + 1], in1=fm(xn_bf, kc),
                        op0=OP.mult, op1=OP.add)
                return ins
            return p.g("vector", mk, after=aft)

        tok_mixw = build_mix_full("x_w", [ev_v_t])
        def w1mm(e):
            ins = None
            for kc in range(PC):
                ins = e.matmul(out=BK(0)[0:64, :], lhsT=w1_sb[:, kc * 64:(kc + 1) * 64],
                               rhs=fm(o_bf, kc), start=(kc == 0), stop=(kc == PC - 1))
            return ins
        tok_w1 = p.g("tensor", w1mm, after=[tok_mixw], wbank=[0])
        tok_w1e = p.g("scalar", lambda e: e.activation(out=t1_bf[:], in_=BK(0)[0:64, :],
                                                       func=AF.Tanh),
                      after=[tok_w1], rbank=[0])
        tok_we = None
        for wv in range(2):
            def w2mm(e, wv=wv):
                ins = None
                for mc4 in range(4):
                    mc = wv * 4 + mc4
                    ins = e.matmul(out=BK(4 + mc4), lhsT=w2_sb[:, mc * 128:(mc + 1) * 128],
                                   rhs=t1_bf[:], start=True, stop=True)
                return ins
            mt = p.g("tensor", w2mm, after=[tok_w1e], wbank=[4, 5, 6, 7])
            def w2ev(e, wv=wv):
                ins = None
                for mc4 in range(4):
                    mc = wv * 4 + mc4
                    ins = e.activation(out=fm(e_bf, mc), in_=BK(4 + mc4),
                                       func=AF.Sigmoid, bias=VC["w0"][:, mc:mc + 1])
                return ins
            tok_we = p.g("scalar", w2ev, after=[mt], rbank=[4, 5, 6, 7])
        tok_mixa = build_mix_full("x_a", [tok_w1])
        def a1mm(e):
            ins = None
            for kc in range(PC):
                ins = e.matmul(out=BK(1)[0:64, :], lhsT=a1_sb[:, kc * 64:(kc + 1) * 64],
                               rhs=fm(o_bf, kc), start=(kc == 0), stop=(kc == PC - 1))
            return ins
        tok_a1 = p.g("tensor", a1mm, after=[tok_mixa], wbank=[1])
        tok_a1e = p.g("scalar", lambda e: e.copy(out=t1_bf[:], in_=BK(1)[0:64, :]),
                      after=[tok_a1, tok_we], rbank=[1])
        tok_ae = None
        for wv in range(2):
            def a2mm(e, wv=wv):
                ins = None
                for mc4 in range(4):
                    mc = wv * 4 + mc4
                    ins = e.matmul(out=BK(mc4), lhsT=a2_sb[:, mc * 128:(mc + 1) * 128],
                                   rhs=t1_bf[:], start=True, stop=True)
                return ins
            mt = p.g("tensor", a2mm, after=[tok_a1e], wbank=[0, 1, 2, 3])
            def a2ev(e, wv=wv):
                ins = None
                for mc4 in range(4):
                    mc = wv * 4 + mc4
                    ins = e.activation(out=fm(b_bf, mc), in_=BK(mc4),
                                       func=AF.Sigmoid, bias=VC["a0"][:, mc:mc + 1])
                return ins
            tok_ae = p.g("scalar", a2ev, after=[mt], rbank=[0, 1, 2, 3])
        tok_mixg = build_mix_full("x_g", [tok_a1])
        def g1mm(e):
            ins = None
            for kc in range(PC):
                ins = e.matmul(out=BK(2), lhsT=g1_sb[:, kc * 128:(kc + 1) * 128],
                               rhs=fm(o_bf, kc), start=(kc == 0), stop=(kc == PC - 1))
            return ins
        tok_g1 = p.g("tensor", g1mm, after=[tok_mixg], wbank=[2])
        tok_g1e = p.g("scalar", lambda e: e.activation(out=sg_bf[:], in_=BK(2),
                                                       func=AF.Sigmoid),
                      after=[tok_g1], rbank=[2])
        tok_ge = None
        for wv in range(2):
            def g2mm(e, wv=wv):
                ins = None
                for mc4 in range(4):
                    mc = wv * 4 + mc4
                    ins = e.matmul(out=BK(4 + mc4), lhsT=g2_sb[:, mc * 128:(mc + 1) * 128],
                                   rhs=sg_bf[:], start=True, stop=True)
                return ins
            mt = p.g("tensor", g2mm, after=[tok_g1e], wbank=[4, 5, 6, 7])
            def g2ev(e, wv=wv):
                ins = None
                for mc4 in range(4):
                    mc = wv * 4 + mc4
                    ins = e.copy(out=fm(g_bf, mc), in_=BK(4 + mc4))
                return ins
            tok_ge = p.g("scalar", g2ev, after=[mt], rbank=[4, 5, 6, 7])

        mm_sq_tok = None
        for kc in range(PC):
            sq_t = p.g("vector", (lambda e, kc=kc: e.tensor_mul(
                out=pp_bf[:], in0=fm(kk_bf, kc), in1=fm(kk_bf, kc))),
                after=[ev_k_t] if kc == 0 else [mm_sq_tok])
            mm_sq_tok = p.g("tensor", (lambda e, kc=kc: e.matmul(
                out=BK(3)[0:16, :], lhsT=seg16b[:, kc * 16:(kc + 1) * 16],
                rhs=pp_bf[:], start=(kc == 0), stop=(kc == PC - 1))),
                after=[sq_t], wbank=[3] if kc == 0 else [])
        tok_kn1 = p.g("scalar", lambda e: e.activation(out=hrow[:, :], in_=BK(3)[0:16, :],
                                                       func=AF.Sqrt),
                      after=[mm_sq_tok], rbank=[3])
        def kkn2(e):
            e.tensor_scalar_max(out=hrow[:, :], in0=hrow[:, :], scalar1=1e-12)
            return e.reciprocal(out=hrow2[:, :], in_=hrow[:, :])
        tok_kn2 = p.g("vector", kkn2, after=[tok_kn1])
        tok_kkn = None
        for pc in range(PC):
            mt = p.g("tensor", (lambda e, pc=pc: e.matmul(
                out=BK(2), lhsT=bc16[0:16, pc * 128:(pc + 1) * 128],
                rhs=hrow2[0:16, :], start=True, stop=True)),
                after=[tok_kn2], wbank=[2])
            def kkap(e, pc=pc):
                e.tensor_tensor(out=fm(kk_bf, pc), in0=fm(kk_bf, pc), in1=BK(2), op=OP.mult)
                e.tensor_scalar(out=pp_bf[:], in0=fm(b_bf, pc),
                                scalar1=VC["k_a"][:, pc:pc + 1],
                                scalar2=VC["omka"][:, pc:pc + 1], op0=OP.mult, op1=OP.add)
                e.tensor_tensor(out=fm(km_bf, pc), in0=fm(km_bf, pc), in1=pp_bf[:], op=OP.mult)
                return e.tensor_tensor(out=fm(b_bf, pc), in0=fm(kk_bf, pc), in1=fm(b_bf, pc),
                                       op=OP.mult)
            tok_kkn = p.g("vector", (lambda e, pc=pc: kkap(e, pc)), after=[mt, tok_ae],
                          rbank=[2])
        dump_bf(t, "d_r", r_bf, [ev_r_t])
        dump_bf(t, "d_e", e_bf, [tok_we])
        dump_bf(t, "d_kk", kk_bf, [tok_kkn])
        dump_bf(t, "d_kmod", km_bf, [tok_kkn])
        dump_bf(t, "d_v", v_bf, [ev_v_t])
        dump_bf(t, "d_b", b_bf, [tok_kkn])
        dump_bf(t, "d_g", g_bf, [tok_ge])

        def tilG(nm, j, c):
            return TIL[nm][:, j * TILE + c * 128: j * TILE + (c + 1) * 128]

        def e2G(j, c):
            return E2G[:, j * TILE + c * 128: j * TILE + (c + 1) * 128]

        tok_grp = tok_kkn
        tok_oev = None
        for G in range(2):
            til_last = None
            for j in range(4):
                pc = 4 * G + j
                tk_cs = p.g("vector", (lambda e, pc=pc: e.tensor_tensor_scan(
                    out=cs_f[:], data0=rscan[:], data1=fm(e_bf, pc), initial=0.0,
                    op0=OP.mult, op1=OP.add)),
                    after=[tok_we, tok_grp, til_last])
                def mkcse(e, pc=pc):
                    e.tensor_tensor(out=cse_f[:], in0=cs_f[:], in1=fm(e_bf, pc),
                                    op=OP.subtract)
                    return e.tensor_copy(
                        out=csL_f[:, 0:TC],
                        in_=AP(cs_f.ap().tensor, 127, [[TILE, 128], [128, TC]]))
                tk_c2 = p.g("vector", mkcse, after=[tk_cs])
                def mkexp1(e, pc=pc, j=j):
                    e.activation(out=dec[0][:], in_=cs_f[:], func=AF.Exp, scale=-SC)
                    e.activation(out=dec[1][:], in_=cse_f[:], func=AF.Exp, scale=-SC)
                    return e.activation(out=DLG[:, j * TC:(j + 1) * TC],
                                        in_=csL_f[:, 0:TC], func=AF.Exp, scale=-SC)
                tk_ex = p.g("scalar", mkexp1, after=[tk_c2])
                def mktil1(e, pc=pc, j=j):
                    sl = slice(j * TILE, (j + 1) * TILE)
                    e.tensor_mul(out=TIL["rt"][:, sl], in0=fm(r_bf, pc), in1=dec[0][:])
                    e.scalar_tensor_tensor(out=TIL["at"][:, sl], in0=fm(kk_bf, pc),
                                           scalar=-1.0, in1=dec[1][:],
                                           op0=OP.mult, op1=OP.mult)
                    ins = None
                    for cq in range(TC):
                        ins = e.tensor_scalar(
                            out=cse_f[:, cq * 128:(cq + 1) * 128],
                            in0=cs_f[:, cq * 128:(cq + 1) * 128],
                            scalar1=csL_f[:, cq:cq + 1], scalar2=None, op0=OP.subtract)
                    return ins
                tk_t1 = p.g("vector", mktil1, after=[tk_ex])
                def mkexp2(e, pc=pc, j=j):
                    e.activation(out=dec[0][:], in_=cs_f[:], func=AF.Exp, scale=SC)
                    return e.activation(out=E2G[:, j * TILE:(j + 1) * TILE], in_=cse_f[:],
                                        func=AF.Exp, scale=SC)
                tk_e2 = p.g("scalar", mkexp2, after=[tk_t1])
                def mktil2(e, pc=pc, j=j):
                    sl = slice(j * TILE, (j + 1) * TILE)
                    e.tensor_mul(out=TIL["bt"][:, sl], in0=fm(b_bf, pc), in1=dec[0][:])
                    return e.tensor_mul(out=TIL["kt"][:, sl], in0=fm(km_bf, pc), in1=dec[0][:])
                til_last = p.g("vector", mktil2, after=[tk_e2])
            tok_til = til_last

            tok_state_ev = None
            for c in range(TC):
                def ghmm(e, G=G, c=c):
                    ins = None
                    for q, (dst, sa, sb) in enumerate(
                            [(PSA, "bt", "at"), (PSB, "at", "bt"), (PSC, "kt", "at")]):
                        for hf in range(2):
                            for j in range(4):
                                sl = 4 * hf + j
                                ins = e.matmul(
                                    out=dst[:, sl * 128:(sl + 1) * 128],
                                    lhsT=tilG(sa, j, c)[hf * 64:(hf + 1) * 64, :],
                                    rhs=tilG(sb, j, c)[hf * 64:(hf + 1) * 64, :],
                                    start=(j == 0), stop=(j == 3),
                                    tile_position=(64 * hf, 0))
                    return ins
                tok_gh = p.g("tensor", ghmm, after=[tok_til],
                             wbank=[0, 1, 2, 3, 4, 5])
                tok_hme = p.g("vector", lambda e: e.tensor_tensor(
                    out=h8(Hm_sb), in0=ps_h8(PSC), in1=mask8(msk_su), op=OP.mult),
                              after=[tok_gh], rbank=[4, 5])
                def ghev(e):
                    e.tensor_tensor(out=h8(X_sb), in0=ps_h8(PSA), in1=mask8(msk_su),
                                    op=OP.mult)
                    return e.tensor_tensor(out=h8(XT_sb), in0=ps_h8(PSB), in1=mask8(msk_sl),
                                           op=OP.mult)
                tok_ghe = p.g("vector", ghev, after=[tok_gh], rbank=[0, 1, 2, 3])
                def vtmm(e, G=G, c=c):
                    ins = None
                    for hf in range(2):
                        for j in range(4):
                            ins = e.matmul(
                                out=BK(6 + hf)[:, j * 64:(j + 1) * 64],
                                lhsT=fmc(v_bf, 4 * G + j, c)[hf * 64:(hf + 1) * 64, :],
                                rhs=id_bf[hf * 64:(hf + 1) * 64, hf * 64:hf * 64 + 64],
                                start=(j == 0), stop=(j == 3),
                                tile_position=(64 * hf, 0))
                    return ins
                tok_vt = p.g("tensor", vtmm, after=[], wbank=[6, 7])
                def vtev(e):
                    e.copy(out=vtm_sb[:, 0:256], in_=BK(6)[:, 0:256])
                    return e.copy(out=vtm_sb[:, 256:512], in_=BK(7)[:, 0:256])
                tok_vte = p.g("scalar", vtev, after=[tok_vt], rbank=[6, 7])
                def rhsA(e, G=G, c=c):
                    ins = None
                    for hf in range(2):
                        for j in range(4):
                            ins = e.matmul(
                                out=BK(hf)[hf * 64:(hf + 1) * 64, j * 128:(j + 1) * 128],
                                lhsT=sT_bf[hf * 64:(hf + 1) * 64,
                                           (4 * G + j) * 64:(4 * G + j + 1) * 64],
                                rhs=tilG("at", j, c)[hf * 64:(hf + 1) * 64, :],
                                start=(j == 0), stop=(j == 3),
                                tile_position=(64 * hf, 64 * hf))
                    return ins
                tok_rhsA = p.g("tensor", rhsA, after=[tok_state_ev, tok_til],
                               wbank=[0, 1])
                def rhs0(e):
                    e.copy(out=pp_bf[0:64, :], in_=BK(0)[0:64, 0:512])
                    return e.copy(out=pp_bf[64:128, :], in_=BK(1)[64:128, 0:512])
                tok_rhs0 = p.g("scalar", rhs0, after=[tok_rhsA], rbank=[0, 1])
                def rhsB(e, G=G, c=c):
                    ins = None
                    for hf in range(2):
                        for j in range(4):
                            sl = 4 * hf + j
                            ins = e.matmul(
                                out=BK(2)[hf * 64:(hf + 1) * 64, j * 128:(j + 1) * 128],
                                lhsT=vtm_sb[:, sl * 64:(sl + 1) * 64],
                                rhs=Hm_sb[:, sl * 128:(sl + 1) * 128],
                                start=(j == 0), stop=(j == 3),
                                tile_position=(0, 64 * hf))
                    return ins
                tok_rhs = p.g("tensor", rhsB, after=[tok_vte, tok_hme], wbank=[2])
                tok_rhse = p.g("vector", lambda e: e.tensor_tensor(
                    out=rhs_sb, in0=BK(2)[:, 0:512], in1=pp_bf[:], op=OP.add),
                               after=[tok_rhs0, tok_rhs], rbank=[2])
                def u0mm(e):
                    ins = None
                    for hf in range(2):
                        for j in range(4):
                            ins = e.matmul(
                                out=BK(hf)[:, j * 64:(j + 1) * 64],
                                lhsT=rhs_sb[hf * 64:(hf + 1) * 64, j * 128:(j + 1) * 128],
                                rhs=id_bf[hf * 64:(hf + 1) * 64, hf * 64:hf * 64 + 64],
                                start=(j == 0), stop=(j == 3),
                                tile_position=(64 * hf, 0))
                    return ins
                tok_u0 = p.g("tensor", u0mm, after=[tok_rhse], wbank=[0, 1])
                def u0ev(e):
                    e.copy(out=u_sb[:, 0:256], in_=BK(0)[:, 0:256])
                    return e.copy(out=u_sb[:, 256:512], in_=BK(1)[:, 0:256])
                tok_ue = p.g("scalar", u0ev, after=[tok_u0], rbank=[0, 1])
                tok_xe = None
                for ks in range(7):
                    do_sq = ks < 6
                    def ummm(e):
                        ins = None
                        for hg in range(8):
                            ins = e.matmul(out=BK(0)[:, hg * 64:(hg + 1) * 64],
                                           lhsT=X_sb[:, hg * 128:(hg + 1) * 128],
                                           rhs=u_sb[:, hg * 64:(hg + 1) * 64],
                                           start=(hg == 0), stop=(hg == 7))
                        return ins
                    tok_um = p.g("tensor", ummm,
                                 after=[tok_ue, tok_xeX if ks > 0 else tok_xe],
                                 wbank=[0])
                    tok_sq = None
                    if do_sq:
                        def sqmm(e, ks=ks):
                            ins = None
                            for hg in range(8):
                                ins = e.matmul(out=PSB[:, hg * 128:(hg + 1) * 128],
                                               lhsT=XT_sb[:, hg * 128:(hg + 1) * 128],
                                               rhs=X_sb[:, hg * 128:(hg + 1) * 128],
                                               start=(hg % 4 == 0), stop=(hg % 4 == 3))
                            if ks < 5:
                                for hg in range(8):
                                    ins = e.matmul(out=PSC[:, hg * 128:(hg + 1) * 128],
                                                   lhsT=X_sb[:, hg * 128:(hg + 1) * 128],
                                                   rhs=XT_sb[:, hg * 128:(hg + 1) * 128],
                                                   start=(hg % 4 == 0), stop=(hg % 4 == 3))
                            return ins
                        tok_sq = p.g("tensor", sqmm, after=[tok_xe],
                                     wbank=[2, 3] + ([4, 5] if ks < 5 else []))
                    tok_ue = p.g("vector", lambda e: e.tensor_tensor(
                        out=u_sb, in0=BK(0)[:, 0:512], in1=u_sb, op=OP.add),
                                 after=[tok_um], rbank=[0])
                    if do_sq:
                        tok_xeX = p.g("vector",
                                      lambda e: e.tensor_copy(out=X_sb, in_=PSB[:]),
                                      after=[tok_sq], rbank=[2, 3])
                        if ks < 5:
                            tok_xe = p.g("vector",
                                         lambda e: e.tensor_copy(out=XT_sb, in_=PSC[:]),
                                         after=[tok_sq], rbank=[4, 5])
                        else:
                            tok_xe = tok_xeX
                def m12mm(e, G=G, c=c):
                    ins = None
                    for q, (dst, sa) in enumerate([(PSB, "bt"), (PSC, "kt")]):
                        for hf in range(2):
                            for j in range(4):
                                sl = 4 * hf + j
                                ins = e.matmul(
                                    out=dst[:, sl * 128:(sl + 1) * 128],
                                    lhsT=tilG(sa, j, c)[hf * 64:(hf + 1) * 64, :],
                                    rhs=tilG("rt", j, c)[hf * 64:(hf + 1) * 64, :],
                                    start=(j == 0), stop=(j == 3),
                                    tile_position=(64 * hf, 0))
                    return ins
                tok_m12 = p.g("tensor", m12mm, after=[tok_xe], wbank=[2, 3, 4, 5])
                def m12ev(e):
                    e.tensor_tensor(out=h8(M1_sb), in0=ps_h8(PSB), in1=mask8(msk_iu),
                                    op=OP.mult)
                    return e.tensor_tensor(out=h8(M2_sb), in0=ps_h8(PSC), in1=mask8(msk_iu),
                                           op=OP.mult)
                tok_m12e = p.g("vector", m12ev, after=[tok_m12], rbank=[2, 3, 4, 5])
                def outA(e, G=G, c=c):
                    ins = None
                    for hf in range(2):
                        for j in range(4):
                            ins = e.matmul(
                                out=BK(hf)[hf * 64:(hf + 1) * 64, j * 128:(j + 1) * 128],
                                lhsT=sT_bf[hf * 64:(hf + 1) * 64,
                                           (4 * G + j) * 64:(4 * G + j + 1) * 64],
                                rhs=tilG("rt", j, c)[hf * 64:(hf + 1) * 64, :],
                                start=(j == 0), stop=(j == 3),
                                tile_position=(64 * hf, 64 * hf))
                    return ins
                tok_outA = p.g("tensor", outA, after=[tok_state_ev, tok_til],
                               wbank=[0, 1])
                def out0(e):
                    e.copy(out=pp_bf[0:64, :], in_=BK(0)[0:64, 0:512])
                    return e.copy(out=pp_bf[64:128, :], in_=BK(1)[64:128, 0:512])
                tok_out0 = p.g("scalar", out0, after=[tok_outA], rbank=[0, 1])
                def outB(e, G=G, c=c):
                    ins = None
                    for hf in range(2):
                        for j in range(4):
                            sl = 4 * hf + j
                            e.matmul(out=BK(6)[hf * 64:(hf + 1) * 64, j * 128:(j + 1) * 128],
                                     lhsT=u_sb[:, sl * 64:(sl + 1) * 64],
                                     rhs=M1_sb[:, sl * 128:(sl + 1) * 128],
                                     start=(j == 0), stop=False, tile_position=(0, 64 * hf))
                        for j in range(4):
                            sl = 4 * hf + j
                            ins = e.matmul(out=BK(6)[hf * 64:(hf + 1) * 64,
                                                     j * 128:(j + 1) * 128],
                                           lhsT=vtm_sb[:, sl * 64:(sl + 1) * 64],
                                           rhs=M2_sb[:, sl * 128:(sl + 1) * 128],
                                           start=False, stop=(j == 3),
                                           tile_position=(0, 64 * hf))
                    return ins
                tok_out = p.g("tensor", outB, after=[tok_m12e, tok_ue], wbank=[6])
                def outev(e, G=G, c=c):
                    return e.tensor_tensor(
                        out=AP(o_bf.ap().tensor, (4 * G) * TILE + c * 128,
                               [[PC * TILE, 128], [TILE, 4], [1, 128]]),
                        in0=BK(6).rearrange("p (j x) -> p j x", j=4),
                        in1=pp_bf[:].rearrange("p (j x) -> p j x", j=4), op=OP.add)
                tok_oev = p.g("vector", outev, after=[tok_out0], rbank=[6])
                def bbkbf0(e, G=G, c=c):
                    ins = None
                    for j in range(4):
                        ins = e.tensor_mul(out=bbkb[:, j * 128:(j + 1) * 128],
                                           in0=fmc(b_bf, 4 * G + j, c), in1=e2G(j, c))
                    return ins
                tok_bk0 = p.g("vector", bbkbf0, after=[tok_til])
                def bbkbf1(e, G=G, c=c):
                    ins = None
                    for j in range(4):
                        ins = e.tensor_mul(out=bbkb[:, 512 + j * 128: 512 + (j + 1) * 128],
                                           in0=fmc(km_bf, 4 * G + j, c), in1=e2G(j, c))
                    return ins
                tok_bk = p.g("vector", bbkbf1, after=[tok_til])
                def mksttr(q):
                    def sttr(e, q=q):
                        ins = None
                        for hf in range(2):
                            for j in range(4):
                                ins = e.matmul(
                                    out=BK(2 + 2 * q + hf)[:, j * 64:(j + 1) * 64],
                                    lhsT=bbkb[hf * 64:(hf + 1) * 64,
                                              q * 512 + j * 128: q * 512 + (j + 1) * 128],
                                    rhs=id_bf[hf * 64:(hf + 1) * 64, hf * 64:hf * 64 + 64],
                                    start=(j == 0), stop=(j == 3),
                                    tile_position=(64 * hf, 0))
                        return ins
                    return sttr
                tok_st1a = p.g("tensor", mksttr(0), after=[tok_bk0], wbank=[2, 3])
                def sttre0(e):
                    e.copy(out=btm_sb[:, 0:256], in_=BK(2)[:, 0:256])
                    return e.copy(out=btm_sb[:, 256:512], in_=BK(3)[:, 0:256])
                tok_st1e0 = p.g("scalar", sttre0, after=[tok_st1a], rbank=[2, 3])
                tok_st1 = p.g("tensor", mksttr(1), after=[tok_bk], wbank=[4, 5])
                def sttre(e):
                    e.copy(out=ktm_sb[:, 0:256], in_=BK(4)[:, 0:256])
                    return e.copy(out=ktm_sb[:, 256:512], in_=BK(5)[:, 0:256])
                tok_st1e = p.g("scalar", sttre, after=[tok_st1], rbank=[4, 5])
                def stprod(e, G=G, c=c):
                    ins = None
                    for hf in range(2):
                        for j in range(4):
                            sl = 4 * hf + j
                            e.matmul(out=BK(6)[hf * 64:(hf + 1) * 64,
                                               j * 64:(j + 1) * 64],
                                     lhsT=btm_sb[:, sl * 64:(sl + 1) * 64],
                                     rhs=u_sb[:, sl * 64:(sl + 1) * 64],
                                     start=(j == 0), stop=False,
                                     tile_position=(0, 64 * hf))
                        for j in range(4):
                            sl = 4 * hf + j
                            ins = e.matmul(out=BK(6)[hf * 64:(hf + 1) * 64,
                                                     j * 64:(j + 1) * 64],
                                           lhsT=ktm_sb[:, sl * 64:(sl + 1) * 64],
                                           rhs=vtm_sb[:, sl * 64:(sl + 1) * 64],
                                           start=False, stop=(j == 3),
                                           tile_position=(0, 64 * hf))
                    return ins
                tok_st2 = p.g("tensor", stprod, after=[tok_st1e0, tok_st1e], wbank=[6])
                def stev(e, G=G, c=c):
                    for j in range(4):
                        e.scalar_tensor_tensor(
                            out=sT[:, (4 * G + j) * 64:(4 * G + j + 1) * 64],
                            in0=sT[:, (4 * G + j) * 64:(4 * G + j + 1) * 64],
                            scalar=DLG[:, j * TC + c: j * TC + c + 1],
                            in1=BK(6)[:, j * 64:(j + 1) * 64],
                            op0=OP.mult, op1=OP.add)
                    return e.tensor_copy(
                        out=sT_bf[:, (4 * G) * 64:(4 * G + 4) * 64],
                        in_=sT[:, (4 * G) * 64:(4 * G + 4) * 64])
                tok_state_ev = p.g("vector", stev, after=[tok_st2], rbank=[6])
            tok_grp = tok_state_ev
        tok_wkv = tok_grp
        dump_bf(t, "d_o", o_bf, [tok_wkv, tok_oev])

        gs_tok = None
        for pc in range(PC):
            gs_tok = p.g("tensor", (lambda e, pc=pc: e.matmul(
                out=BK(0)[0:16, :], lhsT=seg16b[:, pc * 16:(pc + 1) * 16],
                rhs=fm(o_bf, pc), start=(pc == 0), stop=(pc == PC - 1))),
                after=[tok_wkv, tok_oev], wbank=[0] if pc == 0 else [])
        gq_tok = None
        for pc in range(PC):
            sq2 = p.g("vector", (lambda e, pc=pc: e.tensor_mul(
                out=pp_bf[:], in0=fm(o_bf, pc), in1=fm(o_bf, pc))),
                after=[tok_wkv, tok_oev] if pc == 0 else [gq_tok])
            gq_tok = p.g("tensor", (lambda e, pc=pc: e.matmul(
                out=BK(1)[0:16, :], lhsT=seg16b[:, pc * 16:(pc + 1) * 16],
                rhs=pp_bf[:], start=(pc == 0), stop=(pc == PC - 1))),
                after=[sq2], wbank=[1] if pc == 0 else [])
        def gnrows(e):
            e.tensor_scalar_mul(out=hrow3[:, :], in0=BK(0)[0:16, :], scalar1=1.0 / N)
            e.tensor_scalar_mul(out=hrow2[:, :], in0=BK(1)[0:16, :], scalar1=1.0 / N)
            e.tensor_tensor(out=hrow[:, :], in0=hrow3[:, :], in1=hrow3[:, :], op=OP.mult)
            e.tensor_tensor(out=hrow2[:, :], in0=hrow2[:, :], in1=hrow[:, :],
                            op=OP.subtract)
            return e.tensor_scalar_add(out=hrow2[:, :], in0=hrow2[:, :], scalar1=0.00064)
        tok_gr = p.g("vector", gnrows, after=[gs_tok, gq_tok], rbank=[0, 1])
        tok_gq = p.g("scalar", lambda e: e.activation(out=hrow2[:, :], in_=hrow2[:, :],
                                                      func=AF.Sqrt, scale=1.0),
                     after=[tok_gr])
        tok_gi = p.g("vector", lambda e: e.reciprocal(out=hrow2[:, :], in_=hrow2[:, :]),
                     after=[tok_gq])
        rk_tok = None
        mt_prev = tok_gi
        for pc in range(PC):
            mt = p.g("vector", (lambda e, pc=pc: e.scalar_tensor_tensor(
                out=pp_bf[:], in0=fm(r_bf, pc), scalar=VC["r_k"][:, pc:pc + 1],
                in1=fm(km_bf, pc), op0=OP.mult, op1=OP.mult)), after=[mt_prev])
            rk_tok = p.g("tensor", (lambda e, pc=pc: e.matmul(
                out=BK(2)[0:16, :], lhsT=seg16b[:, pc * 16:(pc + 1) * 16],
                rhs=pp_bf[:], start=(pc == 0), stop=(pc == PC - 1))),
                after=[mt], wbank=[2] if pc == 0 else [])
            mt_prev = rk_tok
        tok_rkr = p.g("vector", lambda e: e.tensor_copy(out=hrow[:, :], in_=BK(2)[0:16, :]),
                      after=[rk_tok], rbank=[2])
        tok_og = None
        for pc in range(PC):
            def gnbc(e, pc=pc):
                e.matmul(out=BK(3), lhsT=bc16[0:16, pc * 128:(pc + 1) * 128],
                         rhs=hrow2[0:16, :], start=True, stop=True)
                e.matmul(out=BK(4), lhsT=bc16[0:16, pc * 128:(pc + 1) * 128],
                         rhs=hrow3[0:16, :], start=True, stop=True)
                return e.matmul(out=BK(5), lhsT=bc16[0:16, pc * 128:(pc + 1) * 128],
                                rhs=hrow[0:16, :], start=True, stop=True)
            mt = p.g("tensor", gnbc, after=[tok_gi, tok_rkr], wbank=[3, 4, 5])
            def gnap(e, pc=pc):
                e.tensor_sub(out=s2w_f[:], in0=fm(o_bf, pc), in1=BK(4))
                e.tensor_tensor(out=s2w_f[:], in0=s2w_f[:], in1=BK(3), op=OP.mult)
                e.tensor_scalar(out=s2w_f[:], in0=s2w_f[:],
                                scalar1=VC["gn_w"][:, pc:pc + 1],
                                scalar2=VC["gn_b"][:, pc:pc + 1], op0=OP.mult, op1=OP.add)
                e.tensor_tensor(out=bt2_f[:], in0=BK(5), in1=fm(v_bf, pc), op=OP.mult)
                e.tensor_tensor(out=s2w_f[:], in0=s2w_f[:], in1=bt2_f[:], op=OP.add)
                return e.tensor_tensor(out=fm(kk_bf, pc), in0=s2w_f[:], in1=fm(g_bf, pc),
                                       op=OP.mult)
            tok_og = p.g("vector", (lambda e, pc=pc: gnap(e, pc)), after=[mt],
                         rbank=[3, 4, 5])

        def ev_wo(e, mc, bk):
            e.scalar_tensor_tensor(out=fm(x_fm, mc), in0=fm(x_fm, mc),
                                   scalar=VC["ln_pre_w"][:, mc:mc + 1], in1=bk,
                                   op0=OP.mult, op1=OP.add)
            return e.tensor_scalar_add(out=fm(x_fm, mc), in0=fm(x_fm, mc),
                                       scalar1=VC["ln_pre_b"][:, mc:mc + 1])
        mm_wo, tok_x1 = dense_mm(6, (lambda kc: fm(kk_bf, kc)), ev_wo, "vector",
                                 after0=[tok_og])
        dump_f(t, "d_x1", x_fm, [tok_x1])

        tk_prev = None
        for pc in range(PC):
            sq_tok = p.g("scalar", (lambda e, pc=pc: e.activation(
                out=xsq_fm[:], in_=fm(x_fm, pc), func=AF.Square)),
                after=[tok_x1 if pc == 0 else tk_prev])
            def lnf_mm(e, pc=pc):
                e.matmul(out=BK(0)[0:1, :], lhsT=ones_col[:], rhs=fm(x_fm, pc),
                         start=(pc == 0), stop=(pc == PC - 1))
                return e.matmul(out=BK(3)[0:1, :], lhsT=ones_col[:], rhs=xsq_fm[:],
                                start=(pc == 0), stop=(pc == PC - 1))
            tk_prev = p.g("tensor", lnf_mm, after=[sq_tok, tok_x1],
                          wbank=[0, 3] if pc == 0 else [])
        def lnf_rows(e):
            e.tensor_scalar_mul(out=lnmu, in0=BK(0)[0:1, :], scalar1=1.0 / D)
            e.tensor_scalar_mul(out=lniv, in0=BK(3)[0:1, :], scalar1=1.0 / D)
            e.tensor_tensor(out=s2w_f[0:1, :], in0=lnmu, in1=lnmu,
                            op=OP.mult)
            e.tensor_tensor(out=lniv, in0=lniv, in1=s2w_f[0:1, :],
                            op=OP.subtract)
            return e.tensor_scalar_add(out=lniv, in0=lniv, scalar1=1e-5)
        tok_fr = p.g("vector", lnf_rows, after=[tk_prev], rbank=[0, 3])
        tok_fq = p.g("scalar", lambda e: e.activation(out=lniv, in_=lniv,
                                                      func=AF.Sqrt, scale=1.0),
                     after=[tok_fr])
        def lnf_i(e):
            e.reciprocal(out=lniv, in_=lniv)
            e.tensor_tensor(out=s2w_f[0:1, :], in0=lnmu, in1=lniv,
                            op=OP.mult)
            return e.tensor_scalar_mul(out=s2w_f[0:1, :], in0=s2w_f[0:1, :], scalar1=-1.0)
        tok_fi = p.g("vector", lnf_i, after=[tok_fq])
        def lnf_bc(e):
            e.matmul(out=BK(1), lhsT=ones_r128[:], rhs=lniv, start=True, stop=True)
            return e.matmul(out=BK(2), lhsT=ones_r128[:], rhs=s2w_f[0:1, :],
                            start=True, stop=True)
        tok_fbc = p.g("tensor", lnf_bc, after=[tok_fi], wbank=[1, 2])
        tok_xf = None
        for pc in range(PC):
            def lnf_xf(e, pc=pc):
                e.tensor_tensor(out=bt2_f[:], in0=fm(x_fm, pc), in1=BK(1), op=OP.mult)
                e.tensor_tensor(out=bt2_f[:], in0=bt2_f[:], in1=BK(2), op=OP.add)
                return e.tensor_scalar(out=fm(xn_bf, pc), in0=bt2_f[:],
                                       scalar1=VC["ln_ffn_w"][:, pc:pc + 1],
                                       scalar2=VC["ln_ffn_b"][:, pc:pc + 1],
                                       op0=OP.mult, op1=OP.add)
            tok_xf = p.g("vector", (lambda e, pc=pc: lnf_xf(e, pc)),
                         after=[tok_fbc], rbank=[1, 2])
        dump_bf(t, "d_xf", xn_bf, [tok_xf])
        def p_kf(e):
            ins = None
            for pc in range(PC):
                e.tensor_sub(out=fm(dx_bf, pc)[:, 1:TILE], in0=fm(xn_bf, pc)[:, 0:TILE - 1],
                             in1=fm(xn_bf, pc)[:, 1:TILE])
                e.tensor_sub(out=fm(dx_bf, pc)[:, 0:1], in0=xf_halo[:, pc:pc + 1],
                             in1=fm(xn_bf, pc)[:, 0:1])
                e.tensor_copy(out=xf_halo[:, pc:pc + 1], in_=fm(xn_bf, pc)[:, TILE - 1:TILE])
                ins = e.scalar_tensor_tensor(out=fm(dx_bf, pc), in0=fm(dx_bf, pc),
                                             scalar=VC["ffn_x_k"][:, pc:pc + 1],
                                             in1=fm(xn_bf, pc), op0=OP.mult, op1=OP.add)
            return ins
        tok_kf = p.g("vector", p_kf, after=[tok_xf])

        tok_h2 = None
        for ms in range(4):
            hu0 = 8 + 2 * ms
            def ev_f1(e, mc, bk, ms=ms):
                return e.activation(out=h2c(ms * 8 + mc), in_=bk, func=AF.Relu)
            mmt, evt = dense_mm(hu0, (lambda kc: fm(dx_bf, kc)), ev_f1, "scalar",
                                after0=[tok_kf] if ms == 0 else [tok_kf, tok_h2])
            def sq_f1(e, ms=ms):
                ins = None
                for mc in range(8):
                    q = ms * 8 + mc
                    ins = e.tensor_mul(out=h2c(q), in0=h2c(q), in1=h2c(q))
                return ins
            tok_h2 = p.g("vector", sq_f1, after=[evt])
        mm_f2 = None
        for ksl in range(4):
            hu0 = 16 + 2 * ksl
            ensure_hu(t, hu0 + 1)
            def f2mm(e, ksl=ksl, hu0=hu0):
                ins = None
                for kc in range(PC):
                    for mc in range(8):
                        ins = e.matmul(out=BK(mc), lhsT=hu_sb(hu0 + mc // 4, kc, mc % 4),
                                       rhs=h2c(ksl * 8 + kc),
                                       start=(ksl == 0 and kc == 0),
                                       stop=(ksl == 3 and kc == PC - 1))
                return ins
            aft = [tok_h2, wload[(t, hu0)], wload[(t, hu0 + 1)]]
            mm_f2 = p.g("tensor", f2mm, after=aft,
                        wbank=list(range(8)) if ksl == 0 else [])
            slot_free[hu0 % 2] = mm_f2
            slot_free[(hu0 + 1) % 2] = mm_f2
        def ev_f2(e):
            ins = None
            for mc in range(8):
                ins = e.tensor_tensor(out=fm(x_fm, mc), in0=fm(x_fm, mc), in1=BK(mc),
                                      op=OP.add)
            return ins
        tok_x2 = p.g("vector", ev_f2, after=[mm_f2], rbank=list(range(8)))
        dump_f(t, "d_x2", x_fm, [tok_x2])

        tok_xo = tok_x2
        for tc in range(TC):
            buf = tmb[tc % 2]
            tok_ld = p.dma_g("sync",
                             (lambda e, t=t, tc=tc, buf=buf: [e.dma_start(
                                 out=buf[:],
                                 in_=x_in[t * TILE + tc * 128: t * TILE + (tc + 1) * 128, :])]),
                             1, after=[tmb_free[0], tmb_free[1]])
            for gi in range(2):
                bk = (tc % 2) * 2 + gi
                def tpg2(e, tc=tc, gi=gi, bk=bk, buf=buf):
                    ins = None
                    for pc4 in range(4):
                        pc = gi * 4 + pc4
                        ins = e.matmul(out=BK(bk)[:, pc4 * 128:(pc4 + 1) * 128],
                                       lhsT=buf[:, pc * 128:(pc + 1) * 128],
                                       rhs=id_f32[:], start=(pc4 == 0), stop=(pc4 == 3))
                    return ins
                tk = p.g("tensor", tpg2, after=[tok_ld, tok_x2], wbank=[bk])
                def tpe2(e, tc=tc, gi=gi, bk=bk):
                    ins = None
                    for pc4 in range(4):
                        pc = gi * 4 + pc4
                        ins = e.tensor_tensor(out=fmc(x_fm, pc, tc), in0=fmc(x_fm, pc, tc),
                                              in1=BK(bk)[:, pc4 * 128:(pc4 + 1) * 128],
                                              op=OP.add)
                    return ins
                tok_xo = p.g("vector", tpe2, after=[tk], rbank=[bk])
                tmb_free[tc % 2] = tok_xo

        tk_prev = None
        for pc in range(PC):
            sq_tok = p.g("scalar", (lambda e, pc=pc: e.activation(
                out=xsq_fm[:], in_=fm(x_fm, pc), func=AF.Square)),
                after=[tok_xo if pc == 0 else tk_prev])
            def ln3_mm(e, pc=pc):
                e.matmul(out=BK(4)[0:1, :], lhsT=ones_col[:], rhs=fm(x_fm, pc),
                         start=(pc == 0), stop=(pc == PC - 1))
                return e.matmul(out=BK(7)[0:1, :], lhsT=ones_col[:], rhs=xsq_fm[:],
                                start=(pc == 0), stop=(pc == PC - 1))
            tk_prev = p.g("tensor", ln3_mm, after=[sq_tok, tok_xo],
                          wbank=[4, 7] if pc == 0 else [])
        def ln3_rows(e):
            e.tensor_scalar_mul(out=lnmu, in0=BK(4)[0:1, :], scalar1=1.0 / D)
            e.tensor_scalar_mul(out=lniv, in0=BK(7)[0:1, :], scalar1=1.0 / D)
            e.tensor_tensor(out=s2w_f[0:1, :], in0=lnmu, in1=lnmu,
                            op=OP.mult)
            e.tensor_tensor(out=lniv, in0=lniv, in1=s2w_f[0:1, :],
                            op=OP.subtract)
            return e.tensor_scalar_add(out=lniv, in0=lniv, scalar1=1e-5)
        tok_3r = p.g("vector", ln3_rows, after=[tk_prev], rbank=[4, 7])
        tok_3q = p.g("scalar", lambda e: e.activation(out=lniv, in_=lniv,
                                                      func=AF.Sqrt, scale=1.0),
                     after=[tok_3r])
        def ln3_i(e):
            e.reciprocal(out=lniv, in_=lniv)
            e.tensor_tensor(out=s2w_f[0:1, :], in0=lnmu, in1=lniv,
                            op=OP.mult)
            return e.tensor_scalar_mul(out=s2w_f[0:1, :], in0=s2w_f[0:1, :], scalar1=-1.0)
        tok_3i = p.g("vector", ln3_i, after=[tok_3q])
        def ln3_bc(e):
            e.matmul(out=BK(5), lhsT=ones_r128[:], rhs=lniv, start=True, stop=True)
            return e.matmul(out=BK(6), lhsT=ones_r128[:], rhs=s2w_f[0:1, :],
                            start=True, stop=True)
        tok_3bc = p.g("tensor", ln3_bc, after=[tok_3i], wbank=[5, 6])
        tok_x3 = None
        for pc in range(PC):
            def ln3_ap(e, pc=pc):
                e.tensor_tensor(out=bt2_f[:], in0=fm(x_fm, pc), in1=BK(5), op=OP.mult)
                e.tensor_tensor(out=fm(x_fm, pc), in0=bt2_f[:], in1=BK(6), op=OP.add)
                return e.tensor_copy(out=fm(xn_bf, pc), in_=fm(x_fm, pc))
            tok_x3 = p.g("vector", (lambda e, pc=pc: ln3_ap(e, pc)),
                         after=[tok_3bc], rbank=[5, 6])
        dump_f(t, "d_x3", x_fm, [tok_x3])

        def gt1(e):
            ins = None
            for kc in range(PC):
                for q in range(2):
                    ins = e.matmul(out=BK(q), lhsT=wg1_sb[:, kc * 256 + q * 128:
                                                         kc * 256 + (q + 1) * 128],
                                   rhs=fm(xn_bf, kc), start=(kc == 0), stop=(kc == PC - 1))
            return ins
        tok_gt1 = p.g("tensor", gt1, after=[tok_x3], wbank=[0, 1])
        def gt1e(e):
            e.activation(out=hg_bf[:, 0:TILE], in_=BK(0), func=AF.Gelu,
                         bias=bg1c[:, 0:1])
            return e.activation(out=hg_bf[:, TILE:2 * TILE], in_=BK(1), func=AF.Gelu,
                                bias=bg1c[:, 1:2])
        tok_gt1e = p.g("scalar", gt1e, after=[tok_gt1], rbank=[0, 1])
        def gt2(e):
            e.matmul(out=BK(2)[0:40, :], lhsT=wg2_sb[:, 0:40], rhs=hg_bf[:, 0:TILE],
                     start=True, stop=False)
            return e.matmul(out=BK(2)[0:40, :], lhsT=wg2_sb[:, 40:80],
                            rhs=hg_bf[:, TILE:2 * TILE], start=False, stop=True)
        tok_gt2 = p.g("tensor", gt2, after=[tok_gt1e], wbank=[2])
        tok_lg = p.g("vector", lambda e: e.tensor_scalar_add(
            out=lgt_f[:], in0=BK(2)[0:20, :], scalar1=bg2c[0:20, :]),
                     after=[tok_gt2], rbank=[2])
        def lgT(e):
            ins = None
            for tc in range(TC):
                ins = e.matmul(out=BK(3)[:, tc * 20:(tc + 1) * 20],
                               lhsT=lgt_f[:, tc * 128:(tc + 1) * 128],
                               rhs=id_f32[0:20, 0:20], start=(tc == 0), stop=(tc == 3))
            return ins
        tok_lgT = p.g("tensor", lgT, after=[tok_lg], wbank=[3])
        tok_lgTe = p.g("vector", lambda e: e.tensor_copy(out=lgtT[:], in_=BK(3)[:, 0:TC * NE]),
                       after=[tok_lgT], rbank=[3])
        def lgv(tt):
            return tt[:].rearrange("p (c k) -> p c k", c=TC)
        def growc(i):
            return AP(grow.ap().tensor, i * TC, [[16, 128], [1, TC], [0, NE]])
        def top2(e):
            e.tensor_reduce(out=grow[:, 0:TC], in_=lgv(lgtT), axis=mybir.AxisListType.X,
                            op=OP.max)
            for cq in range(TC):
                e.tensor_scalar(out=gscr[:, cq * NE:(cq + 1) * NE],
                                in0=lgtT[:, cq * NE:(cq + 1) * NE],
                                scalar1=grow[:, cq:cq + 1], scalar2=None, op0=OP.subtract)
            return e.tensor_copy(out=gscr2[:], in_=gscr[:])
        tok_t2a = p.g("vector", top2, after=[tok_lgTe])
        tok_t2b = p.g("scalar", lambda e: e.activation(out=gscr2[:], in_=gscr2[:],
                                                       func=AF.Exp),
                      after=[tok_t2a])
        def top2c(e):
            e.tensor_reduce(out=grow[:, TC:2 * TC], in_=lgv(gscr2),
                            axis=mybir.AxisListType.X, op=OP.add)
            e.reciprocal(out=grow[:, 2 * TC:3 * TC], in_=grow[:, TC:2 * TC])
            for cq in range(TC):
                e.tensor_scalar(out=gscr[:, cq * NE:(cq + 1) * NE],
                                in0=lgtT[:, cq * NE:(cq + 1) * NE],
                                scalar1=grow[:, cq:cq + 1], scalar2=None, op0=OP.is_ge)
            e.scalar_tensor_tensor(out=lgv(gscr), in0=lgv(gscr), scalar=-1e30,
                                   in1=lgv(lgtT), op0=OP.mult, op1=OP.add)
            e.tensor_reduce(out=grow[:, 3 * TC:4 * TC], in_=lgv(gscr),
                            axis=mybir.AxisListType.X, op=OP.max)
            for cq in range(TC):
                e.tensor_scalar(out=gscr[:, cq * NE:(cq + 1) * NE],
                                in0=lgtT[:, cq * NE:(cq + 1) * NE],
                                scalar1=grow[:, 3 * TC + cq:3 * TC + cq + 1], scalar2=None, op0=OP.is_ge)
            e.tensor_tensor(out=gscr[:], in0=gscr[:], in1=gscr2[:], op=OP.mult)
            ins = None
            for cq in range(TC):
                ins = e.tensor_scalar(out=gscr2[:, cq * NE:(cq + 1) * NE],
                                      in0=gscr[:, cq * NE:(cq + 1) * NE],
                                      scalar1=grow[:, 2 * TC + cq:2 * TC + cq + 1],
                                      scalar2=None, op0=OP.mult)
            return ins
        tok_t2c = p.g("vector", top2c, after=[tok_t2b])
        def gTb(e):
            ins = None
            for tc in range(TC):
                ins = e.matmul(out=BK(4)[0:20, tc * 128:(tc + 1) * 128],
                               lhsT=gscr2[:, tc * 20:(tc + 1) * 20],
                               rhs=id_f32[:], start=(tc == 0), stop=(tc == 3))
            return ins
        tok_gTb = p.g("tensor", gTb, after=[tok_t2c], wbank=[4])
        tok_gbe = p.g("scalar", lambda e: e.copy(out=gates_bf[:], in_=BK(4)[0:20, :]),
                      after=[tok_gTb], rbank=[4])
        if debug and t == dbg_tile:
            dbg_after.append(p.dma_g("gpsimd",
                                     (lambda e: [e.dma_start(out=dbg["d_gates"][:],
                                                             in_=gates_bf[:])]),
                                     1, after=[tok_gbe]))

        tok_selmm = None
        tok_selev = None
        for ee in range(NE):
            tok_msel = p.g("vector", (lambda e, ee=ee: e.memset(sel2[ee % 2], 1.0)),
                           after=[tok_gbe] if ee < 2 else [tok_selmm])
            tok_sel = p.g("gpsimd", (lambda e, ee=ee: e.affine_select(
                out=sel2[ee % 2], in_=sel2[ee % 2], pattern=[[0, 128]],
                compare_op=OP.is_equal, fill=0.0, base=-ee, channel_multiplier=1)),
                after=[tok_msel])
            def selmm(e, ee=ee):
                return e.matmul(out=BK(ee % 2), lhsT=sel2[ee % 2],
                                rhs=gates_bf, start=True, stop=True)
            tok_selmm = p.g("tensor", selmm, after=[tok_sel, tok_gbe], wbank=[ee % 2])
            eng_ev = "vector" if ee % 2 == 0 else "scalar"
            def selev(e, ee=ee):
                if ee % 2 == 0:
                    return e.tensor_copy(out=gbc_t(ee), in_=BK(ee % 2))
                return e.copy(out=gbc_t(ee), in_=BK(ee % 2))
            tok_selev = p.g(eng_ev, selev, after=[tok_selmm], rbank=[ee % 2])

        def bemm(e):
            ins = None
            for mc in range(8):
                ins = e.matmul(out=BK(mc), lhsT=be_sb[:, mc * 128:(mc + 1) * 128],
                               rhs=gates_bf[:], start=True, stop=False)
            return ins
        tok_y0 = p.g("tensor", bemm, after=[tok_gbe], wbank=list(range(8)))

        tok_gx = None
        mm_moe = tok_y0
        for ee in range(NE):
            def gxf(e, ee=ee):
                ins = None
                for pc in range(PC):
                    ins = e.tensor_tensor(out=fm(kk_bf, pc), in0=fm(xn_bf, pc),
                                          in1=gbc_t(ee), op=OP.mult)
                return ins
            tok_gx = p.g("vector", gxf, after=[tok_selev, mm_moe])
            hu0 = 24 + 2 * ee
            ensure_hu(t, hu0 + 1)
            def moemm(e, ee=ee, hu0=hu0):
                ins = None
                for kc in range(PC):
                    for mc in range(8):
                        ins = e.matmul(out=BK(mc), lhsT=hu_sb(hu0 + mc // 4, kc, mc % 4),
                                       rhs=fm(kk_bf, kc), start=False,
                                       stop=(ee == NE - 1 and kc == PC - 1))
                return ins
            mm_moe = p.g("tensor", moemm,
                         after=[tok_gx, wload[(t, hu0)], wload[(t, hu0 + 1)]])
            slot_free[hu0 % 2] = mm_moe
            slot_free[(hu0 + 1) % 2] = mm_moe
        def ev_moe(e):
            ins = None
            for mc in range(8):
                ins = e.tensor_tensor(out=fm(x_fm, mc), in0=fm(x_fm, mc), in1=BK(mc),
                                      op=OP.add)
            return ins
        tok_x4 = p.g("vector", ev_moe, after=[mm_moe], rbank=list(range(8)))

        store_toks = []
        for tc in range(TC):
            buf = tmb[tc % 2]
            for gi in range(2):
                bk = (tc % 2) * 2 + gi
                def tpo(e, tc=tc, gi=gi, bk=bk):
                    ins = None
                    for pc4 in range(4):
                        pc = gi * 4 + pc4
                        ins = e.matmul(out=BK(bk)[:, pc4 * 128:(pc4 + 1) * 128],
                                       lhsT=fmc(x_fm, pc, tc), rhs=id_f32[:],
                                       start=(pc4 == 0), stop=(pc4 == 3))
                    return ins
                tk = p.g("tensor", tpo, after=[tok_x4, tmb_free[0], tmb_free[1]], wbank=[bk])
                def tpoe(e, tc=tc, gi=gi, bk=bk, buf=buf):
                    ins = None
                    for pc4 in range(4):
                        pc = gi * 4 + pc4
                        ins = e.tensor_copy(out=buf[:, pc * 128:(pc + 1) * 128],
                                            in_=BK(bk)[:, pc4 * 128:(pc4 + 1) * 128])
                    return ins
                tok_tpe = p.g("vector", tpoe, after=[tk], rbank=[bk])
            def p8s(e, tc=tc, buf=buf):
                for hh in range(2):
                    e.activation(out=xsq_fm[:], in_=buf[:, hh * 512:(hh + 1) * 512],
                                 func=AF.Copy, accum_out=stat_h[:, 4 * tc + hh:4 * tc + hh + 1])
                    ins = e.activation(out=xsq_fm[:], in_=buf[:, hh * 512:(hh + 1) * 512],
                                       func=AF.Square,
                                       accum_out=stat_h[:, 4 * tc + 2 + hh:4 * tc + 3 + hh])
                return ins
            tok_s8 = p.g("scalar", p8s, after=[tok_tpe])
            def p8r(e, tc=tc):
                e.tensor_add(out=stat_s[:, tc:tc + 1], in0=stat_h[:, 4 * tc:4 * tc + 1],
                             in1=stat_h[:, 4 * tc + 1:4 * tc + 2])
                e.tensor_add(out=stat_q[:, tc:tc + 1], in0=stat_h[:, 4 * tc + 2:4 * tc + 3],
                             in1=stat_h[:, 4 * tc + 3:4 * tc + 4])
                e.tensor_scalar_mul(out=stat_m[:, tc:tc + 1], in0=stat_s[:, tc:tc + 1],
                                    scalar1=1.0 / D)
                e.tensor_scalar_mul(out=stat_q[:, tc:tc + 1], in0=stat_q[:, tc:tc + 1],
                                    scalar1=1.0 / D)
                e.tensor_tensor(out=stat_i[:, tc:tc + 1], in0=stat_m[:, tc:tc + 1],
                                in1=stat_m[:, tc:tc + 1], op=OP.mult)
                e.tensor_tensor(out=stat_q[:, tc:tc + 1], in0=stat_q[:, tc:tc + 1],
                                in1=stat_i[:, tc:tc + 1], op=OP.subtract)
                return e.tensor_scalar_add(out=stat_q[:, tc:tc + 1],
                                           in0=stat_q[:, tc:tc + 1], scalar1=1e-5)
            tok_r8 = p.g("vector", p8r, after=[tok_s8])
            tok_q8 = p.g("scalar", (lambda e, tc=tc: e.activation(
                out=stat_q[:, tc:tc + 1], in_=stat_q[:, tc:tc + 1], func=AF.Sqrt,
                scale=1.0)), after=[tok_r8])
            def p8a(e, tc=tc, buf=buf):
                e.reciprocal(out=stat_i[:, tc:tc + 1], in_=stat_q[:, tc:tc + 1])
                return e.tensor_scalar(out=buf[:], in0=buf[:],
                                       scalar1=stat_m[:, tc:tc + 1],
                                       scalar2=stat_i[:, tc:tc + 1],
                                       op0=OP.subtract, op1=OP.mult)
            tok_a8 = p.g("vector", p8a, after=[tok_q8])
            tok_st8 = p.dma_g("sync",
                              (lambda e, t=t, tc=tc, buf=buf: [e.dma_start(
                                  out=y_out[t * TILE + tc * 128: t * TILE + (tc + 1) * 128, :],
                                  in_=buf[:])]), 1, after=[tok_a8])
            store_toks.append(tok_st8)
            tmb_free[tc % 2] = tok_st8
        tok_store_last = store_toks[-1]

    p.g("vector", lambda e: e.tensor_copy(out=pp_bf[0:1, 0:1], in_=pp_bf[0:1, 0:1]),
        after=[tok_store_last] + list(dbg_after))

    cut = int(os.environ.get("DK_CUT", str(p.seq + 10)))
    if os.environ.get("DK_PRINT_SEQ"):
        print("TOTAL_SEQ", p.seq)
    with nc.Block() as block:
        class VecGuard:
            def __init__(self, eng, sp):
                self.e = eng
                self.sp = sp
                self.prev_out = set()
                self.prev_in = set()

            @staticmethod
            def _names(v):
                if isinstance(v, bass.AP):
                    return {v.name}
                return set()

            def wait_ge(self, *a, **k):
                return self.e.wait_ge(*a, **k)

            def __getattr__(self, nm):
                real = getattr(self.e, nm)
                if not callable(real):
                    return real
                def wrap(*args, **kw):
                    outs, ins = set(), set()
                    if nm == "memset" and args:
                        outs |= self._names(args[0])
                        rest = args[1:]
                    else:
                        rest = args
                    for k, v in kw.items():
                        if k in ("out", "accum_out"):
                            outs |= self._names(v)
                        else:
                            ins |= self._names(v)
                    for v in rest:
                        ins |= self._names(v)
                    if (self.prev_out & (outs | ins)) or (self.prev_in & outs):
                        self.e.tensor_copy(out=self.sp, in_=self.sp)
                    self.prev_out, self.prev_in = outs, ins
                    return real(*args, **kw)
                return wrap

        def emit(ename):
            def run(eng):
                if ename == "vector":
                    eng = VecGuard(eng, vspace[:, 0:1])
                for (waits, fn, n_dma, seq) in p.groups[ename]:
                    if seq >= cut:
                        break
                    for s, v in waits.items():
                        if s.startswith("dma_"):
                            eng.wait_ge(dsem[s[4:]], v)
                        else:
                            eng.wait_ge(sem[s], v)
                    res = fn(eng)
                    if n_dma is not None:
                        assert len(res) == n_dma, \
                            f"{ename} dma group: declared {n_dma}, issued {len(res)}"
                        for ins in res:
                            ins.then_inc(dsem[ename], 16)
                    else:
                        res.then_inc(sem[ename], 1)
            return run
        block.sync(emit("sync"))
        block.scalar(emit("scalar"))
        block.vector(emit("vector"))
        block.tensor(emit("tensor"))
        block.gpsimd(emit("gpsimd"))

    es.close()
    return nc



def kernel(**inputs):
    x = np.ascontiguousarray(np.asarray(inputs["L0_attn_x"], dtype=np.float32))
    nc = build_nc(debug=False)
    base = {}
    ren = {"W_ffn1": "L0_attn_W_ffn1", "W_ffn2": "L0_attn_W_ffn2",
           "wg1": "L0_ffn_wg1", "bg1": "L0_ffn_bg1", "wg2": "L0_ffn_wg2",
           "bg2": "L0_ffn_bg2", "We": "L0_ffn_We", "be": "L0_ffn_be"}
    names = (WN1D + ["r_k", "w1", "w2", "a1", "a2", "g1", "g2",
                     "W_r", "W_k", "W_v", "W_o", "W_ffn1", "W_ffn2",
                     "wg1", "bg1", "wg2", "bg2", "We", "be"])
    for nm in names:
        key = ren.get(nm, "L0_attn_" + nm)
        if key not in inputs:
            key = "L0_ffn_" + nm
        base[nm] = np.ascontiguousarray(np.asarray(inputs[key], dtype=np.float32))
    in_maps = []
    for b in range(8):
        m = dict(base)
        m["x"] = x[b]
        in_maps.append(m)
    res = run_bass_kernel_spmd(nc, in_maps, core_ids=list(range(8)))
    return np.stack([res.results[b]["out"] for b in range(8)], axis=0)
